# version 11
# baseline (speedup 1.0000x reference)
import numpy as np
import concourse.bass as bass
import concourse.mybir as mybir
from concourse.alu_op_type import AluOpType as OP
from concourse.bass_utils import run_bass_kernel_spmd

dt = mybir.dt
F32, F32R, BF16, FP8 = dt.float32, dt.float32r, dt.bfloat16, dt.float8e4
AF = mybir.ActivationFunctionType
AX = mybir.AxisListType

S, E, H, D, I = 2048, 1024, 16, 64, 4096
NT, NC, NW, NIC = S // 128, E // 128, S // 512, I // 128
ROPE_BASE, EPS = 10000.0, 1e-5


def _host_constants():
    c = {}
    pos = np.arange(S, dtype=np.float64)
    inv = 1.0 / ROPE_BASE ** (np.arange(0, D, 2, dtype=np.float64) / D)
    ang = inv[:, None] * pos[None, :]
    c["rope_cos"] = np.tile(np.repeat(np.cos(ang), 2, axis=0), (2, 1)).astype(np.float32)
    c["rope_sin"] = np.tile(np.repeat(np.sin(ang), 2, axis=0), (2, 1)).astype(np.float32)
    P = np.zeros((128, 128), np.float32)
    for b in (0, 64):
        for i in range(32):
            P[b + i, b + i + 32] = -1.0
            P[b + 32 + i, b + i] = 1.0
    c["rotPT"] = np.ascontiguousarray(P.T)
    c["ident"] = np.eye(128, dtype=np.float32)
    u = np.arange(896)
    c["causal"] = (np.arange(128)[:, None] <= (u[None, :] - 384)).astype(np.float32)
    hm = np.zeros((128, 2), np.float32)
    hm[:64, 0] = 1.0
    hm[64:, 1] = 1.0
    c["headmask"] = hm
    pm = np.zeros((2, 64), np.float32)
    for h in range(16):
        pm[h % 2, h * 4:(h + 1) * 4] = -0.125
    c["parity"] = pm
    c["ones1"] = np.ones((1, 128), np.float32)
    c["ones2"] = np.ones((2, 128), np.float32)
    return c


def _fm(b):
    return np.ascontiguousarray(np.asarray(b, np.float32).reshape(-1, 128).T)


def _prep_inputs(kw):
    c = _host_constants()
    x = np.asarray(kw["L0_attn_x"], np.float32)
    wq, wk = np.asarray(kw["L0_attn_wq"], np.float32), np.asarray(kw["L0_attn_wk"], np.float32)
    wv, wo = np.asarray(kw["L0_attn_wv"], np.float32), np.asarray(kw["L0_attn_wo"], np.float32)
    win = np.asarray(kw["L1_attn_w_in"], np.float32)
    bin_ = np.asarray(kw["L1_attn_b_in"], np.float32)
    sh = {
        "wqk0T": np.ascontiguousarray(np.concatenate([wq.T, wk.T], axis=1)),
        "wv0T": np.ascontiguousarray(wv.T),
        "wo0T": np.ascontiguousarray(wo.T),
        "bqk0": np.concatenate([_fm(kw["L0_attn_bq"]), _fm(kw["L0_attn_bk"])], axis=1),
        "bv0": np.asarray(kw["L0_attn_bv"], np.float32)[None, :],
        "bo0": np.asarray(kw["L0_attn_bo"], np.float32)[None, :],
        "W1": np.asarray(kw["L0_ffn_W1"], np.float32),
        "b1": _fm(kw["L0_ffn_b1"]),
        "W2": np.asarray(kw["L0_ffn_W2"], np.float32),
        "b2": np.asarray(kw["L0_ffn_b2"], np.float32)[None, :],
        "wqk1T": np.ascontiguousarray(win[:2 * E].T),
        "wv1T": np.ascontiguousarray(win[2 * E:].T),
        "bqk1": np.concatenate([_fm(bin_[:E]), _fm(bin_[E:2 * E])], axis=1),
        "bv1": bin_[2 * E:][None, :],
        "wo1T": np.ascontiguousarray(np.asarray(kw["L1_attn_w_out"], np.float32).T),
        "bo1": np.asarray(kw["L1_attn_b_out"], np.float32)[None, :],
        "gw": np.asarray(kw["L1_ffn_gate_w"], np.float32),
        "uw": np.asarray(kw["L1_ffn_up_w"], np.float32),
        "dw": np.asarray(kw["L1_ffn_dw" if False else "L1_ffn_down_w"], np.float32),
        "s_in": np.asarray(kw["L1_ffn_s_in"], np.float32).reshape(1, 1),
        "s_in_down": np.asarray(kw["L1_ffn_s_in_down"], np.float32).reshape(1, 1),
    }
    sh.update(c)
    return [dict(sh, x0=np.ascontiguousarray(x[core])) for core in range(8)]



class K:
    def __init__(self, nc):
        self.nc = nc
        self.q = {e: [] for e in ("pe", "act", "dve", "gp", "sp")}
        self.cnt = {n: 0 for n in ("ld", "st", "pe", "act", "dve", "gp")}
        self.sems = {}
        self.lastw = {e: {} for e in self.q}
        self.banks = {}
        self.slots = {}
        self.pool = []

    def wait(self, eng, sem, val=None):
        v = self.cnt[sem] if val is None else val
        if v <= 0 or self.lastw[eng].get(sem, 0) >= v:
            return
        self.lastw[eng][sem] = v
        h = self.sems[sem]
        self.q[eng].append(lambda e, h=h, v=v: e.wait_ge(h, v))

    def _op(self, eng, sem, mk):
        if eng in ("act", "dve", "gp"):
            self.wait(eng, sem, self.cnt[sem])
        self.cnt[sem] += 1
        h = self.sems[sem]
        self.q[eng].append(lambda e, mk=mk, h=h: mk().then_inc(h, 1))
        return self.cnt[sem]

    def pe(self, mk):
        return self._op("pe", "pe", mk)

    def peq(self, mk):
        self.q["pe"].append(lambda e, mk=mk: mk())

    def act(self, mk):
        return self._op("act", "act", mk)

    def dve(self, mk):
        return self._op("dve", "dve", mk)

    def gp(self, mk):
        return self._op("gp", "gp", mk)

    def load(self, dst, src, slot="boot"):
        if slot not in self.slots:
            assert self.pool, "semaphore pool exhausted"
            self.slots[slot] = [self.pool.pop(), 0]
        ent = self.slots[slot]
        ent[1] += 16
        h = ent[0]
        nc = self.nc
        self.q["sp"].append(
            lambda e, dst=dst, src=src, h=h: nc.sync.dma_start(dst, src).then_inc(h, 16))
        return ent[1]

    def wld(self, eng, slot, val=None):
        if slot not in self.slots:
            return
        ent = self.slots[slot]
        v = ent[1] if val is None else val
        key = "L_" + slot
        if v <= 0 or self.lastw[eng].get(key, 0) >= v:
            return
        self.lastw[eng][key] = v
        h = ent[0]
        self.q[eng].append(lambda e, h=h, v=v: e.wait_ge(h, v))

    def store(self, dst, src):
        self.cnt["st"] += 16
        h = self.sems["st"]
        nc = self.nc
        self.q["sp"].append(
            lambda e, dst=dst, src=src, h=h: nc.sync.dma_start(dst, src).then_inc(h, 16))
        return self.cnt["st"]

    pump = None

    def do_pump(self, n=1):
        if self.pump is None:
            return
        for _ in range(n):
            try:
                next(self.pump)
            except StopIteration:
                self.pump = None
                return

    def fence_sp(self):
        self.wait("sp", "pe")
        self.wait("sp", "act")
        self.wait("sp", "dve")
        self.wait("sp", "st")

    def bw(self, name):
        st = self.banks.setdefault(name, {"w": 0, "r": {}})
        for eng, v in st["r"].items():
            self.wait("pe", eng, v)
        st["r"] = {}

    def bwd(self, name):
        self.banks[name]["w"] = self.cnt["pe"]

    def br(self, eng, name):
        st = self.banks.setdefault(name, {"w": 0, "r": {}})
        self.wait(eng, "pe", st["w"])

    def brd(self, eng, name):
        self.banks[name]["r"][eng] = self.cnt[eng]



def build_program(tap=None):
    nc = bass.Bass("TRN2", target_bir_lowering=False)
    dram = {}

    def din(name, shape, dtype=F32):
        dram[name] = nc.dram_tensor(name, list(shape), dtype, kind="ExternalInput")

    def dsc(name, shape, dtype=F32):
        dram[name] = nc.dram_tensor(name, list(shape), dtype, kind="Internal")

    din("x0", (S, E))
    din("wqk0T", (E, 2 * E)); din("wv0T", (E, E)); din("wo0T", (E, E))
    din("bqk0", (128, 16)); din("bv0", (1, E)); din("bo0", (1, E))
    din("W1", (E, I)); din("b1", (128, NIC)); din("W2", (I, E)); din("b2", (1, E))
    din("wqk1T", (E, 2 * E)); din("wv1T", (E, E)); din("wo1T", (E, E))
    din("bqk1", (128, 16)); din("bv1", (1, E)); din("bo1", (1, E))
    din("gw", (I, E)); din("uw", (I, E)); din("dw", (E, I))
    din("s_in", (1, 1)); din("s_in_down", (1, 1))
    din("rope_cos", (128, S)); din("rope_sin", (128, S))
    din("rotPT", (128, 128)); din("ident", (128, 128)); din("causal", (128, 896))
    din("headmask", (128, 2)); din("parity", (2, 64))
    din("ones1", (1, 128)); din("ones2", (2, 128))
    dsc("xcur", (S, E))
    dsc("xn_fm", (E, S), BF16)
    dsc("Qs", (E, S), BF16); dsc("Ks", (E, S), BF16)
    dsc("wgq", (E, I), BF16); dsc("wuq", (E, I), BF16); dsc("wdq", (I, E), BF16)
    dram["y"] = nc.dram_tensor("y", [S, E], F32, kind="ExternalOutput")

    import contextlib
    stack = contextlib.ExitStack()
    sb = {}

    def sbuf(name, shape, dtype):
        sb[name] = stack.enter_context(nc.sbuf_tensor("sb_" + name, list(shape), dtype))

    sbuf("rotPT", (128, 128), BF16)
    sbuf("identf", (128, 128), F32)
    sbuf("identb", (128, 128), BF16)
    sbuf("causal", (128, 896), BF16)
    sbuf("headmask", (128, 2), F32)
    sbuf("parity", (2, 64), F32)
    sbuf("ones1", (1, 128), F32)
    sbuf("ones2", (2, 128), F32)
    sbuf("onesb", (1, 128), BF16)
    sbuf("onescol", (128, 1), BF16)
    sbuf("headmaskb", (128, 2), BF16)
    sbuf("bqk", (128, 16), F32)
    sbuf("b1", (128, NIC), F32)
    sbuf("brow", (1, E), F32)
    sbuf("browb", (1, E), BF16)
    sbuf("scl", (128, 16), F32)
    sbuf("stats", (128, 4 * NT), F32)
    sbuf("bnraw", (128, 12), F32)
    sbuf("qw", (128, 16 * NW), F32)
    sbuf("cmaxK", (128, 8), F32)
    sbuf("cmaxM", (128, 16), F32)
    sbuf("mrow", (2, 96), F32)
    sbuf("Bsb", (128, 64), F32)
    sbuf("zeros", (128, 512), BF16)
    sbuf("rsum", (128, 4), F32)
    sbuf("msk8", (128, 512), dt.uint8)
    sbuf("qsml", (128, 6 * 128), F32)
    sbuf("qfp8", (128, 2 * 128), FP8)
    sbuf("qkrot", (128, 2 * 512), BF16)
    sbuf("rtmp", (128, 2 * 512), F32)
    sbuf("at", (128, 2 * 512), BF16)
    sbuf("gtile", (128, E), BF16)
    sbuf("xio", (128, E), F32)
    sbuf("A2", (128, 16384), BF16)
    sbuf("xqT", (128, 16384), BF16)
    sbuf("AV", (128, NT * 16 * 65), BF16)
    sbuf("wblk", (128, 8192), F32)
    sbuf("wblk2", (128, 8192), F32)
    sbuf("wsh", (128, 8192), BF16)

    ps = {}
    for i in range(6):
        ps["b%d" % i] = stack.enter_context(nc.psum_tensor("pb%d" % i, [128, 512], F32))
    ps["t0"] = stack.enter_context(nc.psum_tensor("pt0", [128, 1024], BF16))
    ps["t1"] = stack.enter_context(nc.psum_tensor("pt1", [128, 1024], BF16))

    import contextlib as _ctx
    semstack = stack
    h_ld = semstack.enter_context(nc.semaphore("s_ld"))
    h_st = semstack.enter_context(nc.semaphore("s_st"))
    h_pe = semstack.enter_context(nc.semaphore("s_pe"))
    h_act = semstack.enter_context(nc.semaphore("s_act"))
    h_dve = semstack.enter_context(nc.semaphore("s_dve"))
    h_gp = semstack.enter_context(nc.semaphore("s_gp"))
    pool = [semstack.enter_context(nc.semaphore("s_p%d" % i)) for i in range(32)]
    with nc.Block() as block:
        k = K(nc)
        k.sems = {"ld": h_ld, "st": h_st, "pe": h_pe, "act": h_act,
                  "dve": h_dve, "gp": h_gp}
        k.pool = pool
        _build_all(nc, k, dram, sb, ps, tap)

        @block.sync
        def _(e):
            for fn in k.q["sp"]:
                fn(e)

        @block.tensor
        def _(e):
            for fn in k.q["pe"]:
                fn(e)

        @block.scalar
        def _(e):
            for fn in k.q["act"]:
                fn(e)

        @block.vector
        def _(e):
            for fn in k.q["dve"]:
                fn(e)

        @block.gpsimd
        def _(e):
            for fn in k.q["gp"]:
                fn(e)

    stack.close()
    return nc



def _build_all(nc, k, dram, sb, ps, tap):
    v, t, a = nc.vector, nc.tensor, nc.scalar

    for n, tgt in (("ident", "identf"), ("headmask", "headmask"),
                   ("parity", "parity"), ("ones1", "ones1"), ("ones2", "ones2"),
                   ("b1", "b1")):
        k.load(sb[tgt][:, :], dram[n][:, :], "boot")
    k.load(sb["rtmp"][:, 0:128], dram["rotPT"][:, :], "boot")
    k.load(sb["wblk"][:, 0:896], dram["causal"][:, :], "boot")
    k.wld("dve", "boot")
    k.wld("pe", "boot")
    k.wld("act", "boot")
    k.dve(lambda: v.tensor_copy(sb["identb"][:, :], sb["identf"][:, :]))
    k.dve(lambda: v.tensor_copy(sb["rotPT"][:, :], sb["rtmp"][:, 0:128]))
    k.dve(lambda: v.tensor_copy(sb["causal"][:, :], sb["wblk"][:, 0:896]))
    k.dve(lambda: v.memset(sb["zeros"][:, :], 0.0))
    k.dve(lambda: v.memset(sb["onesb"][:, :], 1.0))
    k.dve(lambda: v.memset(sb["onescol"][:, :], 1.0))
    k.dve(lambda: v.tensor_copy(sb["headmaskb"][:, :], sb["headmask"][:, :]))

    k.load(sb["brow"][0:1, 0:1], dram["s_in"][:, :], "brow")
    k.load(sb["brow"][0:1, 1:2], dram["s_in_down"][:, :], "brow")
    k.wld("pe", "brow")
    k.bw("b5")
    k.pe(lambda: t.matmul(ps["b5"][:, 0:2], sb["ones1"][:, :], sb["brow"][0:1, 0:2],
                          start=True, stop=True))
    k.bwd("b5")
    k.br("dve", "b5")
    k.dve(lambda: v.tensor_copy(sb["scl"][:, 0:1], ps["b5"][:, 0:1]))
    k.dve(lambda: v.tensor_copy(sb["scl"][:, 3:4], ps["b5"][:, 1:2]))
    k.brd("dve", "b5")
    for base in (0, 3):
        k.dve(lambda base=base: v.tensor_scalar(
            sb["scl"][:, base + 1:base + 2], sb["scl"][:, base:base + 1],
            1.0 / 6.0, None, OP.mult))
        k.dve(lambda base=base: v.reciprocal(
            sb["scl"][:, base + 2:base + 3], sb["scl"][:, base:base + 1]))

    _layer(nc, k, dram, sb, ps, 0, tap)
    if tap is not None and tap.endswith("0"):
        _finish(k)
        return
    if tap == "wq":
        _wquant(nc, k, dram, sb, ps)
        _finish(k)
        return
    k.pump = _wquant_gen(nc, k, dram, sb, ps)
    _layer(nc, k, dram, sb, ps, 1, tap)
    _finish(k)


def _wquant_gen(nc, k, dram, sb, ps):
    v, t, a = nc.vector, nc.tensor, nc.scalar
    NQ = 1024
    ldb = sb["wblk2"][:, 4096:5120]
    w2b = sb["wblk2"][:, :].bitcast(BF16)
    TA = sb["wblk2"][:, 6144:7168]
    rtb = sb["rtmp"][:, :].bitcast(BF16)
    NL = rtb[:, 0:1024]
    NM = rtb[:, 1024:2048]
    SG = sb["gtile"][:, 0:1024]
    QO = w2b[:, 14336:15360]
    stage = w2b[:, 15360:16384]
    specs = [("gw", "wgq", I, E, 0, 6), ("uw", "wuq", I, E, 0, 7),
             ("dw", "wdq", E, I, 3, 8)]
    for (nm, qn, NR, NCL, actbase, sci) in specs:
        nrch = NR // 128
        nq = NCL // NQ
        for rc in range(nrch):
            for qq in range(nq):
                idx = rc * nq + qq
                k.wait("sp", "dve")
                k.load(ldb, dram[nm][rc * 128:(rc + 1) * 128, qq * NQ:(qq + 1) * NQ],
                       "wql")
                k.wld("dve", "wql")
                def fam(idx=idx):
                    return v.tensor_reduce(sb["stats"][:, idx:idx + 1], ldb,
                                           AX.X, OP.max, apply_absolute_value=True)
                k.dve(fam)
                yield
        tot = nrch * nq
        k.dve(lambda tot=tot: v.tensor_reduce(sb["scl"][:, 15:16],
                                              sb["stats"][:, 0:tot], AX.X, OP.max))
        k.wait("pe", "dve")
        k.bw("b5")
        k.pe(lambda: t.matmul(ps["b5"][0:1, 0:128], sb["scl"][:, 15:16],
                              sb["identf"][:, :], start=True, stop=True,
                              is_transpose=True))
        k.bwd("b5")
        k.br("dve", "b5")
        k.dve(lambda: v.tensor_reduce(sb["mrow"][0:1, 64:65], ps["b5"][0:1, 0:128],
                                      AX.X, OP.max))
        k.brd("dve", "b5")
        k.wait("pe", "dve")
        k.bw("b5")
        k.pe(lambda: t.matmul(ps["b5"][:, 1:2], sb["ones1"][:, :],
                              sb["mrow"][0:1, 64:65], start=True, stop=True))
        k.bwd("b5")
        k.br("dve", "b5")
        k.dve(lambda: v.tensor_copy(sb["scl"][:, 14:15], ps["b5"][:, 1:2]))
        k.brd("dve", "b5")
        k.dve(lambda: v.reciprocal(sb["scl"][:, 9:10], sb["scl"][:, 14:15]))
        k.dve(lambda: v.tensor_scalar(sb["scl"][:, 9:10], sb["scl"][:, 9:10],
                                      448.0 * 6.0, None, OP.mult))
        k.dve(lambda: v.tensor_scalar(sb["scl"][:, 10:11], sb["scl"][:, 9:10],
                                      1.0 / 6.0, None, OP.mult))
        k.dve(lambda: v.reciprocal(sb["scl"][:, 11:12], sb["scl"][:, 9:10]))
        k.dve(lambda sci=sci, actbase=actbase: v.tensor_tensor(
            sb["scl"][:, sci:sci + 1], sb["scl"][:, 9:10],
            sb["scl"][:, actbase:actbase + 1], OP.mult))
        k.dve(lambda sci=sci: v.reciprocal(sb["scl"][:, sci:sci + 1],
                                           sb["scl"][:, sci:sci + 1]))
        yield
        for rc in range(nrch):
            for qq in range(nq):
                k.wait("sp", "dve")
                k.wait("sp", "act")
                k.load(ldb, dram[nm][rc * 128:(rc + 1) * 128, qq * NQ:(qq + 1) * NQ],
                       "wql")
                k.wld("dve", "wql")
                k.wait("dve", "st")
                _quant_tile(nc, k, sb, ps, src=ldb, n=NQ, scol=9,
                            dst=stage, dst_stride=128, dst_c0=0, dst_off=0,
                            src_slot="wql", bufs=(TA, NL, NM, SG, QO))
                k.wait("sp", "dve")
                k.store(dram[qn].rearrange("(c p) o -> p c o", p=128)[
                    :, qq * 8:(qq + 1) * 8, rc * 128:(rc + 1) * 128],
                    stage.rearrange("p (c o) -> p c o", o=128))
                yield


def _finish(k):
    k.wait("sp", "st", k.cnt["st"])
    for s in list(k.slots):
        k.wld("sp", s)


def _layer(nc, k, dram, sb, ps, l, tap):
    v = nc.vector
    xsrc = dram["x0"] if l == 0 else dram["xcur"]
    xdst = dram["xcur"] if l == 0 else dram["y"]
    if tap == "l0" and l == 0:
        xdst = dram["y"]
    y3 = dram["y"].rearrange("(a p) b -> p a b", p=128)

    k.wait("sp", "act")
    k.wait("sp", "dve")
    k.load(sb["bqk"][:, :], dram["bqk%d" % l][:, :], "bqk")

    def tap_store(src_f32_flat, n):
        k.wait("sp", "dve")
        k.store(y3[:, 0:n // E, :],
                src_f32_flat.rearrange("p (a b) -> p a b", b=E))

    _ph_ln(nc, k, dram, sb, ps, l, xsrc)
    if tap == "ln%d" % l:
        tap_store(sb["A2"][:, :].bitcast(F32), 8192)
        return
    _ph_qk(nc, k, dram, sb, ps, l, tap)
    if tap is not None and (tap.startswith("qk") or tap.startswith("qs")) and tap[2:].isdigit():
        return
    if tap == "qka%d" % l:
        return
    _ph_v(nc, k, dram, sb, ps, l)
    if tap == "vv%d" % l:
        return
    _ph_mbias(nc, k, dram, sb, ps, l)
    if tap == "qkv%d" % l:
        tap_store(sb["AV"][:, 0:16384].bitcast(F32), 8192)
        k.store(y3[:, 15, 0:64], sb["Bsb"][:, :])
        return
    if tap == "q%d" % l:
        k.wait("sp", "dve")
        k.wait("sp", "act")
        k.store(dram["y"][0:1024, 0:1024], dram["Qs"][:, :].bitcast(F32))
        return
    if tap == "k%d" % l:
        k.wait("sp", "dve")
        k.wait("sp", "act")
        k.store(dram["y"][0:1024, 0:1024], dram["Ks"][:, :].bitcast(F32))
        return
    _ph_att(nc, k, dram, sb, ps, l)
    if tap == "att%d" % l:
        tap_store(sb["A2"][:, :].bitcast(F32), 8192)
        return
    _ph_wo(nc, k, dram, sb, ps, l)
    if tap == "wo%d" % l:
        tap_store(sb["AV"][:, 0:16384].bitcast(F32), 8192)
        return
    if l == 0:
        _ph_ffn0(nc, k, dram, sb, ps, xsrc, xdst)
    else:
        _ph_ffn1(nc, k, dram, sb, ps, xsrc, xdst)



def _ph_ln(nc, k, dram, sb, ps, l, xsrc):
    v, t, a = nc.vector, nc.tensor, nc.scalar
    xsv = xsrc.rearrange("(t p) e -> p t e", p=128)
    xbuf = sb["wblk"][:, 0:2048].rearrange("p (b e) -> p b e", b=2)
    nbuf = sb["wblk"][:, 2048:4096].rearrange("p (b e) -> p b e", b=2)
    xnT = sb["A2"]
    st2 = sb["stats"][:, 0:2 * NT].rearrange("p (t two) -> p t two", two=2)
    k.wait("sp", "st")
    k.wait("sp", "pe")
    k.wait("sp", "dve")
    k.wait("sp", "act")
    tld = [0, 0]
    varc = sb["stats"][:, 2 * NT:3 * NT]
    tmp16 = sb["stats"][:, 3 * NT:4 * NT]
    sink = sb["wblk"][:, 4096:5120]
    for i in range(NT):
        b = i % 2
        k.wait("sp", "dve")
        k.wait("sp", "act")
        k.load(xbuf[:, b], xsv[:, i], "xb%d" % b)
        k.wld("dve", "xb%d" % b)
        k.wld("act", "xb%d" % b)
        k.wait("act", "dve")
        def fsq(b=b):
            return a.activation(sink, xbuf[:, b], AF.Square)
        k.act(fsq)
        k.wait("dve", "act")
        def f(i=i, b=b):
            v.tensor_reduce(sb["stats"][:, 2 * i:2 * i + 1], xbuf[:, b], AX.X, OP.add)
            return v.tensor_reduce(varc[:, i:i + 1], sink, AX.X, OP.add)
        k.dve(f)
    k.dve(lambda: v.tensor_scalar(st2[:, :, 0], st2[:, :, 0], 1.0 / E, None, OP.mult))
    k.dve(lambda: v.tensor_scalar(varc, varc, 1.0 / E, None, OP.mult))
    k.dve(lambda: v.tensor_tensor(tmp16, st2[:, :, 0], st2[:, :, 0], OP.mult))
    k.dve(lambda: v.tensor_tensor(varc, varc, tmp16, OP.subtract))
    k.dve(lambda: v.tensor_scalar(varc, varc, EPS, None, OP.add))
    k.wait("act", "dve")
    k.act(lambda: a.activation(varc, varc, AF.Sqrt))
    k.wait("dve", "act")
    k.dve(lambda: v.reciprocal(varc, varc))
    for w in range(NW):
        for j in range(4):
            i = 4 * w + j
            b = i % 2
            k.wait("sp", "dve")
            k.wait("sp", "act")
            k.load(xbuf[:, b], xsv[:, i], "xb%d" % b)
            k.wld("dve", "xb%d" % b)
            k.wld("act", "xb%d" % b)
            k.wait("dve", "pe")
            k.wait("dve", "act")
            xt = nbuf[:, b]
            def fap(i=i, b=b, xt=xt):
                return v.tensor_scalar(xt, xbuf[:, b],
                                       sb["stats"][:, 2 * i:2 * i + 1],
                                       sb["stats"][:, 2 * NT + i:2 * NT + i + 1],
                                       OP.subtract, OP.mult)
            k.dve(fap)
            if l == 1:
                _quant_tile(nc, k, sb, ps, src=xt, n=E, scol=0,
                            dst=sb["xqT"], dst_stride=S, dst_c0=0, dst_off=i * 128)
            k.wait("pe", "dve")
            for half in range(2):
                bk = "b%d" % (4 + half)
                k.bw(bk)
                for c4 in range(4):
                    c = half * 4 + c4
                    def ftr(c=c, c4=c4, bk=bk, xt=xt):
                        return t.transpose(ps[bk][:, c4 * 128:(c4 + 1) * 128],
                                           xt[:, c * 128:(c + 1) * 128],
                                           sb["identf"][:, :])
                    (k.peq if c4 < 3 else k.pe)(ftr)
                k.bwd(bk)
                k.br("dve", bk)
                def fev(i=i, half=half, bk=bk):
                    dst = xnT[:, :].rearrange("p (c s) -> p c s", s=S)[
                        :, half * 4:(half + 1) * 4, i * 128:(i + 1) * 128]
                    return v.tensor_copy(dst, ps[bk][:, :].rearrange(
                        "p (c q) -> p c q", q=128))
                k.dve(fev)
                k.brd("dve", bk)
        k.wait("sp", "dve")
        k.store(dram["xn_fm"].rearrange("(c p) s -> p c s", p=128)[
            :, :, w * 512:(w + 1) * 512],
            xnT[:, :].rearrange("p (c s) -> p c s", s=S)[:, :, w * 512:(w + 1) * 512])


def _quant_tile(nc, k, sb, ps, src, n, scol, dst, dst_stride, dst_c0, dst_off,
                src_slot=None, bufs=None):
    v, t, a = nc.vector, nc.tensor, nc.scalar
    nb = n // 16
    w2f = sb["wblk2"]
    w2b = sb["wblk2"][:, :].bitcast(BF16)
    if bufs is None:
        TA = w2f[:, 0:n]
        NL = w2b[:, 4096:4096 + n]
        NM = w2b[:, 6144:6144 + n]
        SG = w2b[:, 8192:8192 + n]
        QO = w2b[:, 10240:10240 + n]
    else:
        TA, NL, NM, SG, QO = bufs
    sm = sb["qsml"]
    amax, sc, c1f, sc8, r2, msk = (sm[:, i * 128:i * 128 + nb] for i in range(6))
    f8a = sb["qfp8"][:, 0:nb]
    f8b = sb["qfp8"][:, 128:128 + nb]
    gs6 = sb["scl"][:, scol + 1:scol + 2]
    ginv = sb["scl"][:, scol + 2:scol + 3]

    k.wait("act", "dve")
    if src_slot is not None:
        k.wld("act", src_slot)
    k.act(lambda: a.activation(SG, src, AF.Sign))
    k.dve(lambda: v.tensor_reduce(amax, src.rearrange("p (b s) -> p b s", s=16),
                                  AX.X, OP.max, apply_absolute_value=True))
    k.dve(lambda: v.tensor_scalar(sc, amax, gs6, 448.0, OP.mult, OP.min))
    k.dve(lambda: v.tensor_scalar(f8a, sc, 240.0, None, OP.min))
    k.dve(lambda: v.tensor_scalar(c1f, f8a, 1.0, None, OP.mult))
    k.dve(lambda: v.tensor_scalar(f8b, sc, 0.5, None, OP.mult))
    k.dve(lambda: v.tensor_scalar(sc8, f8b, 2.0, None, OP.mult))
    k.dve(lambda: v.tensor_scalar(msk, sc, 240.0, None, OP.is_gt))
    k.dve(lambda: v.tensor_tensor(sc8, sc8, c1f, OP.subtract))
    k.dve(lambda: v.tensor_tensor(sc8, sc8, msk, OP.mult))
    k.dve(lambda: v.tensor_tensor(sc8, sc8, c1f, OP.add))
    k.dve(lambda: v.tensor_scalar(r2, sc8, ginv, None, OP.mult))
    k.dve(lambda: v.tensor_scalar(msk, r2, 0.0, None, OP.is_equal))
    k.dve(lambda: v.tensor_tensor(r2, r2, msk, OP.add))
    k.dve(lambda: v.reciprocal(r2, r2))
    r2b = r2.unsqueeze(2).broadcast_to((128, nb, 16))
    sc8b = sc8.unsqueeze(2).broadcast_to((128, nb, 16))
    k.dve(lambda: v.tensor_tensor(TA.rearrange("p (b s) -> p b s", s=16),
                                  src.rearrange("p (b s) -> p b s", s=16), r2b, OP.mult))
    k.dve(lambda: v.tensor_tensor(TA, TA, TA, OP.mult))
    k.dve(lambda: v.tensor_scalar(NL, TA, 0.0625, None, OP.is_ge))
    for thr in (0.5625, 1.5625, 3.0625):
        k.dve(lambda thr=thr: v.scalar_tensor_tensor(NL, TA, thr, NL, OP.is_ge, OP.add))
    k.dve(lambda: v.tensor_scalar(NM, TA, 6.25, None, OP.is_ge))
    k.dve(lambda: v.scalar_tensor_tensor(NM, TA, 12.25, NM, OP.is_ge, OP.add))
    k.dve(lambda: v.scalar_tensor_tensor(NM, NL, 0.5, NM, OP.mult, OP.add))
    k.dve(lambda: v.tensor_scalar(NL, TA, 25.0, None, OP.is_ge))
    k.dve(lambda: v.scalar_tensor_tensor(NM, NL, 2.0, NM, OP.mult, OP.add))
    k.wait("dve", "act")
    k.dve(lambda: v.tensor_tensor(SG.rearrange("p (b s) -> p b s", s=16),
                                  SG.rearrange("p (b s) -> p b s", s=16), sc8b, OP.mult))
    k.dve(lambda: v.tensor_tensor(QO, NM, SG, OP.mult))
    k.wait("pe", "dve")
    nch = n // 128
    for grp in range((nch + 7) // 8):
        bk = "t%d" % (grp % 2)
        ng = min(8, nch - grp * 8)
        k.bw(bk)
        for j in range(ng):
            c = grp * 8 + j
            def ftr(c=c, j=j, bk=bk):
                return t.transpose(ps[bk][:, j * 128:(j + 1) * 128],
                                   QO[:, c * 128:(c + 1) * 128], sb["identb"][:, :])
            (k.peq if j < ng - 1 else k.pe)(ftr)
        k.bwd(bk)
        k.br("dve", bk)
        def fev(grp=grp, bk=bk, ng=ng):
            dstv = dst[:, :].rearrange("p (c s) -> p c s", s=dst_stride)[
                :, dst_c0 + grp * 8: dst_c0 + grp * 8 + ng, dst_off:dst_off + 128]
            return v.tensor_copy(dstv, ps[bk][:, :].rearrange(
                "p (c q) -> p c q", q=128)[:, 0:ng, :])
        k.dve(fev)
        k.brd("dve", bk)



def _ph_qk(nc, k, dram, sb, ps, l, tap=None):
    lim = int(tap[2:]) if (tap and tap.startswith("qk") and tap[2:].isdigit()) else 10 ** 9
    sub = int(tap[2:]) if (tap and tap.startswith("qs")) else 10 ** 9
    if sub < 10 ** 9:
        lim = 1
    it = [0]
    k.fence_sp()
    v, t, a = nc.vector, nc.tensor, nc.scalar
    xnT = sb["A2"]
    wqk = dram["wqk%dT" % l]
    wst = sb["wblk"][:, :].rearrange("p (b c s) -> p b c s", b=2, c=8)
    wcast = sb["wblk2"][:, :].bitcast(BF16).rearrange(
        "p (x) -> p x")[:, 0:8192].rearrange("p (b c s) -> p b c s", b=2, c=8)
    if l == 0:
        k.load(sb["wblk2"][:, 4096:6144], dram["rope_cos"][:, :].rearrange(
            "p (a s) -> p a s", a=1).squeeze(1)) if False else None
        k.wait("sp", "dve")
        k.load(sb["wblk2"][:, 4096:6144], dram["rope_cos"][:, :], "cossin")
        k.load(sb["wblk2"][:, 6144:8192], dram["rope_sin"][:, :], "cossin")
        k.wld("dve", "cossin")
    cosT = sb["wblk2"][:, 4096:6144]
    sinT = sb["wblk2"][:, 6144:8192]
    k.wld("act", "bqk")
    k.wld("dve", "bqk")
    wuser = [0, 0]
    cuser = [0, 0]
    for db in range(4):
        buf = db % 2
        k.wait("sp", "act", cuser[buf])
        k.load(wst[:, buf], wqk.rearrange("(c p) d -> p c d", p=128)[
            :, :, db * 512:(db + 1) * 512], "wst%d" % buf)
        k.wait("act", "pe", wuser[buf])
        k.wld("act", "wst%d" % buf)
        def fcast(buf=buf):
            return a.activation(wcast[:, buf], wst[:, buf], AF.Identity)
        k.act(fcast)
        cuser[buf] = k.cnt["act"]
        tcast = k.cnt["act"]
        for dc in range(4):
            g = db * 4 + dc
            isq = g < 8
            for w in range(NW):
                if it[0] >= lim:
                    return
                it[0] += 1
                bk = "b%d" % (w % 2)
                k.wait("pe", "act", tcast)
                k.bw(bk)
                for fc in range(NC):
                    def fmm(fc=fc, dc=dc, w=w, buf=buf, bk=bk):
                        lhsT = wcast[:, buf, fc, dc * 128:(dc + 1) * 128]
                        rhs = xnT[:, :].rearrange("p (c s) -> p c s", s=S)[
                            :, fc, w * 512:(w + 1) * 512]
                        return t.matmul(ps[bk][:, :], lhsT, rhs,
                                        start=(fc == 0), stop=(fc == NC - 1))
                    (k.peq if fc < NC - 1 else k.pe)(fmm)
                k.bwd(bk)
                wuser[buf] = k.cnt["pe"]
                if sub <= 1:
                    return
                qb = w % 2
                qkout = sb["qkrot"][:, qb * 512:(qb + 1) * 512]
                k.wait("act", "st")
                k.wait("act", "dve")
                k.br("act", bk)
                def fqb(g=g, bk=bk, qkout=qkout):
                    return a.activation(qkout, ps[bk][:, :], AF.Identity,
                                        bias=sb["bqk"][:, g:g + 1])
                k.act(fqb)
                k.brd("act", bk)
                if sub <= 2:
                    return
                if l == 0:
                    k.wait("pe", "act")
                    k.bw("b2")
                    def frot(qkout=qkout):
                        return t.matmul(ps["b2"][:, :], sb["rotPT"][:, :], qkout,
                                        start=True, stop=True)
                    k.pe(frot)
                    k.bwd("b2")
                    if sub <= 3:
                        return
                    k.br("dve", bk)
                    k.wait("dve", "act")
                    def fq0(g=g, w=w, bk=bk):
                        return v.scalar_tensor_tensor(
                            sb["rtmp"][:, 0:512], ps[bk][:, :], sb["bqk"][:, g:g + 1],
                            cosT[:, w * 512 - 4096 + 4096:w * 512 + 512] if False else
                            sb["wblk2"][:, 4096 + w * 512:4096 + (w + 1) * 512],
                            OP.add, OP.mult)
                    k.dve(fq0)
                    k.brd("dve", bk)
                    if sub <= 41:
                        return
                    k.br("dve", "b2")
                    def ft2(w=w):
                        return v.tensor_tensor(
                            sb["rtmp"][:, 512:1024], ps["b2"][:, :],
                            sb["wblk2"][:, 6144 + w * 512:6144 + (w + 1) * 512], OP.mult)
                    k.dve(ft2)
                    k.brd("dve", "b2")
                    if sub <= 42:
                        return
                    k.wait("dve", "st")
                    def fsum(qkout=qkout):
                        return v.tensor_tensor(qkout, sb["rtmp"][:, 0:512],
                                               sb["rtmp"][:, 512:1024], OP.add)
                    k.dve(fsum)
                    src_eng = "dve"
                    if sub <= 4:
                        return
                else:
                    src_eng = "act"
                if src_eng == "act":
                    k.wait("dve", "act")
                k.wait("dve", "pe")
                def fq2(qkout=qkout):
                    return v.tensor_tensor(sb["at"][:, 0:512], qkout, qkout, OP.mult)
                k.dve(fq2)
                k.wait("pe", "dve")
                k.bw("b3")
                def fnm():
                    return t.matmul(ps["b3"][0:2, 0:512], sb["headmaskb"][:, :],
                                    sb["at"][:, 0:512], start=True, stop=True)
                k.pe(fnm)
                k.bwd("b3")
                k.br("dve", "b3")
                col = g * NW + w
                def fnr(col=col):
                    return v.tensor_reduce(sb["qw"][0:2, col:col + 1],
                                           ps["b3"][0:2, 0:512], AX.X, OP.max)
                k.dve(fnr)
                k.brd("dve", "b3")
                if sub <= 5:
                    return
                k.wait("sp", src_eng)
                dstd = dram["Qs"] if isq else dram["Ks"]
                row0 = (g if isq else g - 8) * 128
                k.store(dstd[row0:row0 + 128, w * 512:(w + 1) * 512], qkout)
                if l == 1:
                    k.do_pump(1)



def _ph_v(nc, k, dram, sb, ps, l):
    k.fence_sp()
    v, t, a = nc.vector, nc.tensor, nc.scalar
    xnT = sb["A2"]
    Vsb = sb["AV"]
    k.wait("sp", "pe")
    k.wait("sp", "dve")
    k.load(sb["brow"][:, :], dram["bv%d" % l][:, :], "brow")
    k.load(sb["wblk"][:, :].rearrange("p (c d) -> p c d", c=8),
           dram["wv%dT" % l].rearrange("(c p) d -> p c d", p=128), "wvf")
    k.wld("act", "wvf")
    k.wld("dve", "brow")
    k.wld("pe", "brow")
    k.wait("act", "pe")
    k.act(lambda: a.activation(sb["wblk2"][:, :].bitcast(BF16)[:, 0:8192],
                               sb["wblk"][:, :], AF.Identity))
    tcastv = k.cnt["act"]
    wvb = sb["wblk2"][:, :].bitcast(BF16)[:, 0:8192].rearrange("p (c d) -> p c d", c=8)
    k.dve(lambda: v.memset(
        Vsb[:, :].rearrange("p (tt h c) -> p tt h c", tt=NT, h=16)[:, :, :, 64], 1.0))
    k.wait("pe", "act", tcastv)
    k.wait("pe", "dve")
    for tt in range(NT):
        for dh in range(2):
            bk = "b%d" % dh
            k.bw(bk)
            for fc in range(NC):
                def fmm(fc=fc, tt=tt, dh=dh, bk=bk):
                    lhsT = xnT[:, :].rearrange("p (c s) -> p c s", s=S)[
                        :, fc, tt * 128:(tt + 1) * 128]
                    rhs = wvb[:, fc, dh * 512:(dh + 1) * 512]
                    return t.matmul(ps[bk][:, :], lhsT, rhs, start=(fc == 0), stop=False)
                k.peq(fmm)
            def fbias(dh=dh, bk=bk):
                return t.matmul(ps[bk][:, :], sb["ones1"][:, :],
                                sb["brow"][0:1, dh * 512:(dh + 1) * 512],
                                start=False, stop=True)
            k.pe(fbias)
            k.bwd(bk)
            k.br("dve", bk)
            def fev(tt=tt, dh=dh, bk=bk):
                dst = Vsb[:, :].rearrange("p (tt h c) -> p tt h c", tt=NT, h=16)[
                    :, tt, dh * 8:(dh + 1) * 8, 0:64]
                return v.tensor_copy(dst, ps[bk][:, :].rearrange("p (h d) -> p h d", d=64))
            k.dve(fev)
            k.brd("dve", bk)
        if l == 1:
            k.do_pump(1)


def _ph_mbias(nc, k, dram, sb, ps, l):
    v, t, a = nc.vector, nc.tensor, nc.scalar
    def fkm():
        return v.tensor_reduce(
            sb["cmaxK"][0:2, 0:8],
            sb["qw"][0:2, 32:64].rearrange("p (g w) -> p g w", w=NW), AX.X, OP.max)
    k.dve(fkm)
    def fpr():
        knB = sb["cmaxK"][0:2, 0:8].unsqueeze(2).broadcast_to((2, 8, NW))
        return v.tensor_tensor(
            sb["mrow"][0:2, 0:32].rearrange("p (g w) -> p g w", w=NW),
            sb["qw"][0:2, 0:32].rearrange("p (g w) -> p g w", w=NW), knB, OP.mult)
    k.dve(fpr)
    k.wait("act", "dve")
    k.act(lambda: a.activation(sb["mrow"][0:2, 0:32], sb["mrow"][0:2, 0:32], AF.Sqrt))
    k.wait("dve", "act")
    def fb():
        src = sb["mrow"][0:2, 0:32].rearrange("p (g w) -> p g w", w=4)
        src = src.unsqueeze(2).broadcast_to((2, 8, 2, 4))
        return v.tensor_tensor(
            sb["mrow"][0:2, 32:96].rearrange("p (g r w) -> p g r w", g=8, r=2), src,
            sb["parity"][:, :].rearrange("p (g r w) -> p g r w", g=8, r=2), OP.mult)
    k.dve(fb)
    k.wait("pe", "dve")
    k.bw("b3")
    k.pe(lambda: t.matmul(ps["b3"][:, 0:64], sb["ones2"][:, :], sb["mrow"][0:2, 32:96],
                          start=True, stop=True))
    k.bwd("b3")
    k.br("dve", "b3")
    k.dve(lambda: v.tensor_copy(sb["Bsb"][:, :], ps["b3"][:, 0:64]))
    k.brd("dve", "b3")



def _ph_att(nc, k, dram, sb, ps, l):
    k.fence_sp()
    v, t, a = nc.vector, nc.tensor, nc.scalar
    C_ = 11.0
    Vsb = sb["AV"]
    otm = sb["A2"]
    wb16 = sb["wblk"][:, :].bitcast(BF16)
    khv = wb16[:, 0:4160].rearrange("p (b s) -> p b s", b=2)
    qhv = wb16[:, 4160:5200].rearrange("p (b s) -> p b s", b=2)
    mrow_p = sb["wblk"][:, 2700:2704]
    k.dve(lambda: v.tensor_scalar(sb["Bsb"][:, :], sb["Bsb"][:, :], 1.0, None, OP.mult)) if False else None
    k.dve(lambda: v.tensor_scalar(sb["mrow"][0:2, 0:64] if False else sb["Bsb"][:, :],
                                  sb["Bsb"][:, :], 1.0, None, OP.bypass)) if False else None
    k.dve(lambda: v.tensor_scalar(sb["cmaxM"][:, 0:16], sb["cmaxM"][:, 0:16], 1.0,
                                  None, OP.mult)) if False else None
    Bc = sb["qw"][:, 0:64]
    k.dve(lambda: v.tensor_scalar(Bc, sb["Bsb"][:, :], 1.0 / C_, None, OP.mult))
    k.dve(lambda: v.memset(khv[64:65, 0:2, 0:2048].squeeze(0) if False else
                           sb["wblk"][:, :].bitcast(BF16)[64:65, 0:4160], 1.0))
    k.wait("act", "dve")
    k.wait("pe", "dve")
    kh_user = [0, 0]
    qh_user = [0, 0]
    at_user = [0, 0]
    for h in range(H):
        kb = h % 2
        row0 = (h // 2) * 128 + (h % 2) * 64
        k.wait("sp", "pe", kh_user[kb])
        k.wait("sp", "st")
        k.load(khv[0:64, kb, 0:2048], dram["Ks"][row0:row0 + 64, :], "kh%d" % kb)
        for w in range(NW):
            qb = w % 2
            kcmax = NT if l == 0 else 4 * (w + 1)
            k.wait("sp", "pe", qh_user[qb])
            k.load(qhv[0:64, qb, 0:512], dram["Qs"][row0:row0 + 64, w * 512:(w + 1) * 512],
                   "qh%d" % qb)
            k.bw("b3")
            def r1_s(kc):
                sbk = "b%d" % (kc % 2)
                k.wld("pe", "kh%d" % kb)
                k.wld("pe", "qh%d" % qb)
                k.bw(sbk)
                def fs(kc=kc, kb=kb, qb=qb, sbk=sbk):
                    return t.matmul(ps[sbk][:, :], khv[0:64, kb, kc * 128:(kc + 1) * 128],
                                    qhv[0:64, qb, 0:512], start=True, stop=True)
                k.pe(fs)
                k.bwd(sbk)
                kh_user[kb] = qh_user[qb] = k.cnt["pe"]
            def r1_t(kc):
                sbk = "b%d" % (kc % 2)
                atb = sb["at"][:, (kc % 2) * 512:(kc % 2 + 1) * 512]
                k.wait("act", "pe", at_user[kc % 2])
                k.br("act", sbk)
                def fe(h=h, w=w, sbk=sbk, atb=atb):
                    return a.activation(atb, ps[sbk][:, :], AF.Exp,
                                        bias=Bc[:, h * NW + w:h * NW + w + 1],
                                        scale=0.125 / C_)
                k.act(fe)
                k.brd("act", sbk)
                if l == 1 and kc >= 4 * w:
                    off = 384 + 512 * w - 128 * kc
                    k.wait("dve", "act")
                    def fm(atb=atb, off=off):
                        return v.tensor_tensor(atb, atb, sb["causal"][:, off:off + 512],
                                               OP.mult)
                    k.dve(fm)
                    k.wait("pe", "dve")
                else:
                    k.wait("pe", "act")
                def fsum(kc=kc, atb=atb, kcm=kcmax):
                    return t.matmul(ps["b3"][64:65, 0:512], sb["onescol"][:, :], atb,
                                    start=(kc == 0), stop=(kc == kcm - 1),
                                    tile_position=(0, 64))
                k.pe(fsum)
                at_user[kc % 2] = k.cnt["pe"]
            r1_s(0)
            for kc in range(1, kcmax):
                r1_s(kc)
                r1_t(kc - 1)
            r1_t(kcmax - 1)
            k.bwd("b3")
            k.br("dve", "b3")
            mq = sb["wblk2"][64:65, 0:512]
            def fev():
                return v.tensor_scalar(mq, ps["b3"][64:65, 0:512], 1e-30, None, OP.add)
            k.dve(fev)
            k.brd("dve", "b3")
            k.wait("act", "dve")
            k.act(lambda: a.activation(mq, mq, AF.Ln))
            k.wait("dve", "act")
            def fmq(h=h, w=w):
                return v.tensor_scalar(mq, mq, C_, sb["Bsb"][64:65, h * NW + w:h * NW + w + 1],
                                       OP.mult, OP.subtract)
            k.dve(fmq)
            k.wld("dve", "qh%d" % qb)
            def faug(qb=qb):
                return v.tensor_scalar(qhv[64:65, qb, 0:512], mq, -8.0, None, OP.mult)
            k.dve(faug)
            k.wait("pe", "dve")
            def r2_s(kc):
                sbk = "b%d" % (kc % 2)
                k.bw(sbk)
                def fs2(kc=kc, kb=kb, qb=qb, sbk=sbk):
                    return t.matmul(ps[sbk][:, :], khv[0:65, kb, kc * 128:(kc + 1) * 128],
                                    qhv[0:65, qb, 0:512], start=True, stop=True)
                k.pe(fs2)
                k.bwd(sbk)
                kh_user[kb] = qh_user[qb] = k.cnt["pe"]
            def r2_av(kc):
                sbk = "b%d" % (kc % 2)
                atb = sb["at"][:, (kc % 2) * 512:(kc % 2 + 1) * 512]
                k.wait("act", "pe", at_user[kc % 2])
                k.br("act", sbk)
                def fe2(sbk=sbk, atb=atb):
                    return a.activation(atb, ps[sbk][:, :], AF.Exp, scale=0.125)
                k.act(fe2)
                k.brd("act", sbk)
                if l == 1 and kc >= 4 * w:
                    off = 384 + 512 * w - 128 * kc
                    k.wait("dve", "act")
                    def fm2(atb=atb, off=off):
                        return v.tensor_tensor(atb, atb, sb["causal"][:, off:off + 512],
                                               OP.mult)
                    k.dve(fm2)
                    k.wait("pe", "dve")
                else:
                    k.wait("pe", "act")
                if kc == 0:
                    k.bw("b2")
                for sub in range(4):
                    def fav(h=h, kc=kc, sub=sub, atb=atb,
                            st=(kc == 0 and sub == 0),
                            sp_=(kc == kcmax - 1 and sub == 3)):
                        return t.matmul(
                            ps["b2"][:, sub * 65:sub * 65 + 65],
                            atb[:, sub * 128:(sub + 1) * 128],
                            Vsb[:, :].rearrange("p (tt c) -> p tt c", c=16 * 65)[
                                :, kc, h * 65:h * 65 + 65],
                            start=st, stop=sp_)
                    (k.peq if sub < 3 else k.pe)(fav)
                at_user[kc % 2] = k.cnt["pe"]
            r2_s(0)
            for kc in range(1, kcmax):
                r2_s(kc)
                r2_av(kc - 1)
            r2_av(kcmax - 1)
            k.bwd("b2")
            if l == 1:
                k.do_pump(2)
            k.br("dve", "b2")
            pv = ps["b2"][:, 0:260].rearrange("p (sub x) -> p sub x", x=65)
            k.dve(lambda: v.reciprocal(sb["rsum"][:, :], pv[:, 0:4, 64]))
            for sub in range(4):
                def fn(h=h, w=w, sub=sub):
                    tt = w * 4 + sub
                    return v.tensor_scalar(
                        otm[:, tt * E + h * 64: tt * E + (h + 1) * 64],
                        pv[:, sub, 0:64], sb["rsum"][:, sub:sub + 1], None, OP.mult)
                k.dve(fn)
            k.brd("dve", "b2")


def _ph_wo(nc, k, dram, sb, ps, l):
    k.fence_sp()
    v, t, a = nc.vector, nc.tensor, nc.scalar
    otm = sb["A2"]
    ofm = sb["wblk"][:, :].bitcast(BF16)
    stm = sb["AV"][:, 0:16384]
    wob = sb["wsh"]
    k.wait("sp", "pe")
    k.wait("sp", "dve")
    k.load(sb["brow"][:, :], dram["bo%d" % l][:, :], "brow")
    k.wld("dve", "brow")
    k.dve(lambda: v.tensor_copy(sb["browb"][:, :], sb["brow"][0:1, :]))
    for hf in range(2):
        k.wait("sp", "act")
        k.load(sb["wblk2"][:, 0:4096].rearrange("p (c d) -> p c d", c=8),
               dram["wo%dT" % l].rearrange("(c p) d -> p c d", p=128)[
                   :, :, hf * 512:(hf + 1) * 512], "wos")
        k.wld("act", "wos")
        def fcast(hf=hf):
            return a.activation(
                wob[:, :].rearrange("p (c d) -> p c d", c=8)[:, :, hf * 512:(hf + 1) * 512],
                sb["wblk2"][:, 0:4096].rearrange("p (c d) -> p c d", c=8), AF.Identity)
        k.act(fcast)
    k.wait("pe", "dve")
    for c in range(NC):
        for tg in range(2):
            bk = "t%d" % (tg % 2)
            k.bw(bk)
            for j in range(8):
                tt = tg * 8 + j
                def ftr(c=c, tt=tt, j=j, bk=bk):
                    return t.transpose(ps[bk][:, j * 128:(j + 1) * 128],
                                       otm[:, tt * E + c * 128:tt * E + (c + 1) * 128],
                                       sb["identb"][:, :])
                (k.peq if j < 7 else k.pe)(ftr)
            k.bwd(bk)
            k.br("dve", bk)
            def fev(c=c, tg=tg, bk=bk):
                return v.tensor_copy(
                    ofm[:, c * S + tg * 1024: c * S + (tg + 1) * 1024], ps[bk][:, :])
            k.dve(fev)
            k.brd("dve", bk)
    k.wait("pe", "act")
    for tt in range(NT):
        for dh in range(2):
            bk = "b%d" % (3 + dh)
            k.bw(bk)
            k.wait("pe", "dve")
            for c in range(NC):
                def fmm(tt=tt, dh=dh, c=c, bk=bk):
                    lhsT = ofm[:, c * S + tt * 128: c * S + (tt + 1) * 128]
                    rhs = wob[:, c * E + dh * 512: c * E + (dh + 1) * 512]
                    return t.matmul(ps[bk][:, :], lhsT, rhs, start=(c == 0), stop=False)
                k.peq(fmm)
            def fbias(dh=dh, bk=bk):
                return t.matmul(ps[bk][:, :], sb["onesb"][:, :],
                                sb["browb"][0:1, dh * 512:(dh + 1) * 512],
                                start=False, stop=True)
            k.pe(fbias)
            k.bwd(bk)
            dstv = stm.rearrange("p (tt e) -> p tt e", e=E)
            if l == 0:
                k.br("dve", bk)
                def fev2(tt=tt, dh=dh, bk=bk, dstv=dstv):
                    return v.tensor_copy(dstv[:, tt, dh * 512:(dh + 1) * 512], ps[bk][:, :])
                k.dve(fev2)
                k.brd("dve", bk)
            else:
                k.br("dve", bk)
                def fmx(bk=bk):
                    return v.tensor_scalar(sb["qkrot"][:, 0:512], ps[bk][:, :],
                                           0.0, None, OP.max)
                k.dve(fmx)
                def fmn(bk=bk):
                    return v.tensor_scalar(sb["qkrot"][:, 512:1024], ps[bk][:, :],
                                           0.0, None, OP.min)
                k.dve(fmn)
                k.brd("dve", bk)
                def fadd2(tt=tt, dh=dh, dstv=dstv):
                    return v.tensor_tensor(dstv[:, tt, dh * 512:(dh + 1) * 512],
                                           sb["qkrot"][:, 0:512],
                                           sb["qkrot"][:, 512:1024], OP.add)
                k.dve(fadd2)



def _ph_ffn0(nc, k, dram, sb, ps, xsrc, xdst):
    k.fence_sp()
    v, t, a = nc.vector, nc.tensor, nc.scalar
    hwin = sb["A2"]
    wbf = sb["wblk"]
    wbb = sb["wblk"][:, :].bitcast(BF16)
    w1st = wbf[:, 0:2048].rearrange("p (b c s) -> p b c s", b=2, c=8)
    w1bf = wbb[:, 4096:6144].rearrange("p (b c s) -> p b c s", b=2, c=8)
    w2st = wbf[:, 3072:4096].rearrange("p (b s) -> p b s", b=2)
    w2bf = wbb[:, 8192:9216].rearrange("p (b s) -> p b s", b=2)
    xnw = sb["wblk2"][:, :].bitcast(BF16)[:, 0:8192].rearrange(
        "p (b c s) -> p b c s", b=2, c=8)
    k.wait("sp", "pe")
    k.wait("sp", "dve")
    k.load(sb["brow"][:, :], dram["b2"][:, :], "brow")
    k.wld("pe", "brow")
    k.wait("sp", "st")
    w1_user = [0, 0]
    w2_user = [0, 0]
    xn_user = [0, 0]
    for w in range(NW):
        xb = w % 2
        k.wait("sp", "pe", max(xn_user[xb], w2_user[0], w2_user[1]))
        k.load(xnw[:, xb], dram["xn_fm"].rearrange("(c p) s -> p c s", p=128)[
            :, :, w * 512:(w + 1) * 512], "xnw%d" % xb)
        for ic in range(NIC):
            wb = ic % 2
            if wb == 0:
                k.wait("sp", "pe", max(w1_user[0], w1_user[1]))
                k.wait("sp", "dve")
                k.load(sb["wblk"][:, 0:2048].rearrange("p (c bi) -> p c bi", bi=256),
                       dram["W1"].rearrange("(c p) i -> p c i", p=128)[
                           :, :, ic * 128:(ic + 2) * 128], "w1p")
            k.wld("act", "w1p")
            k.wait("act", "pe", w1_user[wb])
            def fc1(wb=wb):
                w1v = sb["wblk"][:, 0:2048].rearrange("p (c b s) -> p c b s", b=2, s=128)
                return a.activation(w1bf[:, wb], w1v[:, :, wb, :], AF.Identity)
            k.act(fc1)
            tcast = k.cnt["act"]
            bk = ("b0", "b1", "b4", "b5")[ic % 4]
            k.wld("pe", "xnw%d" % xb)
            k.wait("pe", "act", tcast)
            k.bw(bk)
            for fc in range(NC):
                def fmm(fc=fc, wb=wb, xb=xb, bk=bk):
                    return t.matmul(ps[bk][:, :], w1bf[:, wb, fc], xnw[:, xb, fc],
                                    start=(fc == 0), stop=(fc == NC - 1))
                (k.peq if fc < NC - 1 else k.pe)(fmm)
            k.bwd(bk)
            w1_user[wb] = xn_user[xb] = k.cnt["pe"]
            k.br("dve", bk)
            def fre(ic=ic, bk=bk):
                return v.tensor_scalar(hwin[:, ic * 512:(ic + 1) * 512], ps[bk][:, :],
                                       sb["b1"][:, ic:ic + 1], 0.0, OP.add, OP.max)
            k.dve(fre)
            k.brd("dve", bk)
        for dh in range(2):
            for ic in range(NIC):
                wb2 = ic % 2
                if wb2 == 0:
                    k.wait("sp", "pe", max(w2_user[0], w2_user[1]))
                    k.wait("sp", "dve")
                    k.load(w2st[:, 0:2],
                           dram["W2"][ic * 128:(ic + 2) * 128,
                                      dh * 512:(dh + 1) * 512].rearrange(
                               "(b p) s -> p b s", p=128), "w2p")
                k.wld("act", "w2p")
                k.wait("act", "pe", w2_user[wb2])
                def fc2(wb2=wb2):
                    return a.activation(w2bf[:, wb2], w2st[:, wb2], AF.Identity)
                k.act(fc2)
                tcast = k.cnt["act"]
                k.wait("pe", "act", tcast)
                k.wait("pe", "dve")
                for tj in range(4):
                    bk = "b%d" % (2 + tj)
                    if ic == 0:
                        k.bw(bk)
                    def fmm(ic=ic, tj=tj, wb2=wb2, bk=bk):
                        return t.matmul(
                            ps[bk][:, :],
                            hwin[:, ic * 512 + tj * 128: ic * 512 + (tj + 1) * 128],
                            w2bf[:, wb2], start=(ic == 0), stop=False)
                    (k.peq if tj < 3 else k.pe)(fmm)
                w2_user[wb2] = k.cnt["pe"]
            for tj in range(4):
                bk = "b%d" % (2 + tj)
                def fbias(bk=bk, dh=dh):
                    return t.matmul(ps[bk][:, :], sb["ones1"][:, :],
                                    sb["brow"][0:1, dh * 512:(dh + 1) * 512],
                                    start=False, stop=True)
                k.pe(fbias)
                k.bwd(bk)
                tt = w * 4 + tj
                k.br("act", bk)
                k.wait("act", "dve")
                def fsi(tj=tj, dh=dh, bk=bk):
                    gw = sb["wblk"][:, :].bitcast(BF16)[:, 9216:13312]
                    return a.activation(
                        gw[:, tj * 1024 + dh * 512: tj * 1024 + (dh + 1) * 512],
                        ps[bk][:, :], AF.Silu)
                k.act(fsi)
                k.brd("act", bk)
        _merge_window(nc, k, dram, sb, xsrc, xdst, w,
                      sb["wblk"][:, :].bitcast(BF16)[:, 9216:13312])


def _merge_window(nc, k, dram, sb, xsrc, xdst, w, gwin):
    v = nc.vector
    stm = sb["AV"][:, 0:16384]
    for tj in range(4):
        tt = w * 4 + tj
        k.wait("sp", "st")
        k.wait("sp", "dve")
        k.load(sb["xio"][:, :], xsrc[tt * 128:(tt + 1) * 128, :], "xio")
        k.wld("dve", "xio")
        k.wait("dve", "act")
        def fmul(tt=tt, tj=tj):
            return v.tensor_tensor(sb["wblk2"][:, 4096:5120],
                                   stm[:, tt * E:(tt + 1) * E],
                                   gwin[:, tj * 1024:(tj + 1) * 1024], OP.mult)
        k.dve(fmul)
        def fadd():
            return v.tensor_tensor(sb["xio"][:, :], sb["xio"][:, :],
                                   sb["wblk2"][:, 4096:5120], OP.add)
        k.dve(fadd)
        k.wait("sp", "dve")
        k.store(xdst[tt * 128:(tt + 1) * 128, :], sb["xio"][:, :])



def _wquant(nc, k, dram, sb, ps):
    k.fence_sp()
    """gate/up/down -> bf16 q*sc8, transposed, in HBM (wgq/wuq/wdq).
    scl: [9]=gs_w, [10]=gs_w/6, [11]=1/gs_w (per weight);
    [6]/[7]/[8] = 1/(gs_act*gs_w) evac scales."""
    v, t, a = nc.vector, nc.tensor, nc.scalar
    specs = [("gw", "wgq", I, E, 0, 6), ("uw", "wuq", I, E, 0, 7),
             ("dw", "wdq", E, I, 3, 8)]
    wload = sb["wblk"][:, :].rearrange("p (b s) -> p b s", b=2)
    stage = sb["A2"]
    for (nm, qn, NR, NCL, actbase, sci) in specs:
        nrch = NR // 128
        for rc in range(nrch):
            buf = rc % 2
            k.wait("sp", "dve")
            k.load(wload[:, buf, 0:NCL], dram[nm][rc * 128:(rc + 1) * 128, :],
                   "wl%d" % buf)
            k.wld("dve", "wl%d" % buf)
            def fam(rc=rc, buf=buf, NCL=NCL):
                return v.tensor_reduce(sb["qw"][:, rc:rc + 1], wload[:, buf, 0:NCL],
                                       AX.X, OP.max, apply_absolute_value=True)
            k.dve(fam)
        k.dve(lambda nrch=nrch: v.tensor_reduce(sb["scl"][:, 15:16],
                                                sb["qw"][:, 0:nrch], AX.X, OP.max))
        k.wait("pe", "dve")
        k.bw("b5")
        k.pe(lambda: t.matmul(ps["b5"][0:1, 0:128], sb["scl"][:, 15:16],
                              sb["identf"][:, :], start=True, stop=True,
                              is_transpose=True))
        k.bwd("b5")
        k.br("dve", "b5")
        k.dve(lambda: v.tensor_reduce(sb["mrow"][0:1, 0:1], ps["b5"][0:1, 0:128],
                                      AX.X, OP.max))
        k.brd("dve", "b5")
        k.wait("pe", "dve")
        k.bw("b5")
        k.pe(lambda: t.matmul(ps["b5"][:, 1:2], sb["ones1"][:, :],
                              sb["mrow"][0:1, 0:1], start=True, stop=True))
        k.bwd("b5")
        k.br("dve", "b5")
        k.dve(lambda: v.tensor_copy(sb["scl"][:, 14:15], ps["b5"][:, 1:2]))
        k.brd("dve", "b5")
        k.dve(lambda: v.reciprocal(sb["scl"][:, 9:10], sb["scl"][:, 14:15]))
        k.dve(lambda: v.tensor_scalar(sb["scl"][:, 9:10], sb["scl"][:, 9:10],
                                      448.0 * 6.0, None, OP.mult))
        k.dve(lambda: v.tensor_scalar(sb["scl"][:, 10:11], sb["scl"][:, 9:10],
                                      1.0 / 6.0, None, OP.mult))
        k.dve(lambda: v.reciprocal(sb["scl"][:, 11:12], sb["scl"][:, 9:10]))
        k.dve(lambda sci=sci, actbase=actbase: v.tensor_tensor(
            sb["scl"][:, sci:sci + 1], sb["scl"][:, 9:10],
            sb["scl"][:, actbase:actbase + 1], OP.mult))
        k.dve(lambda sci=sci: v.reciprocal(sb["scl"][:, sci:sci + 1],
                                           sb["scl"][:, sci:sci + 1]))
        ncols = NCL // 128
        for rg in range(nrch // 4):
            k.wait("dve", "st")
            for j in range(4):
                rc = rg * 4 + j
                buf = rc % 2
                k.wait("sp", "dve")
                k.wait("sp", "act")
                k.load(wload[:, buf, 0:NCL], dram[nm][rc * 128:(rc + 1) * 128, :],
                       "wl%d" % buf)
                k.wld("dve", "wl%d" % buf)
                k.wld("act", "wl%d" % buf)
                nhalf = 2048 if NCL > 2048 else NCL
                for hh in range(NCL // nhalf):
                    _quant_tile(nc, k, sb, ps,
                                src=wload[:, buf, hh * nhalf:(hh + 1) * nhalf],
                                n=nhalf, scol=9,
                                dst=stage, dst_stride=512,
                                dst_c0=hh * (nhalf // 128), dst_off=j * 128,
                                src_slot="wl%d" % buf)
            k.wait("sp", "dve")
            k.store(dram[qn].rearrange("(c p) i -> p c i", p=128)[
                :, :, rg * 512:(rg + 1) * 512],
                stage[:, 0:ncols * 512].rearrange("p (c s) -> p c s", s=512))



def _ph_ffn1(nc, k, dram, sb, ps, xsrc, xdst):
    while k.pump is not None:
        k.do_pump(8)
    k.fence_sp()
    v, t, a = nc.vector, nc.tensor, nc.scalar
    hid = sb["A2"]
    hqw = sb["wblk"][:, :].bitcast(BF16)
    wq2 = sb["wsh"][:, :].rearrange("p (b s) -> p b s", b=2)
    xqT = sb["xqT"]
    wq_user = [0]
    for w in range(NW):
        for ic in range(8):
            k.wait("sp", "pe", wq_user[0])
            k.wait("sp", "st")
            k.load(wq2[:, 0].rearrange("p (c s) -> p c s", c=8),
                   dram["wgq"].rearrange("(c p) i -> p c i", p=128)[
                       :, :, ic * 512:(ic + 1) * 512], "wqa")
            k.load(wq2[:, 1].rearrange("p (c s) -> p c s", c=8),
                   dram["wuq"].rearrange("(c p) i -> p c i", p=128)[
                       :, :, ic * 512:(ic + 1) * 512], "wqb")
            for tj in range(4):
                tt = w * 4 + tj
                bg = "b%d" % (2 * (tj % 2))
                bu = "b%d" % (2 * (tj % 2) + 1)
                sgb = sb["qkrot"][:, (tj % 2) * 512:(tj % 2 + 1) * 512]
                k.wld("pe", "wqa")
                k.wld("pe", "wqb")
                k.bw(bg)
                for hc in range(NC):
                    def fg(hc=hc, tt=tt, bg=bg):
                        return t.matmul(
                            ps[bg][:, :],
                            xqT[:, :].rearrange("p (c s) -> p c s", s=S)[
                                :, hc, tt * 128:(tt + 1) * 128],
                            wq2[:, 0].rearrange("p (c s) -> p c s", c=8)[:, hc],
                            start=(hc == 0), stop=(hc == NC - 1))
                    (k.peq if hc < NC - 1 else k.pe)(fg)
                k.bwd(bg)
                k.bw(bu)
                for hc in range(NC):
                    def fu(hc=hc, tt=tt, bu=bu):
                        return t.matmul(
                            ps[bu][:, :],
                            xqT[:, :].rearrange("p (c s) -> p c s", s=S)[
                                :, hc, tt * 128:(tt + 1) * 128],
                            wq2[:, 1].rearrange("p (c s) -> p c s", c=8)[:, hc],
                            start=(hc == 0), stop=(hc == NC - 1))
                    (k.peq if hc < NC - 1 else k.pe)(fu)
                k.bwd(bu)
                wq_user[0] = k.cnt["pe"]
                k.br("act", bg)
                k.wait("act", "dve", k.cnt["dve"] - 1)
                def fsg(bg=bg, sgb=sgb):
                    return a.activation(sgb, ps[bg][:, :], AF.Silu,
                                        scale=sb["scl"][:, 6:7])
                k.act(fsg)
                k.brd("act", bg)
                k.br("dve", bu)
                k.wait("dve", "act")
                def fh(tj=tj, ic=ic, bu=bu, sgb=sgb):
                    return v.scalar_tensor_tensor(
                        hid[:, :].rearrange("p (tj i) -> p tj i", i=I)[
                            :, tj, ic * 512:(ic + 1) * 512],
                        ps[bu][:, :], sb["scl"][:, 7:8], sgb,
                        OP.mult, OP.mult)
                k.dve(fh)
                k.brd("dve", bu)
        k.wait("dve", "pe")
        for tj in range(4):
            for hh in range(2):
                _quant_tile(nc, k, sb, ps,
                            src=hid[:, :].rearrange("p (tj i) -> p tj i", i=I)[
                                :, tj, hh * 2048:(hh + 1) * 2048],
                            n=2048, scol=3,
                            dst=hqw, dst_stride=512, dst_c0=hh * 16, dst_off=tj * 128)
        dn_user = [0, 0]
        k.wait("sp", "pe")
        for dh in range(2):
            for ic in range(NIC):
                wb = ic % 2
                if wb == 0:
                    k.wait("sp", "pe", max(dn_user[0], dn_user[1]))
                    k.load(wq2[:, 0:2, 0:512],
                           dram["wdq"][ic * 128:(ic + 2) * 128,
                                       dh * 512:(dh + 1) * 512].rearrange(
                               "(b p) s -> p b s", p=128), "wqa")
                k.wld("pe", "wqa")
                if ic == 0:
                    k.wait("pe", "dve")
                for tj in range(4):
                    bk = "b%d" % (2 + tj)
                    if ic == 0:
                        k.bw(bk)
                    def fd(ic=ic, tj=tj, wb=wb, bk=bk):
                        return t.matmul(
                            ps[bk][:, :],
                            hqw[:, :].rearrange("p (c s) -> p c s", s=512)[
                                :, ic, tj * 128:(tj + 1) * 128],
                            wq2[:, wb, 0:512],
                            start=(ic == 0), stop=(ic == NIC - 1))
                    (k.peq if tj < 3 else k.pe)(fd)
                dn_user[wb] = k.cnt["pe"]
            for tj in range(4):
                bk = "b%d" % (2 + tj)
                k.bwd(bk)
                tt = w * 4 + tj
                k.br("act", bk)
                k.wait("act", "dve")
                def fsi(tj=tj, dh=dh, bk=bk):
                    gw = sb["A2"][:, 0:4096]
                    return a.activation(
                        gw[:, tj * 1024 + dh * 512: tj * 1024 + (dh + 1) * 512],
                        ps[bk][:, :], AF.Silu, scale=sb["scl"][:, 8:9])
                k.act(fsi)
                k.brd("act", bk)
        wq_user[0] = k.cnt["pe"]
        _merge_window(nc, k, dram, sb, xsrc, xdst, w, sb["A2"][:, 0:4096])



def kernel(**kw):
    maps = _prep_inputs(kw)
    nc = build_program()
    res = run_bass_kernel_spmd(nc, maps, core_ids=list(range(8)))
    return np.stack([r["y"] for r in res.results], axis=0)


# version 12
# speedup vs baseline: 1.0326x; 1.0326x over previous
import numpy as np
import concourse.bass as bass
import concourse.mybir as mybir
from concourse.alu_op_type import AluOpType as OP
from concourse.bass_utils import run_bass_kernel_spmd

dt = mybir.dt
F32, F32R, BF16, FP8 = dt.float32, dt.float32r, dt.bfloat16, dt.float8e4
AF = mybir.ActivationFunctionType
AX = mybir.AxisListType

S, E, H, D, I = 2048, 1024, 16, 64, 4096
NT, NC, NW, NIC = S // 128, E // 128, S // 512, I // 128
ROPE_BASE, EPS = 10000.0, 1e-5


def _host_constants():
    c = {}
    pos = np.arange(S, dtype=np.float64)
    inv = 1.0 / ROPE_BASE ** (np.arange(0, D, 2, dtype=np.float64) / D)
    ang = inv[:, None] * pos[None, :]
    c["rope_cos"] = np.tile(np.repeat(np.cos(ang), 2, axis=0), (2, 1)).astype(np.float32)
    c["rope_sin"] = np.tile(np.repeat(np.sin(ang), 2, axis=0), (2, 1)).astype(np.float32)
    P = np.zeros((128, 128), np.float32)
    for b in (0, 64):
        for i in range(32):
            P[b + i, b + i + 32] = -1.0
            P[b + 32 + i, b + i] = 1.0
    c["rotPT"] = np.ascontiguousarray(P.T)
    c["ident"] = np.eye(128, dtype=np.float32)
    u = np.arange(896)
    c["causal"] = (np.arange(128)[:, None] <= (u[None, :] - 384)).astype(np.float32)
    hm = np.zeros((128, 2), np.float32)
    hm[:64, 0] = 1.0
    hm[64:, 1] = 1.0
    c["headmask"] = hm
    pm = np.zeros((2, 64), np.float32)
    for h in range(16):
        pm[h % 2, h * 4:(h + 1) * 4] = -0.125
    c["parity"] = pm
    c["ones1"] = np.ones((1, 128), np.float32)
    c["ones2"] = np.ones((2, 128), np.float32)
    return c


def _fm(b):
    return np.ascontiguousarray(np.asarray(b, np.float32).reshape(-1, 128).T)


def _prep_inputs(kw):
    c = _host_constants()
    x = np.asarray(kw["L0_attn_x"], np.float32)
    wq, wk = np.asarray(kw["L0_attn_wq"], np.float32), np.asarray(kw["L0_attn_wk"], np.float32)
    wv, wo = np.asarray(kw["L0_attn_wv"], np.float32), np.asarray(kw["L0_attn_wo"], np.float32)
    win = np.asarray(kw["L1_attn_w_in"], np.float32)
    bin_ = np.asarray(kw["L1_attn_b_in"], np.float32)
    sh = {
        "wqk0T": np.ascontiguousarray(np.concatenate([wq.T, wk.T], axis=1)),
        "wv0T": np.ascontiguousarray(wv.T),
        "wo0T": np.ascontiguousarray(wo.T),
        "bqk0": np.concatenate([_fm(kw["L0_attn_bq"]), _fm(kw["L0_attn_bk"])], axis=1),
        "bv0": np.asarray(kw["L0_attn_bv"], np.float32)[None, :],
        "bo0": np.asarray(kw["L0_attn_bo"], np.float32)[None, :],
        "W1": np.asarray(kw["L0_ffn_W1"], np.float32),
        "b1": _fm(kw["L0_ffn_b1"]),
        "W2": np.asarray(kw["L0_ffn_W2"], np.float32),
        "b2": np.asarray(kw["L0_ffn_b2"], np.float32)[None, :],
        "wqk1T": np.ascontiguousarray(win[:2 * E].T),
        "wv1T": np.ascontiguousarray(win[2 * E:].T),
        "bqk1": np.concatenate([_fm(bin_[:E]), _fm(bin_[E:2 * E])], axis=1),
        "bv1": bin_[2 * E:][None, :],
        "wo1T": np.ascontiguousarray(np.asarray(kw["L1_attn_w_out"], np.float32).T),
        "bo1": np.asarray(kw["L1_attn_b_out"], np.float32)[None, :],
        "gw": np.asarray(kw["L1_ffn_gate_w"], np.float32),
        "uw": np.asarray(kw["L1_ffn_up_w"], np.float32),
        "dw": np.asarray(kw["L1_ffn_dw" if False else "L1_ffn_down_w"], np.float32),
        "s_in": np.asarray(kw["L1_ffn_s_in"], np.float32).reshape(1, 1),
        "s_in_down": np.asarray(kw["L1_ffn_s_in_down"], np.float32).reshape(1, 1),
    }
    sh.update(c)
    return [dict(sh, x0=np.ascontiguousarray(x[core])) for core in range(8)]



class K:
    def __init__(self, nc):
        self.nc = nc
        self.q = {e: [] for e in ("pe", "act", "dve", "gp", "sp")}
        self.cnt = {n: 0 for n in ("ld", "st", "pe", "act", "dve", "gp")}
        self.sems = {}
        self.lastw = {e: {} for e in self.q}
        self.banks = {}
        self.slots = {}
        self.pool = []

    def wait(self, eng, sem, val=None):
        v = self.cnt[sem] if val is None else val
        if v <= 0 or self.lastw[eng].get(sem, 0) >= v:
            return
        self.lastw[eng][sem] = v
        h = self.sems[sem]
        self.q[eng].append(lambda e, h=h, v=v: e.wait_ge(h, v))

    def _op(self, eng, sem, mk):
        if eng in ("act", "dve", "gp"):
            self.wait(eng, sem, self.cnt[sem])
        self.cnt[sem] += 1
        h = self.sems[sem]
        self.q[eng].append(lambda e, mk=mk, h=h: mk().then_inc(h, 1))
        return self.cnt[sem]

    def pe(self, mk):
        return self._op("pe", "pe", mk)

    def peq(self, mk):
        self.q["pe"].append(lambda e, mk=mk: mk())

    def act(self, mk):
        return self._op("act", "act", mk)

    def dve(self, mk):
        return self._op("dve", "dve", mk)

    def gp(self, mk):
        return self._op("gp", "gp", mk)

    def load(self, dst, src, slot="boot"):
        if slot not in self.slots:
            assert self.pool, "semaphore pool exhausted"
            self.slots[slot] = [self.pool.pop(), 0]
        ent = self.slots[slot]
        ent[1] += 16
        h = ent[0]
        nc = self.nc
        self.q["sp"].append(
            lambda e, dst=dst, src=src, h=h: nc.sync.dma_start(dst, src).then_inc(h, 16))
        return ent[1]

    def wld(self, eng, slot, val=None):
        if slot not in self.slots:
            return
        ent = self.slots[slot]
        v = ent[1] if val is None else val
        key = "L_" + slot
        if v <= 0 or self.lastw[eng].get(key, 0) >= v:
            return
        self.lastw[eng][key] = v
        h = ent[0]
        self.q[eng].append(lambda e, h=h, v=v: e.wait_ge(h, v))

    def store(self, dst, src):
        self.cnt["st"] += 16
        h = self.sems["st"]
        nc = self.nc
        self.q["sp"].append(
            lambda e, dst=dst, src=src, h=h: nc.sync.dma_start(dst, src).then_inc(h, 16))
        return self.cnt["st"]

    pump = None

    def do_pump(self, n=1):
        if self.pump is None:
            return
        for _ in range(n):
            try:
                next(self.pump)
            except StopIteration:
                self.pump = None
                return

    def fence_sp(self):
        self.wait("sp", "pe")
        self.wait("sp", "act")
        self.wait("sp", "dve")
        self.wait("sp", "st")

    def bw(self, name):
        st = self.banks.setdefault(name, {"w": 0, "r": {}})
        for eng, v in st["r"].items():
            self.wait("pe", eng, v)
        st["r"] = {}

    def bwd(self, name):
        self.banks[name]["w"] = self.cnt["pe"]

    def br(self, eng, name):
        st = self.banks.setdefault(name, {"w": 0, "r": {}})
        self.wait(eng, "pe", st["w"])

    def brd(self, eng, name):
        self.banks[name]["r"][eng] = self.cnt[eng]



def build_program(tap=None):
    nc = bass.Bass("TRN2", target_bir_lowering=False)
    dram = {}

    def din(name, shape, dtype=F32):
        dram[name] = nc.dram_tensor(name, list(shape), dtype, kind="ExternalInput")

    def dsc(name, shape, dtype=F32):
        dram[name] = nc.dram_tensor(name, list(shape), dtype, kind="Internal")

    din("x0", (S, E))
    din("wqk0T", (E, 2 * E)); din("wv0T", (E, E)); din("wo0T", (E, E))
    din("bqk0", (128, 16)); din("bv0", (1, E)); din("bo0", (1, E))
    din("W1", (E, I)); din("b1", (128, NIC)); din("W2", (I, E)); din("b2", (1, E))
    din("wqk1T", (E, 2 * E)); din("wv1T", (E, E)); din("wo1T", (E, E))
    din("bqk1", (128, 16)); din("bv1", (1, E)); din("bo1", (1, E))
    din("gw", (I, E)); din("uw", (I, E)); din("dw", (E, I))
    din("s_in", (1, 1)); din("s_in_down", (1, 1))
    din("rope_cos", (128, S)); din("rope_sin", (128, S))
    din("rotPT", (128, 128)); din("ident", (128, 128)); din("causal", (128, 896))
    din("headmask", (128, 2)); din("parity", (2, 64))
    din("ones1", (1, 128)); din("ones2", (2, 128))
    dsc("xcur", (S, E))
    dsc("xn_fm", (E, S), BF16)
    dsc("Qs", (E, S), BF16); dsc("Ks", (E, S), BF16)
    dsc("wgq", (E, I), BF16); dsc("wuq", (E, I), BF16); dsc("wdq", (I, E), BF16)
    dram["y"] = nc.dram_tensor("y", [S, E], F32, kind="ExternalOutput")

    import contextlib
    stack = contextlib.ExitStack()
    sb = {}

    def sbuf(name, shape, dtype):
        sb[name] = stack.enter_context(nc.sbuf_tensor("sb_" + name, list(shape), dtype))

    sbuf("rotPT", (128, 128), BF16)
    sbuf("identf", (128, 128), F32)
    sbuf("identb", (128, 128), BF16)
    sbuf("causal", (128, 896), BF16)
    sbuf("headmask", (128, 2), F32)
    sbuf("parity", (2, 64), F32)
    sbuf("ones1", (1, 128), F32)
    sbuf("ones2", (2, 128), F32)
    sbuf("onesb", (1, 128), BF16)
    sbuf("onescol", (128, 1), BF16)
    sbuf("headmaskb", (128, 2), BF16)
    sbuf("bqk", (128, 16), F32)
    sbuf("b1", (128, NIC), F32)
    sbuf("brow", (1, E), F32)
    sbuf("browb", (1, E), BF16)
    sbuf("scl", (128, 16), F32)
    sbuf("stats", (128, 4 * NT), F32)
    sbuf("bnraw", (128, 12), F32)
    sbuf("qw", (128, 16 * NW), F32)
    sbuf("cmaxK", (128, 8), F32)
    sbuf("cmaxM", (128, 16), F32)
    sbuf("mrow", (2, 96), F32)
    sbuf("Bsb", (128, 64), F32)
    sbuf("zeros", (128, 512), BF16)
    sbuf("rsum", (128, 4), F32)
    sbuf("msk8", (128, 512), dt.uint8)
    sbuf("qsml", (128, 6 * 128), F32)
    sbuf("qfp8", (128, 2 * 128), FP8)
    sbuf("qkrot", (128, 2 * 512), BF16)
    sbuf("rtmp", (128, 2 * 512), F32)
    sbuf("at", (128, 2 * 512), BF16)
    sbuf("gtile", (128, E), BF16)
    sbuf("xio", (128, E), F32)
    sbuf("A2", (128, 16384), BF16)
    sbuf("xqT", (128, 16384), BF16)
    sbuf("AV", (128, NT * 16 * 65), BF16)
    sbuf("wblk", (128, 8192), F32)
    sbuf("wblk2", (128, 8192), F32)
    sbuf("wsh", (128, 8192), BF16)

    ps = {}
    for i in range(6):
        ps["b%d" % i] = stack.enter_context(nc.psum_tensor("pb%d" % i, [128, 512], F32))
    ps["t0"] = stack.enter_context(nc.psum_tensor("pt0", [128, 1024], BF16))
    ps["t1"] = stack.enter_context(nc.psum_tensor("pt1", [128, 1024], BF16))

    import contextlib as _ctx
    semstack = stack
    h_ld = semstack.enter_context(nc.semaphore("s_ld"))
    h_st = semstack.enter_context(nc.semaphore("s_st"))
    h_pe = semstack.enter_context(nc.semaphore("s_pe"))
    h_act = semstack.enter_context(nc.semaphore("s_act"))
    h_dve = semstack.enter_context(nc.semaphore("s_dve"))
    h_gp = semstack.enter_context(nc.semaphore("s_gp"))
    pool = [semstack.enter_context(nc.semaphore("s_p%d" % i)) for i in range(32)]
    with nc.Block() as block:
        k = K(nc)
        k.sems = {"ld": h_ld, "st": h_st, "pe": h_pe, "act": h_act,
                  "dve": h_dve, "gp": h_gp}
        k.pool = pool
        _build_all(nc, k, dram, sb, ps, tap)

        @block.sync
        def _(e):
            for fn in k.q["sp"]:
                fn(e)

        @block.tensor
        def _(e):
            for fn in k.q["pe"]:
                fn(e)

        @block.scalar
        def _(e):
            for fn in k.q["act"]:
                fn(e)

        @block.vector
        def _(e):
            for fn in k.q["dve"]:
                fn(e)

        @block.gpsimd
        def _(e):
            for fn in k.q["gp"]:
                fn(e)

    stack.close()
    return nc



def _build_all(nc, k, dram, sb, ps, tap):
    v, t, a = nc.vector, nc.tensor, nc.scalar

    for n, tgt in (("ident", "identf"), ("headmask", "headmask"),
                   ("parity", "parity"), ("ones1", "ones1"), ("ones2", "ones2"),
                   ("b1", "b1")):
        k.load(sb[tgt][:, :], dram[n][:, :], "boot")
    k.load(sb["rtmp"][:, 0:128], dram["rotPT"][:, :], "boot")
    k.load(sb["wblk"][:, 0:896], dram["causal"][:, :], "boot")
    k.wld("dve", "boot")
    k.wld("pe", "boot")
    k.wld("act", "boot")
    k.dve(lambda: v.tensor_copy(sb["identb"][:, :], sb["identf"][:, :]))
    k.dve(lambda: v.tensor_copy(sb["rotPT"][:, :], sb["rtmp"][:, 0:128]))
    k.dve(lambda: v.tensor_copy(sb["causal"][:, :], sb["wblk"][:, 0:896]))
    k.dve(lambda: v.memset(sb["zeros"][:, :], 0.0))
    k.dve(lambda: v.memset(sb["onesb"][:, :], 1.0))
    k.dve(lambda: v.memset(sb["onescol"][:, :], 1.0))
    k.dve(lambda: v.tensor_copy(sb["headmaskb"][:, :], sb["headmask"][:, :]))

    k.load(sb["brow"][0:1, 0:1], dram["s_in"][:, :], "brow")
    k.load(sb["brow"][0:1, 1:2], dram["s_in_down"][:, :], "brow")
    k.wld("pe", "brow")
    k.bw("b5")
    k.pe(lambda: t.matmul(ps["b5"][:, 0:2], sb["ones1"][:, :], sb["brow"][0:1, 0:2],
                          start=True, stop=True))
    k.bwd("b5")
    k.br("dve", "b5")
    k.dve(lambda: v.tensor_copy(sb["scl"][:, 0:1], ps["b5"][:, 0:1]))
    k.dve(lambda: v.tensor_copy(sb["scl"][:, 3:4], ps["b5"][:, 1:2]))
    k.brd("dve", "b5")
    for base in (0, 3):
        k.dve(lambda base=base: v.tensor_scalar(
            sb["scl"][:, base + 1:base + 2], sb["scl"][:, base:base + 1],
            1.0 / 6.0, None, OP.mult))
        k.dve(lambda base=base: v.reciprocal(
            sb["scl"][:, base + 2:base + 3], sb["scl"][:, base:base + 1]))

    _layer(nc, k, dram, sb, ps, 0, tap)
    if tap is not None and tap.endswith("0"):
        _finish(k)
        return
    if tap == "wq":
        _wquant(nc, k, dram, sb, ps)
        _finish(k)
        return
    k.pump = _wquant_gen(nc, k, dram, sb, ps)
    _layer(nc, k, dram, sb, ps, 1, tap)
    _finish(k)


def _wquant_gen(nc, k, dram, sb, ps):
    v, t, a = nc.vector, nc.tensor, nc.scalar
    NQ = 1024
    ldb = sb["wblk2"][:, 4096:5120]
    w2b = sb["wblk2"][:, :].bitcast(BF16)
    TA = sb["wblk2"][:, 6144:7168]
    rtb = sb["rtmp"][:, :].bitcast(BF16)
    NL = rtb[:, 0:1024]
    NM = rtb[:, 1024:2048]
    SG = sb["gtile"][:, 0:1024]
    QO = w2b[:, 14336:15360]
    stage = w2b[:, 15360:16384]
    specs = [("gw", "wgq", I, E, 0, 6), ("uw", "wuq", I, E, 0, 7),
             ("dw", "wdq", E, I, 3, 8)]
    st_ia = st_isrc = st_iq = st_ist = st_ifam = 0
    first = True
    for (nm, qn, NR, NCL, actbase, sci) in specs:
        nrch = NR // 128
        nq = NCL // NQ
        for rc in range(nrch):
            for qq in range(nq):
                idx = rc * nq + qq
                if first:
                    k.wait("sp", "dve")
                    k.wait("sp", "act")
                    first = False
                else:
                    k.wait("sp", "act", st_ia)
                    k.wait("sp", "dve", max(st_isrc, st_ifam))
                k.load(ldb, dram[nm][rc * 128:(rc + 1) * 128, qq * NQ:(qq + 1) * NQ],
                       "wql")
                k.wld("dve", "wql")
                def fam(idx=idx):
                    return v.tensor_reduce(sb["stats"][:, idx:idx + 1], ldb,
                                           AX.X, OP.max, apply_absolute_value=True)
                st_ifam = k.dve(fam)
                yield
        tot = nrch * nq
        ird = k.dve(lambda tot=tot: v.tensor_reduce(sb["scl"][:, 15:16],
                                              sb["stats"][:, 0:tot], AX.X, OP.max))
        k.wait("pe", "dve", ird)
        k.bw("b5")
        k.pe(lambda: t.matmul(ps["b5"][0:1, 0:128], sb["scl"][:, 15:16],
                              sb["identf"][:, :], start=True, stop=True,
                              is_transpose=True))
        k.bwd("b5")
        k.br("dve", "b5")
        imr = k.dve(lambda: v.tensor_reduce(sb["mrow"][0:1, 64:65], ps["b5"][0:1, 0:128],
                                      AX.X, OP.max))
        k.brd("dve", "b5")
        k.wait("pe", "dve", imr)
        k.bw("b5")
        k.pe(lambda: t.matmul(ps["b5"][:, 1:2], sb["ones1"][:, :],
                              sb["mrow"][0:1, 64:65], start=True, stop=True))
        k.bwd("b5")
        k.br("dve", "b5")
        k.dve(lambda: v.tensor_copy(sb["scl"][:, 14:15], ps["b5"][:, 1:2]))
        k.brd("dve", "b5")
        k.dve(lambda: v.reciprocal(sb["scl"][:, 9:10], sb["scl"][:, 14:15]))
        k.dve(lambda: v.tensor_scalar(sb["scl"][:, 9:10], sb["scl"][:, 9:10],
                                      448.0 * 6.0, None, OP.mult))
        k.dve(lambda: v.tensor_scalar(sb["scl"][:, 10:11], sb["scl"][:, 9:10],
                                      1.0 / 6.0, None, OP.mult))
        k.dve(lambda: v.reciprocal(sb["scl"][:, 11:12], sb["scl"][:, 9:10]))
        k.dve(lambda sci=sci, actbase=actbase: v.tensor_tensor(
            sb["scl"][:, sci:sci + 1], sb["scl"][:, 9:10],
            sb["scl"][:, actbase:actbase + 1], OP.mult))
        k.dve(lambda sci=sci: v.reciprocal(sb["scl"][:, sci:sci + 1],
                                           sb["scl"][:, sci:sci + 1]))
        yield
        for rc in range(nrch):
            for qq in range(nq):
                k.wait("sp", "act", st_ia)
                k.wait("sp", "dve", max(st_isrc, st_ifam))
                k.load(ldb, dram[nm][rc * 128:(rc + 1) * 128, qq * NQ:(qq + 1) * NQ],
                       "wql")
                k.wld("dve", "wql")
                st_ia, st_isrc, st_iq = _quant_tile(
                    nc, k, sb, ps, src=ldb, n=NQ, scol=9,
                    dst=stage, dst_stride=128, dst_c0=0, dst_off=0,
                    src_slot="wql", bufs=(TA, NL, NM, SG, QO),
                    prec={"sg_free": st_iq}, do_transpose=False)
                yield
                ifev = _qt_transpose(nc, k, sb, ps, NQ, stage, 128, 0, 0, QO,
                                     st_iq, st_ist)
                k.wait("sp", "dve", ifev)
                st_ist = k.store(dram[qn].rearrange("(c p) o -> p c o", p=128)[
                    :, qq * 8:(qq + 1) * 8, rc * 128:(rc + 1) * 128],
                    stage.rearrange("p (c o) -> p c o", o=128))
                yield


def _finish(k):
    k.wait("sp", "st", k.cnt["st"])
    for s in list(k.slots):
        k.wld("sp", s)


def _layer(nc, k, dram, sb, ps, l, tap):
    v = nc.vector
    xsrc = dram["x0"] if l == 0 else dram["xcur"]
    xdst = dram["xcur"] if l == 0 else dram["y"]
    if tap == "l0" and l == 0:
        xdst = dram["y"]
    y3 = dram["y"].rearrange("(a p) b -> p a b", p=128)

    k.wait("sp", "act")
    k.wait("sp", "dve")
    k.load(sb["bqk"][:, :], dram["bqk%d" % l][:, :], "bqk")

    def tap_store(src_f32_flat, n):
        k.wait("sp", "dve")
        k.store(y3[:, 0:n // E, :],
                src_f32_flat.rearrange("p (a b) -> p a b", b=E))

    _ph_ln(nc, k, dram, sb, ps, l, xsrc)
    if tap == "ln%d" % l:
        tap_store(sb["A2"][:, :].bitcast(F32), 8192)
        return
    _ph_qk(nc, k, dram, sb, ps, l, tap)
    if tap is not None and (tap.startswith("qk") or tap.startswith("qs")) and tap[2:].isdigit():
        return
    if tap == "qka%d" % l:
        return
    _ph_v(nc, k, dram, sb, ps, l)
    if tap == "vv%d" % l:
        return
    _ph_mbias(nc, k, dram, sb, ps, l)
    if tap == "qkv%d" % l:
        tap_store(sb["AV"][:, 0:16384].bitcast(F32), 8192)
        k.store(y3[:, 15, 0:64], sb["Bsb"][:, :])
        return
    if tap == "q%d" % l:
        k.wait("sp", "dve")
        k.wait("sp", "act")
        k.store(dram["y"][0:1024, 0:1024], dram["Qs"][:, :].bitcast(F32))
        return
    if tap == "k%d" % l:
        k.wait("sp", "dve")
        k.wait("sp", "act")
        k.store(dram["y"][0:1024, 0:1024], dram["Ks"][:, :].bitcast(F32))
        return
    _ph_att(nc, k, dram, sb, ps, l)
    if tap == "att%d" % l:
        tap_store(sb["A2"][:, :].bitcast(F32), 8192)
        return
    _ph_wo(nc, k, dram, sb, ps, l)
    if tap == "wo%d" % l:
        tap_store(sb["AV"][:, 0:16384].bitcast(F32), 8192)
        return
    if l == 0:
        _ph_ffn0(nc, k, dram, sb, ps, xsrc, xdst)
    else:
        _ph_ffn1(nc, k, dram, sb, ps, xsrc, xdst)



def _ph_ln(nc, k, dram, sb, ps, l, xsrc):
    v, t, a = nc.vector, nc.tensor, nc.scalar
    xsv = xsrc.rearrange("(t p) e -> p t e", p=128)
    xbuf = sb["wblk"][:, 0:2048].rearrange("p (b e) -> p b e", b=2)
    nbuf = sb["wblk"][:, 2048:4096].rearrange("p (b e) -> p b e", b=2)
    xnT = sb["A2"]
    st2 = sb["stats"][:, 0:2 * NT].rearrange("p (t two) -> p t two", two=2)
    k.wait("sp", "st")
    k.wait("sp", "pe")
    k.wait("sp", "dve")
    k.wait("sp", "act")
    tld = [0, 0]
    varc = sb["stats"][:, 2 * NT:3 * NT]
    tmp16 = sb["stats"][:, 3 * NT:4 * NT]
    sink = sb["wblk"][:, 4096:5120]
    for i in range(NT):
        b = i % 2
        k.wait("sp", "dve")
        k.wait("sp", "act")
        k.load(xbuf[:, b], xsv[:, i], "xb%d" % b)
        k.wld("dve", "xb%d" % b)
        k.wld("act", "xb%d" % b)
        k.wait("act", "dve")
        def fsq(b=b):
            return a.activation(sink, xbuf[:, b], AF.Square)
        k.act(fsq)
        k.wait("dve", "act")
        def f(i=i, b=b):
            v.tensor_reduce(sb["stats"][:, 2 * i:2 * i + 1], xbuf[:, b], AX.X, OP.add)
            return v.tensor_reduce(varc[:, i:i + 1], sink, AX.X, OP.add)
        k.dve(f)
    k.dve(lambda: v.tensor_scalar(st2[:, :, 0], st2[:, :, 0], 1.0 / E, None, OP.mult))
    k.dve(lambda: v.tensor_scalar(varc, varc, 1.0 / E, None, OP.mult))
    k.dve(lambda: v.tensor_tensor(tmp16, st2[:, :, 0], st2[:, :, 0], OP.mult))
    k.dve(lambda: v.tensor_tensor(varc, varc, tmp16, OP.subtract))
    k.dve(lambda: v.tensor_scalar(varc, varc, EPS, None, OP.add))
    k.wait("act", "dve")
    k.act(lambda: a.activation(varc, varc, AF.Sqrt))
    k.wait("dve", "act")
    k.dve(lambda: v.reciprocal(varc, varc))
    for w in range(NW):
        for j in range(4):
            i = 4 * w + j
            b = i % 2
            k.wait("sp", "dve")
            k.wait("sp", "act")
            k.load(xbuf[:, b], xsv[:, i], "xb%d" % b)
            k.wld("dve", "xb%d" % b)
            k.wld("act", "xb%d" % b)
            k.wait("dve", "pe")
            k.wait("dve", "act")
            xt = nbuf[:, b]
            def fap(i=i, b=b, xt=xt):
                return v.tensor_scalar(xt, xbuf[:, b],
                                       sb["stats"][:, 2 * i:2 * i + 1],
                                       sb["stats"][:, 2 * NT + i:2 * NT + i + 1],
                                       OP.subtract, OP.mult)
            k.dve(fap)
            if l == 1:
                _quant_tile(nc, k, sb, ps, src=xt, n=E, scol=0,
                            dst=sb["xqT"], dst_stride=S, dst_c0=0, dst_off=i * 128)
            k.wait("pe", "dve")
            for half in range(2):
                bk = "b%d" % (4 + half)
                k.bw(bk)
                for c4 in range(4):
                    c = half * 4 + c4
                    def ftr(c=c, c4=c4, bk=bk, xt=xt):
                        return t.transpose(ps[bk][:, c4 * 128:(c4 + 1) * 128],
                                           xt[:, c * 128:(c + 1) * 128],
                                           sb["identf"][:, :])
                    (k.peq if c4 < 3 else k.pe)(ftr)
                k.bwd(bk)
                k.br("dve", bk)
                def fev(i=i, half=half, bk=bk):
                    dst = xnT[:, :].rearrange("p (c s) -> p c s", s=S)[
                        :, half * 4:(half + 1) * 4, i * 128:(i + 1) * 128]
                    return v.tensor_copy(dst, ps[bk][:, :].rearrange(
                        "p (c q) -> p c q", q=128))
                k.dve(fev)
                k.brd("dve", bk)
        k.wait("sp", "dve")
        k.store(dram["xn_fm"].rearrange("(c p) s -> p c s", p=128)[
            :, :, w * 512:(w + 1) * 512],
            xnT[:, :].rearrange("p (c s) -> p c s", s=S)[:, :, w * 512:(w + 1) * 512])


def _quant_tile(nc, k, sb, ps, src, n, scol, dst, dst_stride, dst_c0, dst_off,
                src_slot=None, bufs=None, prec=None, do_transpose=True):
    v, t, a = nc.vector, nc.tensor, nc.scalar
    nb = n // 16
    w2f = sb["wblk2"]
    w2b = sb["wblk2"][:, :].bitcast(BF16)
    if bufs is None:
        TA = w2f[:, 0:n]
        NL = w2b[:, 4096:4096 + n]
        NM = w2b[:, 6144:6144 + n]
        SG = w2b[:, 8192:8192 + n]
        QO = w2b[:, 10240:10240 + n]
    else:
        TA, NL, NM, SG, QO = bufs
    sm = sb["qsml"]
    amax, sc, c1f, sc8, r2, msk = (sm[:, i * 128:i * 128 + nb] for i in range(6))
    f8a = sb["qfp8"][:, 0:nb]
    f8b = sb["qfp8"][:, 128:128 + nb]
    gs6 = sb["scl"][:, scol + 1:scol + 2]
    ginv = sb["scl"][:, scol + 2:scol + 3]

    if prec is None:
        k.wait("act", "dve")
    else:
        k.wait("act", "dve", prec.get("sg_free", 0))
    if src_slot is not None:
        k.wld("act", src_slot)
    ia = k.act(lambda: a.activation(SG, src, AF.Sign))
    k.dve(lambda: v.tensor_reduce(amax, src.rearrange("p (b s) -> p b s", s=16),
                                  AX.X, OP.max, apply_absolute_value=True))
    k.dve(lambda: v.tensor_scalar(sc, amax, gs6, 448.0, OP.mult, OP.min))
    k.dve(lambda: v.tensor_scalar(f8a, sc, 240.0, None, OP.min))
    k.dve(lambda: v.tensor_scalar(c1f, f8a, 1.0, None, OP.mult))
    k.dve(lambda: v.tensor_scalar(f8b, sc, 0.5, None, OP.mult))
    k.dve(lambda: v.tensor_scalar(sc8, f8b, 2.0, None, OP.mult))
    k.dve(lambda: v.tensor_scalar(msk, sc, 240.0, None, OP.is_gt))
    k.dve(lambda: v.tensor_tensor(sc8, sc8, c1f, OP.subtract))
    k.dve(lambda: v.tensor_tensor(sc8, sc8, msk, OP.mult))
    k.dve(lambda: v.tensor_tensor(sc8, sc8, c1f, OP.add))
    k.dve(lambda: v.tensor_scalar(r2, sc8, ginv, None, OP.mult))
    k.dve(lambda: v.tensor_scalar(msk, r2, 0.0, None, OP.is_equal))
    k.dve(lambda: v.tensor_tensor(r2, r2, msk, OP.add))
    k.dve(lambda: v.reciprocal(r2, r2))
    r2b = r2.unsqueeze(2).broadcast_to((128, nb, 16))
    sc8b = sc8.unsqueeze(2).broadcast_to((128, nb, 16))
    isrc = k.dve(lambda: v.tensor_tensor(TA.rearrange("p (b s) -> p b s", s=16),
                                  src.rearrange("p (b s) -> p b s", s=16), r2b, OP.mult))
    k.dve(lambda: v.tensor_tensor(TA, TA, TA, OP.mult))
    k.dve(lambda: v.tensor_scalar(NL, TA, 0.0625, None, OP.is_ge))
    for thr in (0.5625, 1.5625, 3.0625):
        k.dve(lambda thr=thr: v.scalar_tensor_tensor(NL, TA, thr, NL, OP.is_ge, OP.add))
    k.dve(lambda: v.tensor_scalar(NM, TA, 6.25, None, OP.is_ge))
    k.dve(lambda: v.scalar_tensor_tensor(NM, TA, 12.25, NM, OP.is_ge, OP.add))
    k.dve(lambda: v.scalar_tensor_tensor(NM, NL, 0.5, NM, OP.mult, OP.add))
    k.dve(lambda: v.tensor_scalar(NL, TA, 25.0, None, OP.is_ge))
    k.dve(lambda: v.scalar_tensor_tensor(NM, NL, 2.0, NM, OP.mult, OP.add))
    if prec is None:
        k.wait("dve", "act")
    else:
        k.wait("dve", "act", ia)
    k.dve(lambda: v.tensor_tensor(SG.rearrange("p (b s) -> p b s", s=16),
                                  SG.rearrange("p (b s) -> p b s", s=16), sc8b, OP.mult))
    iq = k.dve(lambda: v.tensor_tensor(QO, NM, SG, OP.mult))
    if do_transpose:
        _qt_transpose(nc, k, sb, ps, n, dst, dst_stride, dst_c0, dst_off, QO,
                      None, 0)
    return ia, isrc, iq


def _qt_transpose(nc, k, sb, ps, n, dst, dst_stride, dst_c0, dst_off, QO,
                  iq, stage_free):
    v, t = nc.vector, nc.tensor
    if iq is None:
        k.wait("pe", "dve")
    else:
        k.wait("pe", "dve", iq)
    nch = n // 128
    ifev = 0
    for grp in range((nch + 7) // 8):
        bk = "t%d" % (grp % 2)
        ng = min(8, nch - grp * 8)
        k.bw(bk)
        for j in range(ng):
            c = grp * 8 + j
            def ftr(c=c, j=j, bk=bk, QO=QO):
                return t.transpose(ps[bk][:, j * 128:(j + 1) * 128],
                                   QO[:, c * 128:(c + 1) * 128], sb["identb"][:, :])
            (k.peq if j < ng - 1 else k.pe)(ftr)
        k.bwd(bk)
        k.br("dve", bk)
        if stage_free:
            k.wait("dve", "st", stage_free)
        def fev(grp=grp, bk=bk, ng=ng):
            dstv = dst[:, :].rearrange("p (c s) -> p c s", s=dst_stride)[
                :, dst_c0 + grp * 8: dst_c0 + grp * 8 + ng, dst_off:dst_off + 128]
            return v.tensor_copy(dstv, ps[bk][:, :].rearrange(
                "p (c q) -> p c q", q=128)[:, 0:ng, :])
        ifev = k.dve(fev)
        k.brd("dve", bk)
    return ifev



def _ph_qk(nc, k, dram, sb, ps, l, tap=None):
    lim = int(tap[2:]) if (tap and tap.startswith("qk") and tap[2:].isdigit()) else 10 ** 9
    sub = int(tap[2:]) if (tap and tap.startswith("qs")) else 10 ** 9
    if sub < 10 ** 9:
        lim = 1
    it = [0]
    k.fence_sp()
    v, t, a = nc.vector, nc.tensor, nc.scalar
    xnT = sb["A2"]
    wqk = dram["wqk%dT" % l]
    wst = sb["wblk"][:, :].rearrange("p (b c s) -> p b c s", b=2, c=8)
    wcast = sb["wblk2"][:, :].bitcast(BF16).rearrange(
        "p (x) -> p x")[:, 0:8192].rearrange("p (b c s) -> p b c s", b=2, c=8)
    if l == 0:
        k.load(sb["wblk2"][:, 4096:6144], dram["rope_cos"][:, :].rearrange(
            "p (a s) -> p a s", a=1).squeeze(1)) if False else None
        k.wait("sp", "dve")
        k.load(sb["wblk2"][:, 4096:6144], dram["rope_cos"][:, :], "cossin")
        k.load(sb["wblk2"][:, 6144:8192], dram["rope_sin"][:, :], "cossin")
        k.wld("dve", "cossin")
    cosT = sb["wblk2"][:, 4096:6144]
    sinT = sb["wblk2"][:, 6144:8192]
    k.wld("act", "bqk")
    k.wld("dve", "bqk")
    wuser = [0, 0]
    cuser = [0, 0]
    for db in range(4):
        buf = db % 2
        k.wait("sp", "act", cuser[buf])
        k.load(wst[:, buf], wqk.rearrange("(c p) d -> p c d", p=128)[
            :, :, db * 512:(db + 1) * 512], "wst%d" % buf)
        k.wait("act", "pe", wuser[buf])
        k.wld("act", "wst%d" % buf)
        def fcast(buf=buf):
            return a.activation(wcast[:, buf], wst[:, buf], AF.Identity)
        k.act(fcast)
        cuser[buf] = k.cnt["act"]
        tcast = k.cnt["act"]
        for dc in range(4):
            g = db * 4 + dc
            isq = g < 8
            for w in range(NW):
                if it[0] >= lim:
                    return
                it[0] += 1
                bk = "b%d" % (w % 2)
                k.wait("pe", "act", tcast)
                k.bw(bk)
                for fc in range(NC):
                    def fmm(fc=fc, dc=dc, w=w, buf=buf, bk=bk):
                        lhsT = wcast[:, buf, fc, dc * 128:(dc + 1) * 128]
                        rhs = xnT[:, :].rearrange("p (c s) -> p c s", s=S)[
                            :, fc, w * 512:(w + 1) * 512]
                        return t.matmul(ps[bk][:, :], lhsT, rhs,
                                        start=(fc == 0), stop=(fc == NC - 1))
                    (k.peq if fc < NC - 1 else k.pe)(fmm)
                k.bwd(bk)
                wuser[buf] = k.cnt["pe"]
                if sub <= 1:
                    return
                qb = w % 2
                qkout = sb["qkrot"][:, qb * 512:(qb + 1) * 512]
                k.wait("act", "st")
                k.wait("act", "dve")
                k.br("act", bk)
                def fqb(g=g, bk=bk, qkout=qkout):
                    return a.activation(qkout, ps[bk][:, :], AF.Identity,
                                        bias=sb["bqk"][:, g:g + 1])
                k.act(fqb)
                k.brd("act", bk)
                if sub <= 2:
                    return
                if l == 0:
                    k.wait("pe", "act")
                    k.bw("b2")
                    def frot(qkout=qkout):
                        return t.matmul(ps["b2"][:, :], sb["rotPT"][:, :], qkout,
                                        start=True, stop=True)
                    k.pe(frot)
                    k.bwd("b2")
                    if sub <= 3:
                        return
                    k.br("dve", bk)
                    k.wait("dve", "act")
                    def fq0(g=g, w=w, bk=bk):
                        return v.scalar_tensor_tensor(
                            sb["rtmp"][:, 0:512], ps[bk][:, :], sb["bqk"][:, g:g + 1],
                            cosT[:, w * 512 - 4096 + 4096:w * 512 + 512] if False else
                            sb["wblk2"][:, 4096 + w * 512:4096 + (w + 1) * 512],
                            OP.add, OP.mult)
                    k.dve(fq0)
                    k.brd("dve", bk)
                    if sub <= 41:
                        return
                    k.br("dve", "b2")
                    def ft2(w=w):
                        return v.tensor_tensor(
                            sb["rtmp"][:, 512:1024], ps["b2"][:, :],
                            sb["wblk2"][:, 6144 + w * 512:6144 + (w + 1) * 512], OP.mult)
                    k.dve(ft2)
                    k.brd("dve", "b2")
                    if sub <= 42:
                        return
                    k.wait("dve", "st")
                    def fsum(qkout=qkout):
                        return v.tensor_tensor(qkout, sb["rtmp"][:, 0:512],
                                               sb["rtmp"][:, 512:1024], OP.add)
                    k.dve(fsum)
                    src_eng = "dve"
                    if sub <= 4:
                        return
                else:
                    src_eng = "act"
                if src_eng == "act":
                    k.wait("dve", "act")
                k.wait("dve", "pe")
                def fq2(qkout=qkout):
                    return v.tensor_tensor(sb["at"][:, 0:512], qkout, qkout, OP.mult)
                k.dve(fq2)
                k.wait("pe", "dve")
                k.bw("b3")
                def fnm():
                    return t.matmul(ps["b3"][0:2, 0:512], sb["headmaskb"][:, :],
                                    sb["at"][:, 0:512], start=True, stop=True)
                k.pe(fnm)
                k.bwd("b3")
                k.br("dve", "b3")
                col = g * NW + w
                def fnr(col=col):
                    return v.tensor_reduce(sb["qw"][0:2, col:col + 1],
                                           ps["b3"][0:2, 0:512], AX.X, OP.max)
                k.dve(fnr)
                k.brd("dve", "b3")
                if sub <= 5:
                    return
                k.wait("sp", src_eng)
                dstd = dram["Qs"] if isq else dram["Ks"]
                row0 = (g if isq else g - 8) * 128
                k.store(dstd[row0:row0 + 128, w * 512:(w + 1) * 512], qkout)
                if l == 1:
                    k.do_pump(1)



def _ph_v(nc, k, dram, sb, ps, l):
    k.fence_sp()
    v, t, a = nc.vector, nc.tensor, nc.scalar
    xnT = sb["A2"]
    Vsb = sb["AV"]
    k.wait("sp", "pe")
    k.wait("sp", "dve")
    k.load(sb["brow"][:, :], dram["bv%d" % l][:, :], "brow")
    k.load(sb["wblk"][:, :].rearrange("p (c d) -> p c d", c=8),
           dram["wv%dT" % l].rearrange("(c p) d -> p c d", p=128), "wvf")
    k.wld("act", "wvf")
    k.wld("dve", "brow")
    k.wld("pe", "brow")
    k.wait("act", "pe")
    k.act(lambda: a.activation(sb["wblk2"][:, :].bitcast(BF16)[:, 0:8192],
                               sb["wblk"][:, :], AF.Identity))
    tcastv = k.cnt["act"]
    wvb = sb["wblk2"][:, :].bitcast(BF16)[:, 0:8192].rearrange("p (c d) -> p c d", c=8)
    k.dve(lambda: v.memset(
        Vsb[:, :].rearrange("p (tt h c) -> p tt h c", tt=NT, h=16)[:, :, :, 64], 1.0))
    k.wait("pe", "act", tcastv)
    k.wait("pe", "dve")
    for tt in range(NT):
        for dh in range(2):
            bk = "b%d" % dh
            k.bw(bk)
            for fc in range(NC):
                def fmm(fc=fc, tt=tt, dh=dh, bk=bk):
                    lhsT = xnT[:, :].rearrange("p (c s) -> p c s", s=S)[
                        :, fc, tt * 128:(tt + 1) * 128]
                    rhs = wvb[:, fc, dh * 512:(dh + 1) * 512]
                    return t.matmul(ps[bk][:, :], lhsT, rhs, start=(fc == 0), stop=False)
                k.peq(fmm)
            def fbias(dh=dh, bk=bk):
                return t.matmul(ps[bk][:, :], sb["ones1"][:, :],
                                sb["brow"][0:1, dh * 512:(dh + 1) * 512],
                                start=False, stop=True)
            k.pe(fbias)
            k.bwd(bk)
            k.br("dve", bk)
            def fev(tt=tt, dh=dh, bk=bk):
                dst = Vsb[:, :].rearrange("p (tt h c) -> p tt h c", tt=NT, h=16)[
                    :, tt, dh * 8:(dh + 1) * 8, 0:64]
                return v.tensor_copy(dst, ps[bk][:, :].rearrange("p (h d) -> p h d", d=64))
            k.dve(fev)
            k.brd("dve", bk)
        if l == 1:
            k.do_pump(1)


def _ph_mbias(nc, k, dram, sb, ps, l):
    v, t, a = nc.vector, nc.tensor, nc.scalar
    def fkm():
        return v.tensor_reduce(
            sb["cmaxK"][0:2, 0:8],
            sb["qw"][0:2, 32:64].rearrange("p (g w) -> p g w", w=NW), AX.X, OP.max)
    k.dve(fkm)
    def fpr():
        knB = sb["cmaxK"][0:2, 0:8].unsqueeze(2).broadcast_to((2, 8, NW))
        return v.tensor_tensor(
            sb["mrow"][0:2, 0:32].rearrange("p (g w) -> p g w", w=NW),
            sb["qw"][0:2, 0:32].rearrange("p (g w) -> p g w", w=NW), knB, OP.mult)
    k.dve(fpr)
    k.wait("act", "dve")
    k.act(lambda: a.activation(sb["mrow"][0:2, 0:32], sb["mrow"][0:2, 0:32], AF.Sqrt))
    k.wait("dve", "act")
    def fb():
        src = sb["mrow"][0:2, 0:32].rearrange("p (g w) -> p g w", w=4)
        src = src.unsqueeze(2).broadcast_to((2, 8, 2, 4))
        return v.tensor_tensor(
            sb["mrow"][0:2, 32:96].rearrange("p (g r w) -> p g r w", g=8, r=2), src,
            sb["parity"][:, :].rearrange("p (g r w) -> p g r w", g=8, r=2), OP.mult)
    k.dve(fb)
    k.wait("pe", "dve")
    k.bw("b3")
    k.pe(lambda: t.matmul(ps["b3"][:, 0:64], sb["ones2"][:, :], sb["mrow"][0:2, 32:96],
                          start=True, stop=True))
    k.bwd("b3")
    k.br("dve", "b3")
    k.dve(lambda: v.tensor_copy(sb["Bsb"][:, :], ps["b3"][:, 0:64]))
    k.brd("dve", "b3")



def _ph_att(nc, k, dram, sb, ps, l):
    k.fence_sp()
    v, t, a = nc.vector, nc.tensor, nc.scalar
    C_ = 11.0
    Vsb = sb["AV"]
    otm = sb["A2"]
    wb16 = sb["wblk"][:, :].bitcast(BF16)
    khv = wb16[:, 0:4160].rearrange("p (b s) -> p b s", b=2)
    qhv = wb16[:, 4160:5200].rearrange("p (b s) -> p b s", b=2)
    mrow_p = sb["wblk"][:, 2700:2704]
    k.dve(lambda: v.tensor_scalar(sb["Bsb"][:, :], sb["Bsb"][:, :], 1.0, None, OP.mult)) if False else None
    k.dve(lambda: v.tensor_scalar(sb["mrow"][0:2, 0:64] if False else sb["Bsb"][:, :],
                                  sb["Bsb"][:, :], 1.0, None, OP.bypass)) if False else None
    k.dve(lambda: v.tensor_scalar(sb["cmaxM"][:, 0:16], sb["cmaxM"][:, 0:16], 1.0,
                                  None, OP.mult)) if False else None
    Bc = sb["qw"][:, 0:64]
    k.dve(lambda: v.tensor_scalar(Bc, sb["Bsb"][:, :], 1.0 / C_, None, OP.mult))
    k.dve(lambda: v.memset(khv[64:65, 0:2, 0:2048].squeeze(0) if False else
                           sb["wblk"][:, :].bitcast(BF16)[64:65, 0:4160], 1.0))
    k.wait("act", "dve")
    k.wait("pe", "dve")
    kh_user = [0, 0]
    qh_user = [0, 0]
    at_user = [0, 0]
    for h in range(H):
        kb = h % 2
        row0 = (h // 2) * 128 + (h % 2) * 64
        k.wait("sp", "pe", kh_user[kb])
        k.wait("sp", "st")
        k.load(khv[0:64, kb, 0:2048], dram["Ks"][row0:row0 + 64, :], "kh%d" % kb)
        for w in range(NW):
            qb = w % 2
            kcmax = NT if l == 0 else 4 * (w + 1)
            k.wait("sp", "pe", qh_user[qb])
            k.load(qhv[0:64, qb, 0:512], dram["Qs"][row0:row0 + 64, w * 512:(w + 1) * 512],
                   "qh%d" % qb)
            k.bw("b3")
            def r1_s(kc):
                sbk = "b%d" % (kc % 2)
                k.wld("pe", "kh%d" % kb)
                k.wld("pe", "qh%d" % qb)
                k.bw(sbk)
                def fs(kc=kc, kb=kb, qb=qb, sbk=sbk):
                    return t.matmul(ps[sbk][:, :], khv[0:64, kb, kc * 128:(kc + 1) * 128],
                                    qhv[0:64, qb, 0:512], start=True, stop=True)
                k.pe(fs)
                k.bwd(sbk)
                kh_user[kb] = qh_user[qb] = k.cnt["pe"]
            def r1_t(kc):
                sbk = "b%d" % (kc % 2)
                atb = sb["at"][:, (kc % 2) * 512:(kc % 2 + 1) * 512]
                k.wait("act", "pe", at_user[kc % 2])
                k.br("act", sbk)
                def fe(h=h, w=w, sbk=sbk, atb=atb):
                    return a.activation(atb, ps[sbk][:, :], AF.Exp,
                                        bias=Bc[:, h * NW + w:h * NW + w + 1],
                                        scale=0.125 / C_)
                k.act(fe)
                k.brd("act", sbk)
                if l == 1 and kc >= 4 * w:
                    off = 384 + 512 * w - 128 * kc
                    k.wait("dve", "act")
                    def fm(atb=atb, off=off):
                        return v.tensor_tensor(atb, atb, sb["causal"][:, off:off + 512],
                                               OP.mult)
                    k.dve(fm)
                    k.wait("pe", "dve")
                else:
                    k.wait("pe", "act")
                def fsum(kc=kc, atb=atb, kcm=kcmax):
                    return t.matmul(ps["b3"][64:65, 0:512], sb["onescol"][:, :], atb,
                                    start=(kc == 0), stop=(kc == kcm - 1),
                                    tile_position=(0, 64))
                k.pe(fsum)
                at_user[kc % 2] = k.cnt["pe"]
            r1_s(0)
            for kc in range(1, kcmax):
                r1_s(kc)
                r1_t(kc - 1)
            r1_t(kcmax - 1)
            k.bwd("b3")
            k.br("dve", "b3")
            mq = sb["wblk2"][64:65, 0:512]
            def fev():
                return v.tensor_scalar(mq, ps["b3"][64:65, 0:512], 1e-30, None, OP.add)
            k.dve(fev)
            k.brd("dve", "b3")
            k.wait("act", "dve")
            k.act(lambda: a.activation(mq, mq, AF.Ln))
            k.wait("dve", "act")
            def fmq(h=h, w=w):
                return v.tensor_scalar(mq, mq, C_, sb["Bsb"][64:65, h * NW + w:h * NW + w + 1],
                                       OP.mult, OP.subtract)
            k.dve(fmq)
            k.wld("dve", "qh%d" % qb)
            def faug(qb=qb):
                return v.tensor_scalar(qhv[64:65, qb, 0:512], mq, -8.0, None, OP.mult)
            k.dve(faug)
            k.wait("pe", "dve")
            def r2_s(kc):
                sbk = "b%d" % (kc % 2)
                k.bw(sbk)
                def fs2(kc=kc, kb=kb, qb=qb, sbk=sbk):
                    return t.matmul(ps[sbk][:, :], khv[0:65, kb, kc * 128:(kc + 1) * 128],
                                    qhv[0:65, qb, 0:512], start=True, stop=True)
                k.pe(fs2)
                k.bwd(sbk)
                kh_user[kb] = qh_user[qb] = k.cnt["pe"]
            def r2_av(kc):
                sbk = "b%d" % (kc % 2)
                atb = sb["at"][:, (kc % 2) * 512:(kc % 2 + 1) * 512]
                k.wait("act", "pe", at_user[kc % 2])
                k.br("act", sbk)
                def fe2(sbk=sbk, atb=atb):
                    return a.activation(atb, ps[sbk][:, :], AF.Exp, scale=0.125)
                k.act(fe2)
                k.brd("act", sbk)
                if l == 1 and kc >= 4 * w:
                    off = 384 + 512 * w - 128 * kc
                    k.wait("dve", "act")
                    def fm2(atb=atb, off=off):
                        return v.tensor_tensor(atb, atb, sb["causal"][:, off:off + 512],
                                               OP.mult)
                    k.dve(fm2)
                    k.wait("pe", "dve")
                else:
                    k.wait("pe", "act")
                if kc == 0:
                    k.bw("b2")
                for sub in range(4):
                    def fav(h=h, kc=kc, sub=sub, atb=atb,
                            st=(kc == 0 and sub == 0),
                            sp_=(kc == kcmax - 1 and sub == 3)):
                        return t.matmul(
                            ps["b2"][:, sub * 65:sub * 65 + 65],
                            atb[:, sub * 128:(sub + 1) * 128],
                            Vsb[:, :].rearrange("p (tt c) -> p tt c", c=16 * 65)[
                                :, kc, h * 65:h * 65 + 65],
                            start=st, stop=sp_)
                    (k.peq if sub < 3 else k.pe)(fav)
                at_user[kc % 2] = k.cnt["pe"]
            r2_s(0)
            for kc in range(1, kcmax):
                r2_s(kc)
                r2_av(kc - 1)
            r2_av(kcmax - 1)
            k.bwd("b2")
            if l == 1:
                k.do_pump(2)
            k.br("dve", "b2")
            pv = ps["b2"][:, 0:260].rearrange("p (sub x) -> p sub x", x=65)
            k.dve(lambda: v.reciprocal(sb["rsum"][:, :], pv[:, 0:4, 64]))
            for sub in range(4):
                def fn(h=h, w=w, sub=sub):
                    tt = w * 4 + sub
                    return v.tensor_scalar(
                        otm[:, tt * E + h * 64: tt * E + (h + 1) * 64],
                        pv[:, sub, 0:64], sb["rsum"][:, sub:sub + 1], None, OP.mult)
                k.dve(fn)
            k.brd("dve", "b2")


def _ph_wo(nc, k, dram, sb, ps, l):
    k.fence_sp()
    v, t, a = nc.vector, nc.tensor, nc.scalar
    otm = sb["A2"]
    ofm = sb["wblk"][:, :].bitcast(BF16)
    stm = sb["AV"][:, 0:16384]
    wob = sb["wsh"]
    k.wait("sp", "pe")
    k.wait("sp", "dve")
    k.load(sb["brow"][:, :], dram["bo%d" % l][:, :], "brow")
    k.wld("dve", "brow")
    k.dve(lambda: v.tensor_copy(sb["browb"][:, :], sb["brow"][0:1, :]))
    for hf in range(2):
        k.wait("sp", "act")
        k.load(sb["wblk2"][:, 0:4096].rearrange("p (c d) -> p c d", c=8),
               dram["wo%dT" % l].rearrange("(c p) d -> p c d", p=128)[
                   :, :, hf * 512:(hf + 1) * 512], "wos")
        k.wld("act", "wos")
        def fcast(hf=hf):
            return a.activation(
                wob[:, :].rearrange("p (c d) -> p c d", c=8)[:, :, hf * 512:(hf + 1) * 512],
                sb["wblk2"][:, 0:4096].rearrange("p (c d) -> p c d", c=8), AF.Identity)
        k.act(fcast)
    k.wait("pe", "dve")
    for c in range(NC):
        for tg in range(2):
            bk = "t%d" % (tg % 2)
            k.bw(bk)
            for j in range(8):
                tt = tg * 8 + j
                def ftr(c=c, tt=tt, j=j, bk=bk):
                    return t.transpose(ps[bk][:, j * 128:(j + 1) * 128],
                                       otm[:, tt * E + c * 128:tt * E + (c + 1) * 128],
                                       sb["identb"][:, :])
                (k.peq if j < 7 else k.pe)(ftr)
            k.bwd(bk)
            k.br("dve", bk)
            def fev(c=c, tg=tg, bk=bk):
                return v.tensor_copy(
                    ofm[:, c * S + tg * 1024: c * S + (tg + 1) * 1024], ps[bk][:, :])
            k.dve(fev)
            k.brd("dve", bk)
    k.wait("pe", "act")
    for tt in range(NT):
        for dh in range(2):
            bk = "b%d" % (3 + dh)
            k.bw(bk)
            k.wait("pe", "dve")
            for c in range(NC):
                def fmm(tt=tt, dh=dh, c=c, bk=bk):
                    lhsT = ofm[:, c * S + tt * 128: c * S + (tt + 1) * 128]
                    rhs = wob[:, c * E + dh * 512: c * E + (dh + 1) * 512]
                    return t.matmul(ps[bk][:, :], lhsT, rhs, start=(c == 0), stop=False)
                k.peq(fmm)
            def fbias(dh=dh, bk=bk):
                return t.matmul(ps[bk][:, :], sb["onesb"][:, :],
                                sb["browb"][0:1, dh * 512:(dh + 1) * 512],
                                start=False, stop=True)
            k.pe(fbias)
            k.bwd(bk)
            dstv = stm.rearrange("p (tt e) -> p tt e", e=E)
            if l == 0:
                k.br("dve", bk)
                def fev2(tt=tt, dh=dh, bk=bk, dstv=dstv):
                    return v.tensor_copy(dstv[:, tt, dh * 512:(dh + 1) * 512], ps[bk][:, :])
                k.dve(fev2)
                k.brd("dve", bk)
            else:
                k.br("dve", bk)
                def fmx(bk=bk):
                    return v.tensor_scalar(sb["qkrot"][:, 0:512], ps[bk][:, :],
                                           0.0, None, OP.max)
                k.dve(fmx)
                def fmn(bk=bk):
                    return v.tensor_scalar(sb["qkrot"][:, 512:1024], ps[bk][:, :],
                                           0.0, None, OP.min)
                k.dve(fmn)
                k.brd("dve", bk)
                def fadd2(tt=tt, dh=dh, dstv=dstv):
                    return v.tensor_tensor(dstv[:, tt, dh * 512:(dh + 1) * 512],
                                           sb["qkrot"][:, 0:512],
                                           sb["qkrot"][:, 512:1024], OP.add)
                k.dve(fadd2)



def _ph_ffn0(nc, k, dram, sb, ps, xsrc, xdst):
    k.fence_sp()
    v, t, a = nc.vector, nc.tensor, nc.scalar
    hwin = sb["A2"]
    wbf = sb["wblk"]
    wbb = sb["wblk"][:, :].bitcast(BF16)
    w1st = wbf[:, 0:2048].rearrange("p (b c s) -> p b c s", b=2, c=8)
    w1bf = wbb[:, 4096:6144].rearrange("p (b c s) -> p b c s", b=2, c=8)
    w2st = wbf[:, 3072:4096].rearrange("p (b s) -> p b s", b=2)
    w2bf = wbb[:, 8192:9216].rearrange("p (b s) -> p b s", b=2)
    xnw = sb["wblk2"][:, :].bitcast(BF16)[:, 0:8192].rearrange(
        "p (b c s) -> p b c s", b=2, c=8)
    k.wait("sp", "pe")
    k.wait("sp", "dve")
    k.load(sb["brow"][:, :], dram["b2"][:, :], "brow")
    k.wld("pe", "brow")
    k.wait("sp", "st")
    w1_user = [0, 0]
    w2_user = [0, 0]
    xn_user = [0, 0]
    for w in range(NW):
        xb = w % 2
        k.wait("sp", "pe", max(xn_user[xb], w2_user[0], w2_user[1]))
        k.load(xnw[:, xb], dram["xn_fm"].rearrange("(c p) s -> p c s", p=128)[
            :, :, w * 512:(w + 1) * 512], "xnw%d" % xb)
        for ic in range(NIC):
            wb = ic % 2
            if wb == 0:
                k.wait("sp", "pe", max(w1_user[0], w1_user[1]))
                k.wait("sp", "dve")
                k.load(sb["wblk"][:, 0:2048].rearrange("p (c bi) -> p c bi", bi=256),
                       dram["W1"].rearrange("(c p) i -> p c i", p=128)[
                           :, :, ic * 128:(ic + 2) * 128], "w1p")
            k.wld("act", "w1p")
            k.wait("act", "pe", w1_user[wb])
            def fc1(wb=wb):
                w1v = sb["wblk"][:, 0:2048].rearrange("p (c b s) -> p c b s", b=2, s=128)
                return a.activation(w1bf[:, wb], w1v[:, :, wb, :], AF.Identity)
            k.act(fc1)
            tcast = k.cnt["act"]
            bk = ("b0", "b1", "b4", "b5")[ic % 4]
            k.wld("pe", "xnw%d" % xb)
            k.wait("pe", "act", tcast)
            k.bw(bk)
            for fc in range(NC):
                def fmm(fc=fc, wb=wb, xb=xb, bk=bk):
                    return t.matmul(ps[bk][:, :], w1bf[:, wb, fc], xnw[:, xb, fc],
                                    start=(fc == 0), stop=(fc == NC - 1))
                (k.peq if fc < NC - 1 else k.pe)(fmm)
            k.bwd(bk)
            w1_user[wb] = xn_user[xb] = k.cnt["pe"]
            k.br("dve", bk)
            def fre(ic=ic, bk=bk):
                return v.tensor_scalar(hwin[:, ic * 512:(ic + 1) * 512], ps[bk][:, :],
                                       sb["b1"][:, ic:ic + 1], 0.0, OP.add, OP.max)
            k.dve(fre)
            k.brd("dve", bk)
        for dh in range(2):
            for ic in range(NIC):
                wb2 = ic % 2
                if wb2 == 0:
                    k.wait("sp", "pe", max(w2_user[0], w2_user[1]))
                    k.wait("sp", "dve")
                    k.load(w2st[:, 0:2],
                           dram["W2"][ic * 128:(ic + 2) * 128,
                                      dh * 512:(dh + 1) * 512].rearrange(
                               "(b p) s -> p b s", p=128), "w2p")
                k.wld("act", "w2p")
                k.wait("act", "pe", w2_user[wb2])
                def fc2(wb2=wb2):
                    return a.activation(w2bf[:, wb2], w2st[:, wb2], AF.Identity)
                k.act(fc2)
                tcast = k.cnt["act"]
                k.wait("pe", "act", tcast)
                k.wait("pe", "dve")
                for tj in range(4):
                    bk = "b%d" % (2 + tj)
                    if ic == 0:
                        k.bw(bk)
                    def fmm(ic=ic, tj=tj, wb2=wb2, bk=bk):
                        return t.matmul(
                            ps[bk][:, :],
                            hwin[:, ic * 512 + tj * 128: ic * 512 + (tj + 1) * 128],
                            w2bf[:, wb2], start=(ic == 0), stop=False)
                    (k.peq if tj < 3 else k.pe)(fmm)
                w2_user[wb2] = k.cnt["pe"]
            for tj in range(4):
                bk = "b%d" % (2 + tj)
                def fbias(bk=bk, dh=dh):
                    return t.matmul(ps[bk][:, :], sb["ones1"][:, :],
                                    sb["brow"][0:1, dh * 512:(dh + 1) * 512],
                                    start=False, stop=True)
                k.pe(fbias)
                k.bwd(bk)
                tt = w * 4 + tj
                k.br("act", bk)
                k.wait("act", "dve")
                def fsi(tj=tj, dh=dh, bk=bk):
                    gw = sb["wblk"][:, :].bitcast(BF16)[:, 9216:13312]
                    return a.activation(
                        gw[:, tj * 1024 + dh * 512: tj * 1024 + (dh + 1) * 512],
                        ps[bk][:, :], AF.Silu)
                k.act(fsi)
                k.brd("act", bk)
        _merge_window(nc, k, dram, sb, xsrc, xdst, w,
                      sb["wblk"][:, :].bitcast(BF16)[:, 9216:13312])


def _merge_window(nc, k, dram, sb, xsrc, xdst, w, gwin):
    v = nc.vector
    stm = sb["AV"][:, 0:16384]
    for tj in range(4):
        tt = w * 4 + tj
        k.wait("sp", "st")
        k.wait("sp", "dve")
        k.load(sb["xio"][:, :], xsrc[tt * 128:(tt + 1) * 128, :], "xio")
        k.wld("dve", "xio")
        k.wait("dve", "act")
        def fmul(tt=tt, tj=tj):
            return v.tensor_tensor(sb["wblk2"][:, 4096:5120],
                                   stm[:, tt * E:(tt + 1) * E],
                                   gwin[:, tj * 1024:(tj + 1) * 1024], OP.mult)
        k.dve(fmul)
        def fadd():
            return v.tensor_tensor(sb["xio"][:, :], sb["xio"][:, :],
                                   sb["wblk2"][:, 4096:5120], OP.add)
        k.dve(fadd)
        k.wait("sp", "dve")
        k.store(xdst[tt * 128:(tt + 1) * 128, :], sb["xio"][:, :])



def _wquant(nc, k, dram, sb, ps):
    k.fence_sp()
    """gate/up/down -> bf16 q*sc8, transposed, in HBM (wgq/wuq/wdq).
    scl: [9]=gs_w, [10]=gs_w/6, [11]=1/gs_w (per weight);
    [6]/[7]/[8] = 1/(gs_act*gs_w) evac scales."""
    v, t, a = nc.vector, nc.tensor, nc.scalar
    specs = [("gw", "wgq", I, E, 0, 6), ("uw", "wuq", I, E, 0, 7),
             ("dw", "wdq", E, I, 3, 8)]
    wload = sb["wblk"][:, :].rearrange("p (b s) -> p b s", b=2)
    stage = sb["A2"]
    for (nm, qn, NR, NCL, actbase, sci) in specs:
        nrch = NR // 128
        for rc in range(nrch):
            buf = rc % 2
            k.wait("sp", "dve")
            k.load(wload[:, buf, 0:NCL], dram[nm][rc * 128:(rc + 1) * 128, :],
                   "wl%d" % buf)
            k.wld("dve", "wl%d" % buf)
            def fam(rc=rc, buf=buf, NCL=NCL):
                return v.tensor_reduce(sb["qw"][:, rc:rc + 1], wload[:, buf, 0:NCL],
                                       AX.X, OP.max, apply_absolute_value=True)
            k.dve(fam)
        k.dve(lambda nrch=nrch: v.tensor_reduce(sb["scl"][:, 15:16],
                                                sb["qw"][:, 0:nrch], AX.X, OP.max))
        k.wait("pe", "dve")
        k.bw("b5")
        k.pe(lambda: t.matmul(ps["b5"][0:1, 0:128], sb["scl"][:, 15:16],
                              sb["identf"][:, :], start=True, stop=True,
                              is_transpose=True))
        k.bwd("b5")
        k.br("dve", "b5")
        k.dve(lambda: v.tensor_reduce(sb["mrow"][0:1, 0:1], ps["b5"][0:1, 0:128],
                                      AX.X, OP.max))
        k.brd("dve", "b5")
        k.wait("pe", "dve")
        k.bw("b5")
        k.pe(lambda: t.matmul(ps["b5"][:, 1:2], sb["ones1"][:, :],
                              sb["mrow"][0:1, 0:1], start=True, stop=True))
        k.bwd("b5")
        k.br("dve", "b5")
        k.dve(lambda: v.tensor_copy(sb["scl"][:, 14:15], ps["b5"][:, 1:2]))
        k.brd("dve", "b5")
        k.dve(lambda: v.reciprocal(sb["scl"][:, 9:10], sb["scl"][:, 14:15]))
        k.dve(lambda: v.tensor_scalar(sb["scl"][:, 9:10], sb["scl"][:, 9:10],
                                      448.0 * 6.0, None, OP.mult))
        k.dve(lambda: v.tensor_scalar(sb["scl"][:, 10:11], sb["scl"][:, 9:10],
                                      1.0 / 6.0, None, OP.mult))
        k.dve(lambda: v.reciprocal(sb["scl"][:, 11:12], sb["scl"][:, 9:10]))
        k.dve(lambda sci=sci, actbase=actbase: v.tensor_tensor(
            sb["scl"][:, sci:sci + 1], sb["scl"][:, 9:10],
            sb["scl"][:, actbase:actbase + 1], OP.mult))
        k.dve(lambda sci=sci: v.reciprocal(sb["scl"][:, sci:sci + 1],
                                           sb["scl"][:, sci:sci + 1]))
        ncols = NCL // 128
        for rg in range(nrch // 4):
            k.wait("dve", "st")
            for j in range(4):
                rc = rg * 4 + j
                buf = rc % 2
                k.wait("sp", "dve")
                k.wait("sp", "act")
                k.load(wload[:, buf, 0:NCL], dram[nm][rc * 128:(rc + 1) * 128, :],
                       "wl%d" % buf)
                k.wld("dve", "wl%d" % buf)
                k.wld("act", "wl%d" % buf)
                nhalf = 2048 if NCL > 2048 else NCL
                for hh in range(NCL // nhalf):
                    _quant_tile(nc, k, sb, ps,
                                src=wload[:, buf, hh * nhalf:(hh + 1) * nhalf],
                                n=nhalf, scol=9,
                                dst=stage, dst_stride=512,
                                dst_c0=hh * (nhalf // 128), dst_off=j * 128,
                                src_slot="wl%d" % buf)
            k.wait("sp", "dve")
            k.store(dram[qn].rearrange("(c p) i -> p c i", p=128)[
                :, :, rg * 512:(rg + 1) * 512],
                stage[:, 0:ncols * 512].rearrange("p (c s) -> p c s", s=512))



def _ph_ffn1(nc, k, dram, sb, ps, xsrc, xdst):
    while k.pump is not None:
        k.do_pump(8)
    k.fence_sp()
    v, t, a = nc.vector, nc.tensor, nc.scalar
    hid = sb["A2"]
    hqw = sb["wblk"][:, :].bitcast(BF16)
    wq2 = sb["wsh"][:, :].rearrange("p (b s) -> p b s", b=2)
    xqT = sb["xqT"]
    wq_user = [0]
    for w in range(NW):
        for ic in range(8):
            k.wait("sp", "pe", wq_user[0])
            k.wait("sp", "st")
            k.load(wq2[:, 0].rearrange("p (c s) -> p c s", c=8),
                   dram["wgq"].rearrange("(c p) i -> p c i", p=128)[
                       :, :, ic * 512:(ic + 1) * 512], "wqa")
            k.load(wq2[:, 1].rearrange("p (c s) -> p c s", c=8),
                   dram["wuq"].rearrange("(c p) i -> p c i", p=128)[
                       :, :, ic * 512:(ic + 1) * 512], "wqb")
            for tj in range(4):
                tt = w * 4 + tj
                bg = "b%d" % (2 * (tj % 2))
                bu = "b%d" % (2 * (tj % 2) + 1)
                sgb = sb["qkrot"][:, (tj % 2) * 512:(tj % 2 + 1) * 512]
                k.wld("pe", "wqa")
                k.wld("pe", "wqb")
                k.bw(bg)
                for hc in range(NC):
                    def fg(hc=hc, tt=tt, bg=bg):
                        return t.matmul(
                            ps[bg][:, :],
                            xqT[:, :].rearrange("p (c s) -> p c s", s=S)[
                                :, hc, tt * 128:(tt + 1) * 128],
                            wq2[:, 0].rearrange("p (c s) -> p c s", c=8)[:, hc],
                            start=(hc == 0), stop=(hc == NC - 1))
                    (k.peq if hc < NC - 1 else k.pe)(fg)
                k.bwd(bg)
                k.bw(bu)
                for hc in range(NC):
                    def fu(hc=hc, tt=tt, bu=bu):
                        return t.matmul(
                            ps[bu][:, :],
                            xqT[:, :].rearrange("p (c s) -> p c s", s=S)[
                                :, hc, tt * 128:(tt + 1) * 128],
                            wq2[:, 1].rearrange("p (c s) -> p c s", c=8)[:, hc],
                            start=(hc == 0), stop=(hc == NC - 1))
                    (k.peq if hc < NC - 1 else k.pe)(fu)
                k.bwd(bu)
                wq_user[0] = k.cnt["pe"]
                k.br("act", bg)
                k.wait("act", "dve", k.cnt["dve"] - 1)
                def fsg(bg=bg, sgb=sgb):
                    return a.activation(sgb, ps[bg][:, :], AF.Silu,
                                        scale=sb["scl"][:, 6:7])
                k.act(fsg)
                k.brd("act", bg)
                k.br("dve", bu)
                k.wait("dve", "act")
                def fh(tj=tj, ic=ic, bu=bu, sgb=sgb):
                    return v.scalar_tensor_tensor(
                        hid[:, :].rearrange("p (tj i) -> p tj i", i=I)[
                            :, tj, ic * 512:(ic + 1) * 512],
                        ps[bu][:, :], sb["scl"][:, 7:8], sgb,
                        OP.mult, OP.mult)
                k.dve(fh)
                k.brd("dve", bu)
        k.wait("dve", "pe")
        for tj in range(4):
            for hh in range(2):
                _quant_tile(nc, k, sb, ps,
                            src=hid[:, :].rearrange("p (tj i) -> p tj i", i=I)[
                                :, tj, hh * 2048:(hh + 1) * 2048],
                            n=2048, scol=3,
                            dst=hqw, dst_stride=512, dst_c0=hh * 16, dst_off=tj * 128)
        dn_user = [0, 0]
        k.wait("sp", "pe")
        for dh in range(2):
            for ic in range(NIC):
                wb = ic % 2
                if wb == 0:
                    k.wait("sp", "pe", max(dn_user[0], dn_user[1]))
                    k.load(wq2[:, 0:2, 0:512],
                           dram["wdq"][ic * 128:(ic + 2) * 128,
                                       dh * 512:(dh + 1) * 512].rearrange(
                               "(b p) s -> p b s", p=128), "wqa")
                k.wld("pe", "wqa")
                if ic == 0:
                    k.wait("pe", "dve")
                for tj in range(4):
                    bk = "b%d" % (2 + tj)
                    if ic == 0:
                        k.bw(bk)
                    def fd(ic=ic, tj=tj, wb=wb, bk=bk):
                        return t.matmul(
                            ps[bk][:, :],
                            hqw[:, :].rearrange("p (c s) -> p c s", s=512)[
                                :, ic, tj * 128:(tj + 1) * 128],
                            wq2[:, wb, 0:512],
                            start=(ic == 0), stop=(ic == NIC - 1))
                    (k.peq if tj < 3 else k.pe)(fd)
                dn_user[wb] = k.cnt["pe"]
            for tj in range(4):
                bk = "b%d" % (2 + tj)
                k.bwd(bk)
                tt = w * 4 + tj
                k.br("act", bk)
                k.wait("act", "dve")
                def fsi(tj=tj, dh=dh, bk=bk):
                    gw = sb["A2"][:, 0:4096]
                    return a.activation(
                        gw[:, tj * 1024 + dh * 512: tj * 1024 + (dh + 1) * 512],
                        ps[bk][:, :], AF.Silu, scale=sb["scl"][:, 8:9])
                k.act(fsi)
                k.brd("act", bk)
        wq_user[0] = k.cnt["pe"]
        _merge_window(nc, k, dram, sb, xsrc, xdst, w, sb["A2"][:, 0:4096])



def kernel(**kw):
    maps = _prep_inputs(kw)
    nc = build_program()
    res = run_bass_kernel_spmd(nc, maps, core_ids=list(range(8)))
    return np.stack([r["y"] for r in res.results], axis=0)


# version 14 (final)
# speedup vs baseline: 1.0875x; 1.0532x over previous
import numpy as np
import concourse.bass as bass
import concourse.mybir as mybir
from concourse.alu_op_type import AluOpType as OP
from concourse.bass_utils import run_bass_kernel_spmd

dt = mybir.dt
F32, F32R, BF16, FP8 = dt.float32, dt.float32r, dt.bfloat16, dt.float8e4
AF = mybir.ActivationFunctionType
AX = mybir.AxisListType

S, E, H, D, I = 2048, 1024, 16, 64, 4096
NT, NC, NW, NIC = S // 128, E // 128, S // 512, I // 128
ROPE_BASE, EPS = 10000.0, 1e-5


def _host_constants():
    c = {}
    pos = np.arange(S, dtype=np.float64)
    inv = 1.0 / ROPE_BASE ** (np.arange(0, D, 2, dtype=np.float64) / D)
    ang = inv[:, None] * pos[None, :]
    c["rope_cos"] = np.tile(np.repeat(np.cos(ang), 2, axis=0), (2, 1)).astype(np.float32)
    c["rope_sin"] = np.tile(np.repeat(np.sin(ang), 2, axis=0), (2, 1)).astype(np.float32)
    P = np.zeros((128, 128), np.float32)
    for b in (0, 64):
        for i in range(32):
            P[b + i, b + i + 32] = -1.0
            P[b + 32 + i, b + i] = 1.0
    c["rotPT"] = np.ascontiguousarray(P.T)
    c["ident"] = np.eye(128, dtype=np.float32)
    u = np.arange(896)
    c["causal"] = (np.arange(128)[:, None] <= (u[None, :] - 384)).astype(np.float32)
    hm = np.zeros((128, 2), np.float32)
    hm[:64, 0] = 1.0
    hm[64:, 1] = 1.0
    c["headmask"] = hm
    pm = np.zeros((2, 64), np.float32)
    for h in range(16):
        pm[h % 2, h * 4:(h + 1) * 4] = -0.125
    c["parity"] = pm
    c["ones1"] = np.ones((1, 128), np.float32)
    c["ones2"] = np.ones((2, 128), np.float32)
    return c


def _fm(b):
    return np.ascontiguousarray(np.asarray(b, np.float32).reshape(-1, 128).T)


def _prep_inputs(kw):
    c = _host_constants()
    x = np.asarray(kw["L0_attn_x"], np.float32)
    wq, wk = np.asarray(kw["L0_attn_wq"], np.float32), np.asarray(kw["L0_attn_wk"], np.float32)
    wv, wo = np.asarray(kw["L0_attn_wv"], np.float32), np.asarray(kw["L0_attn_wo"], np.float32)
    win = np.asarray(kw["L1_attn_w_in"], np.float32)
    bin_ = np.asarray(kw["L1_attn_b_in"], np.float32)
    sh = {
        "wqk0T": np.ascontiguousarray(np.concatenate([wq.T, wk.T], axis=1)),
        "wv0T": np.ascontiguousarray(wv.T),
        "wo0T": np.ascontiguousarray(wo.T),
        "bqk0": np.concatenate([_fm(kw["L0_attn_bq"]), _fm(kw["L0_attn_bk"])], axis=1),
        "bv0": np.asarray(kw["L0_attn_bv"], np.float32)[None, :],
        "bo0": np.asarray(kw["L0_attn_bo"], np.float32)[None, :],
        "W1": np.asarray(kw["L0_ffn_W1"], np.float32),
        "b1": _fm(kw["L0_ffn_b1"]),
        "W2": np.asarray(kw["L0_ffn_W2"], np.float32),
        "b2": np.asarray(kw["L0_ffn_b2"], np.float32)[None, :],
        "wqk1T": np.ascontiguousarray(win[:2 * E].T),
        "wv1T": np.ascontiguousarray(win[2 * E:].T),
        "bqk1": np.concatenate([_fm(bin_[:E]), _fm(bin_[E:2 * E])], axis=1),
        "bv1": bin_[2 * E:][None, :],
        "wo1T": np.ascontiguousarray(np.asarray(kw["L1_attn_w_out"], np.float32).T),
        "bo1": np.asarray(kw["L1_attn_b_out"], np.float32)[None, :],
        "gw": np.asarray(kw["L1_ffn_gate_w"], np.float32),
        "uw": np.asarray(kw["L1_ffn_up_w"], np.float32),
        "dw": np.asarray(kw["L1_ffn_dw" if False else "L1_ffn_down_w"], np.float32),
        "s_in": np.asarray(kw["L1_ffn_s_in"], np.float32).reshape(1, 1),
        "s_in_down": np.asarray(kw["L1_ffn_s_in_down"], np.float32).reshape(1, 1),
    }
    sh.update(c)
    return [dict(sh, x0=np.ascontiguousarray(x[core])) for core in range(8)]



class K:
    def __init__(self, nc):
        self.nc = nc
        self.q = {e: [] for e in ("pe", "act", "dve", "gp", "sp")}
        self.cnt = {n: 0 for n in ("ld", "st", "gst", "pe", "act", "dve", "gp",
                                   "pump")}
        self.sems = {}
        self.lastw = {e: {} for e in self.q}
        self.banks = {}
        self.slots = {}
        self.pool = []
        self.wq_done = {}

    def wait(self, eng, sem, val=None):
        v = self.cnt[sem] if val is None else val
        if v <= 0 or self.lastw[eng].get(sem, 0) >= v:
            return
        self.lastw[eng][sem] = v
        h = self.sems[sem]
        self.q[eng].append(lambda e, h=h, v=v: e.wait_ge(h, v))

    def _op(self, eng, sem, mk):
        if eng in ("act", "dve", "gp"):
            self.wait(eng, sem, self.cnt[sem])
        self.cnt[sem] += 1
        h = self.sems[sem]
        self.q[eng].append(lambda e, mk=mk, h=h: mk().then_inc(h, 1))
        return self.cnt[sem]

    def pe(self, mk):
        return self._op("pe", "pe", mk)

    def peq(self, mk):
        self.q["pe"].append(lambda e, mk=mk: mk())

    def act(self, mk):
        return self._op("act", "act", mk)

    def dve(self, mk):
        return self._op("dve", "dve", mk)

    def gp(self, mk):
        return self._op("gp", "gp", mk)

    def load(self, dst, src, slot="boot"):
        if slot not in self.slots:
            assert self.pool, "semaphore pool exhausted"
            self.slots[slot] = [self.pool.pop(), 0]
        ent = self.slots[slot]
        ent[1] += 16
        h = ent[0]
        nc = self.nc
        self.q["sp"].append(
            lambda e, dst=dst, src=src, h=h: nc.sync.dma_start(dst, src).then_inc(h, 16))
        return ent[1]

    def wld(self, eng, slot, val=None):
        if slot not in self.slots:
            return
        ent = self.slots[slot]
        v = ent[1] if val is None else val
        key = "L_" + slot
        if v <= 0 or self.lastw[eng].get(key, 0) >= v:
            return
        self.lastw[eng][key] = v
        h = ent[0]
        self.q[eng].append(lambda e, h=h, v=v: e.wait_ge(h, v))

    def store(self, dst, src):
        self.cnt["st"] += 16
        h = self.sems["st"]
        nc = self.nc
        self.q["sp"].append(
            lambda e, dst=dst, src=src, h=h: nc.sync.dma_start(dst, src).then_inc(h, 16))
        return self.cnt["st"]

    def gload(self, dst, src, slot):
        if slot not in self.slots:
            assert self.pool, "semaphore pool exhausted"
            self.slots[slot] = [self.pool.pop(), 0]
        ent = self.slots[slot]
        ent[1] += 16
        h = ent[0]
        nc = self.nc
        self.q["gp"].append(
            lambda e, dst=dst, src=src, h=h: nc.gpsimd.dma_start(dst, src).then_inc(h, 16))
        return ent[1]

    def gstore(self, dst, src):
        self.cnt["gst"] += 16
        h = self.sems["gst"]
        nc = self.nc
        self.q["gp"].append(
            lambda e, dst=dst, src=src, h=h: nc.gpsimd.dma_start(dst, src).then_inc(h, 16))
        return self.cnt["gst"]

    pump = None

    def do_pump(self, n=1):
        if self.pump is None:
            return
        for _ in range(n):
            try:
                next(self.pump)
                self.cnt["pump"] += 1
            except StopIteration:
                self.pump = None
                return

    def fence_sp(self):
        self.wait("sp", "pe")
        self.wait("sp", "act")
        self.wait("sp", "dve")
        self.wait("sp", "st")

    def bw(self, name):
        st = self.banks.setdefault(name, {"w": 0, "r": {}})
        for eng, v in st["r"].items():
            self.wait("pe", eng, v)
        st["r"] = {}

    def bwd(self, name):
        self.banks[name]["w"] = self.cnt["pe"]

    def br(self, eng, name):
        st = self.banks.setdefault(name, {"w": 0, "r": {}})
        self.wait(eng, "pe", st["w"])

    def brd(self, eng, name):
        self.banks[name]["r"][eng] = self.cnt[eng]



def build_program(tap=None):
    nc = bass.Bass("TRN2", target_bir_lowering=False)
    dram = {}

    def din(name, shape, dtype=F32):
        dram[name] = nc.dram_tensor(name, list(shape), dtype, kind="ExternalInput")

    def dsc(name, shape, dtype=F32):
        dram[name] = nc.dram_tensor(name, list(shape), dtype, kind="Internal")

    din("x0", (S, E))
    din("wqk0T", (E, 2 * E)); din("wv0T", (E, E)); din("wo0T", (E, E))
    din("bqk0", (128, 16)); din("bv0", (1, E)); din("bo0", (1, E))
    din("W1", (E, I)); din("b1", (128, NIC)); din("W2", (I, E)); din("b2", (1, E))
    din("wqk1T", (E, 2 * E)); din("wv1T", (E, E)); din("wo1T", (E, E))
    din("bqk1", (128, 16)); din("bv1", (1, E)); din("bo1", (1, E))
    din("gw", (I, E)); din("uw", (I, E)); din("dw", (E, I))
    din("s_in", (1, 1)); din("s_in_down", (1, 1))
    din("rope_cos", (128, S)); din("rope_sin", (128, S))
    din("rotPT", (128, 128)); din("ident", (128, 128)); din("causal", (128, 896))
    din("headmask", (128, 2)); din("parity", (2, 64))
    din("ones1", (1, 128)); din("ones2", (2, 128))
    dsc("xcur", (S, E))
    dsc("xn_fm", (E, S), BF16)
    dsc("Qs", (E, S), BF16); dsc("Ks", (E, S), BF16)
    dsc("wgq", (E, I), BF16); dsc("wuq", (E, I), BF16); dsc("wdq", (I, E), BF16)
    dram["y"] = nc.dram_tensor("y", [S, E], F32, kind="ExternalOutput")

    import contextlib
    stack = contextlib.ExitStack()
    sb = {}

    def sbuf(name, shape, dtype):
        sb[name] = stack.enter_context(nc.sbuf_tensor("sb_" + name, list(shape), dtype))

    sbuf("rotPT", (128, 128), BF16)
    sbuf("identf", (128, 128), F32)
    sbuf("identb", (128, 128), BF16)
    sbuf("causal", (128, 896), BF16)
    sbuf("headmask", (128, 2), F32)
    sbuf("parity", (2, 64), F32)
    sbuf("ones1", (1, 128), F32)
    sbuf("ones2", (2, 128), F32)
    sbuf("onesb", (1, 128), BF16)
    sbuf("onescol", (128, 1), BF16)
    sbuf("headmaskb", (128, 2), BF16)
    sbuf("bqk", (128, 16), F32)
    sbuf("b1", (128, NIC), F32)
    sbuf("brow", (1, E), F32)
    sbuf("browb", (1, E), BF16)
    sbuf("scl", (128, 16), F32)
    sbuf("stats", (128, 4 * NT), F32)
    sbuf("bnraw", (128, 12), F32)
    sbuf("qw", (128, 16 * NW), F32)
    sbuf("cmaxK", (128, 8), F32)
    sbuf("cmaxM", (128, 16), F32)
    sbuf("mrow", (2, 96), F32)
    sbuf("Bsb", (128, 64), F32)
    sbuf("zeros", (128, 512), BF16)
    sbuf("rsum", (128, 4), F32)
    sbuf("msk8", (128, 512), dt.uint8)
    sbuf("qsml", (128, 6 * 128), F32)
    sbuf("qfp8", (128, 2 * 128), FP8)
    sbuf("qkrot", (128, 2 * 512), BF16)
    sbuf("rtmp", (128, 2 * 512), F32)
    sbuf("at", (128, 2 * 512), BF16)
    sbuf("gtile", (128, E), BF16)
    sbuf("xio", (128, E), F32)
    sbuf("A2", (128, 16384), BF16)
    sbuf("xqT", (128, 16384), BF16)
    sbuf("AV", (128, NT * 16 * 65), BF16)
    sbuf("wblk", (128, 8192), F32)
    sbuf("wblk2", (128, 8192), F32)
    sbuf("wsh", (128, 8192), BF16)

    ps = {}
    for i in range(6):
        ps["b%d" % i] = stack.enter_context(nc.psum_tensor("pb%d" % i, [128, 512], F32))
    ps["t0"] = stack.enter_context(nc.psum_tensor("pt0", [128, 1024], BF16))
    ps["t1"] = stack.enter_context(nc.psum_tensor("pt1", [128, 1024], BF16))

    import contextlib as _ctx
    semstack = stack
    h_ld = semstack.enter_context(nc.semaphore("s_ld"))
    h_st = semstack.enter_context(nc.semaphore("s_st"))
    h_pe = semstack.enter_context(nc.semaphore("s_pe"))
    h_act = semstack.enter_context(nc.semaphore("s_act"))
    h_dve = semstack.enter_context(nc.semaphore("s_dve"))
    h_gp = semstack.enter_context(nc.semaphore("s_gp"))
    h_gst = semstack.enter_context(nc.semaphore("s_gst"))
    pool = [semstack.enter_context(nc.semaphore("s_p%d" % i)) for i in range(32)]
    with nc.Block() as block:
        k = K(nc)
        k.sems = {"ld": h_ld, "st": h_st, "pe": h_pe, "act": h_act,
                  "dve": h_dve, "gp": h_gp, "gst": h_gst}
        k.pool = pool
        _build_all(nc, k, dram, sb, ps, tap)
        global _LAST_CNT
        _LAST_CNT = dict(k.cnt)

        @block.sync
        def _(e):
            for fn in k.q["sp"]:
                fn(e)

        @block.tensor
        def _(e):
            for fn in k.q["pe"]:
                fn(e)

        @block.scalar
        def _(e):
            for fn in k.q["act"]:
                fn(e)

        @block.vector
        def _(e):
            for fn in k.q["dve"]:
                fn(e)

        @block.gpsimd
        def _(e):
            for fn in k.q["gp"]:
                fn(e)

    stack.close()
    return nc



def _build_all(nc, k, dram, sb, ps, tap):
    v, t, a = nc.vector, nc.tensor, nc.scalar

    for n, tgt in (("ident", "identf"), ("headmask", "headmask"),
                   ("parity", "parity"), ("ones1", "ones1"), ("ones2", "ones2"),
                   ("b1", "b1")):
        k.load(sb[tgt][:, :], dram[n][:, :], "boot")
    k.load(sb["rtmp"][:, 0:128], dram["rotPT"][:, :], "boot")
    k.load(sb["wblk"][:, 0:896], dram["causal"][:, :], "boot")
    k.wld("dve", "boot")
    k.wld("pe", "boot")
    k.wld("act", "boot")
    k.dve(lambda: v.tensor_copy(sb["identb"][:, :], sb["identf"][:, :]))
    k.dve(lambda: v.tensor_copy(sb["rotPT"][:, :], sb["rtmp"][:, 0:128]))
    k.dve(lambda: v.tensor_copy(sb["causal"][:, :], sb["wblk"][:, 0:896]))
    k.dve(lambda: v.memset(sb["zeros"][:, :], 0.0))
    k.dve(lambda: v.memset(sb["onesb"][:, :], 1.0))
    k.dve(lambda: v.memset(sb["onescol"][:, :], 1.0))
    k.dve(lambda: v.tensor_copy(sb["headmaskb"][:, :], sb["headmask"][:, :]))

    k.load(sb["brow"][0:1, 0:1], dram["s_in"][:, :], "brow")
    k.load(sb["brow"][0:1, 1:2], dram["s_in_down"][:, :], "brow")
    k.wld("pe", "brow")
    k.bw("b5")
    k.pe(lambda: t.matmul(ps["b5"][:, 0:2], sb["ones1"][:, :], sb["brow"][0:1, 0:2],
                          start=True, stop=True))
    k.bwd("b5")
    k.br("dve", "b5")
    k.dve(lambda: v.tensor_copy(sb["scl"][:, 0:1], ps["b5"][:, 0:1]))
    k.dve(lambda: v.tensor_copy(sb["scl"][:, 3:4], ps["b5"][:, 1:2]))
    k.brd("dve", "b5")
    for base in (0, 3):
        k.dve(lambda base=base: v.tensor_scalar(
            sb["scl"][:, base + 1:base + 2], sb["scl"][:, base:base + 1],
            1.0 / 6.0, None, OP.mult))
        k.dve(lambda base=base: v.reciprocal(
            sb["scl"][:, base + 2:base + 3], sb["scl"][:, base:base + 1]))

    if tap != "wq":
        k.pump = _wquant_gen(nc, k, dram, sb, ps)
    _layer(nc, k, dram, sb, ps, 0, tap)
    k.cnt["pump_l0"] = k.cnt["pump"]
    k.wait("gp", "dve")
    k.wait("gp", "act")
    if tap is not None and tap.endswith("0"):
        _finish(k)
        return
    if tap == "wq":
        _wquant(nc, k, dram, sb, ps)
        _finish(k)
        return
    _layer(nc, k, dram, sb, ps, 1, tap)
    _finish(k)


def _wquant_gen(nc, k, dram, sb, ps):
    v, t, a = nc.vector, nc.tensor, nc.scalar
    NQ = 1024
    ldb = sb["wblk2"][:, 4096:5120]
    w2b = sb["wblk2"][:, :].bitcast(BF16)
    TA = sb["wblk2"][:, 6144:7168]
    rtb = sb["rtmp"][:, :].bitcast(BF16)
    NL = rtb[:, 0:1024]
    NM = rtb[:, 1024:2048]
    SG = sb["gtile"][:, 0:1024]
    QO = w2b[:, 14336:15360]
    stage = w2b[:, 15360:16384]
    specs = [("gw", "wgq", I, E, 0, 6), ("uw", "wuq", I, E, 0, 7),
             ("dw", "wdq", E, I, 3, 8)]
    st_ia = st_isrc = st_iq = st_ist = st_ifam = 0
    first = True
    for (nm, qn, NR, NCL, actbase, sci) in specs:
        nrch = NR // 128
        nq = NCL // NQ
        for rc in range(nrch):
            for qq in range(nq):
                idx = rc * nq + qq
                if first:
                    k.wait("gp", "dve")
                    k.wait("gp", "act")
                    first = False
                else:
                    k.wait("gp", "act", st_ia)
                    k.wait("gp", "dve", max(st_isrc, st_ifam))
                k.gload(ldb, dram[nm][rc * 128:(rc + 1) * 128, qq * NQ:(qq + 1) * NQ],
                        "wql")
                k.wld("dve", "wql")
                def fam(idx=idx):
                    return v.tensor_reduce(sb["stats"][:, idx:idx + 1], ldb,
                                           AX.X, OP.max, apply_absolute_value=True)
                st_ifam = k.dve(fam)
                yield
        tot = nrch * nq
        ird = k.dve(lambda tot=tot: v.tensor_reduce(sb["scl"][:, 15:16],
                                              sb["stats"][:, 0:tot], AX.X, OP.max))
        k.wait("pe", "dve", ird)
        k.bw("b5")
        k.pe(lambda: t.matmul(ps["b5"][0:1, 0:128], sb["scl"][:, 15:16],
                              sb["identf"][:, :], start=True, stop=True,
                              is_transpose=True))
        k.bwd("b5")
        k.br("dve", "b5")
        imr = k.dve(lambda: v.tensor_reduce(sb["mrow"][0:1, 64:65], ps["b5"][0:1, 0:128],
                                      AX.X, OP.max))
        k.brd("dve", "b5")
        k.wait("pe", "dve", imr)
        k.bw("b5")
        k.pe(lambda: t.matmul(ps["b5"][:, 1:2], sb["ones1"][:, :],
                              sb["mrow"][0:1, 64:65], start=True, stop=True))
        k.bwd("b5")
        k.br("dve", "b5")
        k.dve(lambda: v.tensor_copy(sb["scl"][:, 14:15], ps["b5"][:, 1:2]))
        k.brd("dve", "b5")
        k.dve(lambda: v.reciprocal(sb["scl"][:, 9:10], sb["scl"][:, 14:15]))
        k.dve(lambda: v.tensor_scalar(sb["scl"][:, 9:10], sb["scl"][:, 9:10],
                                      448.0 * 6.0, None, OP.mult))
        k.dve(lambda: v.tensor_scalar(sb["scl"][:, 10:11], sb["scl"][:, 9:10],
                                      1.0 / 6.0, None, OP.mult))
        k.dve(lambda: v.reciprocal(sb["scl"][:, 11:12], sb["scl"][:, 9:10]))
        k.dve(lambda sci=sci, actbase=actbase: v.tensor_tensor(
            sb["scl"][:, sci:sci + 1], sb["scl"][:, 9:10],
            sb["scl"][:, actbase:actbase + 1], OP.mult))
        k.dve(lambda sci=sci: v.reciprocal(sb["scl"][:, sci:sci + 1],
                                           sb["scl"][:, sci:sci + 1]))
        yield
        for rc in range(nrch):
            for qq in range(nq):
                k.wait("gp", "act", st_ia)
                k.wait("gp", "dve", max(st_isrc, st_ifam))
                k.gload(ldb, dram[nm][rc * 128:(rc + 1) * 128, qq * NQ:(qq + 1) * NQ],
                        "wql")
                k.wld("dve", "wql")
                st_ia, st_isrc, st_iq = _quant_tile(
                    nc, k, sb, ps, src=ldb, n=NQ, scol=9,
                    dst=stage, dst_stride=128, dst_c0=0, dst_off=0,
                    src_slot="wql", bufs=(TA, NL, NM, SG, QO),
                    prec={"sg_free": st_iq}, do_transpose=False)
                yield
                ifev = _qt_transpose(nc, k, sb, ps, NQ, stage, 128, 0, 0, QO,
                                     st_iq, st_ist)
                k.wait("gp", "dve", ifev)
                st_ist = k.gstore(dram[qn].rearrange("(c p) o -> p c o", p=128)[
                    :, qq * 8:(qq + 1) * 8, rc * 128:(rc + 1) * 128],
                    stage.rearrange("p (c o) -> p c o", o=128))
                yield
        k.wq_done[qn] = k.cnt["gst"]


def _finish(k):
    k.wait("sp", "st", k.cnt["st"])
    k.wait("sp", "gst", k.cnt["gst"])
    for s in list(k.slots):
        k.wld("sp", s)


def _layer(nc, k, dram, sb, ps, l, tap):
    v = nc.vector
    xsrc = dram["x0"] if l == 0 else dram["xcur"]
    xdst = dram["xcur"] if l == 0 else dram["y"]
    if tap == "l0" and l == 0:
        xdst = dram["y"]
    y3 = dram["y"].rearrange("(a p) b -> p a b", p=128)

    k.wait("sp", "act")
    k.wait("sp", "dve")
    k.load(sb["bqk"][:, :], dram["bqk%d" % l][:, :], "bqk")

    def tap_store(src_f32_flat, n):
        k.wait("sp", "dve")
        k.store(y3[:, 0:n // E, :],
                src_f32_flat.rearrange("p (a b) -> p a b", b=E))

    _ph_ln(nc, k, dram, sb, ps, l, xsrc)
    if tap == "ln%d" % l:
        tap_store(sb["A2"][:, :].bitcast(F32), 8192)
        return
    _ph_qk(nc, k, dram, sb, ps, l, tap)
    if tap is not None and (tap.startswith("qk") or tap.startswith("qs")) and tap[2:].isdigit():
        return
    if tap == "qka%d" % l:
        return
    _ph_v(nc, k, dram, sb, ps, l)
    if tap == "vv%d" % l:
        return
    _ph_mbias(nc, k, dram, sb, ps, l)
    if tap == "qkv%d" % l:
        tap_store(sb["AV"][:, 0:16384].bitcast(F32), 8192)
        k.store(y3[:, 15, 0:64], sb["Bsb"][:, :])
        return
    if tap == "q%d" % l:
        k.wait("sp", "dve")
        k.wait("sp", "act")
        k.store(dram["y"][0:1024, 0:1024], dram["Qs"][:, :].bitcast(F32))
        return
    if tap == "k%d" % l:
        k.wait("sp", "dve")
        k.wait("sp", "act")
        k.store(dram["y"][0:1024, 0:1024], dram["Ks"][:, :].bitcast(F32))
        return
    _ph_att(nc, k, dram, sb, ps, l)
    if tap == "att%d" % l:
        tap_store(sb["A2"][:, :].bitcast(F32), 8192)
        return
    _ph_wo(nc, k, dram, sb, ps, l)
    if tap == "wo%d" % l:
        tap_store(sb["AV"][:, 0:16384].bitcast(F32), 8192)
        return
    if l == 0:
        _ph_ffn0(nc, k, dram, sb, ps, xsrc, xdst)
    else:
        _ph_ffn1(nc, k, dram, sb, ps, xsrc, xdst)



def _ph_ln(nc, k, dram, sb, ps, l, xsrc):
    v, t, a = nc.vector, nc.tensor, nc.scalar
    xsv = xsrc.rearrange("(t p) e -> p t e", p=128)
    xbuf = sb["wblk"][:, 0:2048].rearrange("p (b e) -> p b e", b=2)
    nbuf = sb["wblk"][:, 2048:4096].rearrange("p (b e) -> p b e", b=2)
    xnT = sb["A2"]
    st2 = sb["stats"][:, 0:2 * NT].rearrange("p (t two) -> p t two", two=2)
    k.wait("sp", "st")
    k.wait("sp", "pe")
    k.wait("sp", "dve")
    k.wait("sp", "act")
    tld = [0, 0]
    varc = sb["stats"][:, 2 * NT:3 * NT]
    tmp16 = sb["stats"][:, 3 * NT:4 * NT]
    sink = sb["wblk"][:, 4096:5120]
    for i in range(NT):
        b = i % 2
        k.wait("sp", "dve")
        k.wait("sp", "act")
        k.load(xbuf[:, b], xsv[:, i], "xb%d" % b)
        k.wld("dve", "xb%d" % b)
        k.wld("act", "xb%d" % b)
        k.wait("act", "dve")
        def fsq(b=b):
            return a.activation(sink, xbuf[:, b], AF.Square)
        k.act(fsq)
        k.wait("dve", "act")
        def f(i=i, b=b):
            v.tensor_reduce(sb["stats"][:, 2 * i:2 * i + 1], xbuf[:, b], AX.X, OP.add)
            return v.tensor_reduce(varc[:, i:i + 1], sink, AX.X, OP.add)
        k.dve(f)
    k.dve(lambda: v.tensor_scalar(st2[:, :, 0], st2[:, :, 0], 1.0 / E, None, OP.mult))
    k.dve(lambda: v.tensor_scalar(varc, varc, 1.0 / E, None, OP.mult))
    k.dve(lambda: v.tensor_tensor(tmp16, st2[:, :, 0], st2[:, :, 0], OP.mult))
    k.dve(lambda: v.tensor_tensor(varc, varc, tmp16, OP.subtract))
    k.dve(lambda: v.tensor_scalar(varc, varc, EPS, None, OP.add))
    k.wait("act", "dve")
    k.act(lambda: a.activation(varc, varc, AF.Sqrt))
    k.wait("dve", "act")
    k.dve(lambda: v.reciprocal(varc, varc))
    for w in range(NW):
        for j in range(4):
            i = 4 * w + j
            b = i % 2
            k.wait("sp", "dve")
            k.wait("sp", "act")
            k.load(xbuf[:, b], xsv[:, i], "xb%d" % b)
            k.wld("dve", "xb%d" % b)
            k.wld("act", "xb%d" % b)
            k.wait("dve", "pe")
            k.wait("dve", "act")
            xt = nbuf[:, b]
            def fap(i=i, b=b, xt=xt):
                return v.tensor_scalar(xt, xbuf[:, b],
                                       sb["stats"][:, 2 * i:2 * i + 1],
                                       sb["stats"][:, 2 * NT + i:2 * NT + i + 1],
                                       OP.subtract, OP.mult)
            k.dve(fap)
            if l == 1:
                _quant_tile(nc, k, sb, ps, src=xt, n=E, scol=0,
                            dst=sb["xqT"], dst_stride=S, dst_c0=0, dst_off=i * 128)
            k.wait("pe", "dve")
            for half in range(2):
                bk = "b%d" % (4 + half)
                k.bw(bk)
                for c4 in range(4):
                    c = half * 4 + c4
                    def ftr(c=c, c4=c4, bk=bk, xt=xt):
                        return t.transpose(ps[bk][:, c4 * 128:(c4 + 1) * 128],
                                           xt[:, c * 128:(c + 1) * 128],
                                           sb["identf"][:, :])
                    (k.peq if c4 < 3 else k.pe)(ftr)
                k.bwd(bk)
                k.br("dve", bk)
                def fev(i=i, half=half, bk=bk):
                    dst = xnT[:, :].rearrange("p (c s) -> p c s", s=S)[
                        :, half * 4:(half + 1) * 4, i * 128:(i + 1) * 128]
                    return v.tensor_copy(dst, ps[bk][:, :].rearrange(
                        "p (c q) -> p c q", q=128))
                k.dve(fev)
                k.brd("dve", bk)
        k.wait("sp", "dve")
        k.store(dram["xn_fm"].rearrange("(c p) s -> p c s", p=128)[
            :, :, w * 512:(w + 1) * 512],
            xnT[:, :].rearrange("p (c s) -> p c s", s=S)[:, :, w * 512:(w + 1) * 512])


def _quant_tile(nc, k, sb, ps, src, n, scol, dst, dst_stride, dst_c0, dst_off,
                src_slot=None, bufs=None, prec=None, do_transpose=True):
    v, t, a = nc.vector, nc.tensor, nc.scalar
    nb = n // 16
    w2f = sb["wblk2"]
    w2b = sb["wblk2"][:, :].bitcast(BF16)
    if bufs is None:
        TA = w2f[:, 0:n]
        NL = w2b[:, 4096:4096 + n]
        NM = w2b[:, 6144:6144 + n]
        SG = w2b[:, 8192:8192 + n]
        QO = w2b[:, 10240:10240 + n]
    else:
        TA, NL, NM, SG, QO = bufs
    sm = sb["qsml"]
    amax, sc, c1f, sc8, r2, msk = (sm[:, i * 128:i * 128 + nb] for i in range(6))
    f8a = sb["qfp8"][:, 0:nb]
    f8b = sb["qfp8"][:, 128:128 + nb]
    gs6 = sb["scl"][:, scol + 1:scol + 2]
    ginv = sb["scl"][:, scol + 2:scol + 3]

    if prec is None:
        k.wait("act", "dve")
    else:
        k.wait("act", "dve", prec.get("sg_free", 0))
    if src_slot is not None:
        k.wld("act", src_slot)
    ia = k.act(lambda: a.activation(SG, src, AF.Sign))
    k.dve(lambda: v.tensor_reduce(amax, src.rearrange("p (b s) -> p b s", s=16),
                                  AX.X, OP.max, apply_absolute_value=True))
    k.dve(lambda: v.tensor_scalar(sc, amax, gs6, 448.0, OP.mult, OP.min))
    k.dve(lambda: v.tensor_scalar(f8a, sc, 240.0, None, OP.min))
    k.dve(lambda: v.tensor_scalar(c1f, f8a, 1.0, None, OP.mult))
    k.dve(lambda: v.tensor_scalar(f8b, sc, 0.5, None, OP.mult))
    k.dve(lambda: v.tensor_scalar(sc8, f8b, 2.0, None, OP.mult))
    k.dve(lambda: v.tensor_scalar(msk, sc, 240.0, None, OP.is_gt))
    k.dve(lambda: v.tensor_tensor(sc8, sc8, c1f, OP.subtract))
    k.dve(lambda: v.tensor_tensor(sc8, sc8, msk, OP.mult))
    k.dve(lambda: v.tensor_tensor(sc8, sc8, c1f, OP.add))
    k.dve(lambda: v.tensor_scalar(r2, sc8, ginv, None, OP.mult))
    k.dve(lambda: v.tensor_scalar(msk, r2, 0.0, None, OP.is_equal))
    k.dve(lambda: v.tensor_tensor(r2, r2, msk, OP.add))
    k.dve(lambda: v.reciprocal(r2, r2))
    r2b = r2.unsqueeze(2).broadcast_to((128, nb, 16))
    sc8b = sc8.unsqueeze(2).broadcast_to((128, nb, 16))
    isrc = k.dve(lambda: v.tensor_tensor(TA.rearrange("p (b s) -> p b s", s=16),
                                  src.rearrange("p (b s) -> p b s", s=16), r2b, OP.mult))
    k.dve(lambda: v.tensor_tensor(TA, TA, TA, OP.mult))
    k.dve(lambda: v.tensor_scalar(NL, TA, 0.0625, None, OP.is_ge))
    for thr in (0.5625, 1.5625, 3.0625):
        k.dve(lambda thr=thr: v.scalar_tensor_tensor(NL, TA, thr, NL, OP.is_ge, OP.add))
    k.dve(lambda: v.tensor_scalar(NM, TA, 6.25, None, OP.is_ge))
    k.dve(lambda: v.scalar_tensor_tensor(NM, TA, 12.25, NM, OP.is_ge, OP.add))
    k.dve(lambda: v.scalar_tensor_tensor(NM, NL, 0.5, NM, OP.mult, OP.add))
    k.dve(lambda: v.tensor_scalar(NL, TA, 25.0, None, OP.is_ge))
    k.dve(lambda: v.scalar_tensor_tensor(NM, NL, 2.0, NM, OP.mult, OP.add))
    if prec is None:
        k.wait("dve", "act")
    else:
        k.wait("dve", "act", ia)
    k.dve(lambda: v.tensor_tensor(SG.rearrange("p (b s) -> p b s", s=16),
                                  SG.rearrange("p (b s) -> p b s", s=16), sc8b, OP.mult))
    iq = k.dve(lambda: v.tensor_tensor(QO, NM, SG, OP.mult))
    if do_transpose:
        _qt_transpose(nc, k, sb, ps, n, dst, dst_stride, dst_c0, dst_off, QO,
                      None, 0)
    return ia, isrc, iq


def _qt_transpose(nc, k, sb, ps, n, dst, dst_stride, dst_c0, dst_off, QO,
                  iq, stage_free):
    v, t = nc.vector, nc.tensor
    if iq is None:
        k.wait("pe", "dve")
    else:
        k.wait("pe", "dve", iq)
    nch = n // 128
    ifev = 0
    for grp in range((nch + 7) // 8):
        bk = "t%d" % (grp % 2)
        ng = min(8, nch - grp * 8)
        k.bw(bk)
        for j in range(ng):
            c = grp * 8 + j
            def ftr(c=c, j=j, bk=bk, QO=QO):
                return t.transpose(ps[bk][:, j * 128:(j + 1) * 128],
                                   QO[:, c * 128:(c + 1) * 128], sb["identb"][:, :])
            (k.peq if j < ng - 1 else k.pe)(ftr)
        k.bwd(bk)
        k.br("dve", bk)
        if stage_free:
            k.wait("dve", "gst", stage_free)
        def fev(grp=grp, bk=bk, ng=ng):
            dstv = dst[:, :].rearrange("p (c s) -> p c s", s=dst_stride)[
                :, dst_c0 + grp * 8: dst_c0 + grp * 8 + ng, dst_off:dst_off + 128]
            return v.tensor_copy(dstv, ps[bk][:, :].rearrange(
                "p (c q) -> p c q", q=128)[:, 0:ng, :])
        ifev = k.dve(fev)
        k.brd("dve", bk)
    return ifev



def _ph_qk(nc, k, dram, sb, ps, l, tap=None):
    lim = int(tap[2:]) if (tap and tap.startswith("qk") and tap[2:].isdigit()) else 10 ** 9
    sub = int(tap[2:]) if (tap and tap.startswith("qs")) else 10 ** 9
    if sub < 10 ** 9:
        lim = 1
    it = [0]
    k.fence_sp()
    v, t, a = nc.vector, nc.tensor, nc.scalar
    xnT = sb["A2"]
    wqk = dram["wqk%dT" % l]
    wst = sb["wblk"][:, :].rearrange("p (b c s) -> p b c s", b=2, c=8)
    wcast = sb["wblk2"][:, :].bitcast(BF16).rearrange(
        "p (x) -> p x")[:, 0:8192].rearrange("p (b c s) -> p b c s", b=2, c=8)
    if l == 0:
        k.load(sb["wblk2"][:, 4096:6144], dram["rope_cos"][:, :].rearrange(
            "p (a s) -> p a s", a=1).squeeze(1)) if False else None
        k.wait("sp", "dve")
        k.load(sb["wblk2"][:, 4096:6144], dram["rope_cos"][:, :], "cossin")
        k.load(sb["wblk2"][:, 6144:8192], dram["rope_sin"][:, :], "cossin")
        k.wld("dve", "cossin")
    cosT = sb["wblk2"][:, 4096:6144]
    sinT = sb["wblk2"][:, 6144:8192]
    k.wld("act", "bqk")
    k.wld("dve", "bqk")
    wuser = [0, 0]
    cuser = [0, 0]
    for db in range(4):
        buf = db % 2
        k.wait("sp", "act", cuser[buf])
        k.load(wst[:, buf], wqk.rearrange("(c p) d -> p c d", p=128)[
            :, :, db * 512:(db + 1) * 512], "wst%d" % buf)
        k.wait("act", "pe", wuser[buf])
        k.wld("act", "wst%d" % buf)
        def fcast(buf=buf):
            return a.activation(wcast[:, buf], wst[:, buf], AF.Identity)
        k.act(fcast)
        cuser[buf] = k.cnt["act"]
        tcast = k.cnt["act"]
        for dc in range(4):
            g = db * 4 + dc
            isq = g < 8
            for w in range(NW):
                if it[0] >= lim:
                    return
                it[0] += 1
                bk = "b%d" % (w % 2)
                k.wait("pe", "act", tcast)
                k.bw(bk)
                for fc in range(NC):
                    def fmm(fc=fc, dc=dc, w=w, buf=buf, bk=bk):
                        lhsT = wcast[:, buf, fc, dc * 128:(dc + 1) * 128]
                        rhs = xnT[:, :].rearrange("p (c s) -> p c s", s=S)[
                            :, fc, w * 512:(w + 1) * 512]
                        return t.matmul(ps[bk][:, :], lhsT, rhs,
                                        start=(fc == 0), stop=(fc == NC - 1))
                    (k.peq if fc < NC - 1 else k.pe)(fmm)
                k.bwd(bk)
                wuser[buf] = k.cnt["pe"]
                if sub <= 1:
                    return
                qb = w % 2
                qkout = sb["qkrot"][:, qb * 512:(qb + 1) * 512]
                k.wait("act", "st")
                k.wait("act", "dve")
                k.br("act", bk)
                def fqb(g=g, bk=bk, qkout=qkout):
                    return a.activation(qkout, ps[bk][:, :], AF.Identity,
                                        bias=sb["bqk"][:, g:g + 1])
                k.act(fqb)
                k.brd("act", bk)
                if sub <= 2:
                    return
                if l == 0:
                    k.wait("pe", "act")
                    k.bw("b2")
                    def frot(qkout=qkout):
                        return t.matmul(ps["b2"][:, :], sb["rotPT"][:, :], qkout,
                                        start=True, stop=True)
                    k.pe(frot)
                    k.bwd("b2")
                    if sub <= 3:
                        return
                    k.br("dve", bk)
                    k.wait("dve", "act")
                    def fq0(g=g, w=w, bk=bk):
                        return v.scalar_tensor_tensor(
                            sb["rtmp"][:, 0:512], ps[bk][:, :], sb["bqk"][:, g:g + 1],
                            cosT[:, w * 512 - 4096 + 4096:w * 512 + 512] if False else
                            sb["wblk2"][:, 4096 + w * 512:4096 + (w + 1) * 512],
                            OP.add, OP.mult)
                    k.dve(fq0)
                    k.brd("dve", bk)
                    if sub <= 41:
                        return
                    k.br("dve", "b2")
                    def ft2(w=w):
                        return v.tensor_tensor(
                            sb["rtmp"][:, 512:1024], ps["b2"][:, :],
                            sb["wblk2"][:, 6144 + w * 512:6144 + (w + 1) * 512], OP.mult)
                    k.dve(ft2)
                    k.brd("dve", "b2")
                    if sub <= 42:
                        return
                    k.wait("dve", "st")
                    def fsum(qkout=qkout):
                        return v.tensor_tensor(qkout, sb["rtmp"][:, 0:512],
                                               sb["rtmp"][:, 512:1024], OP.add)
                    k.dve(fsum)
                    src_eng = "dve"
                    if sub <= 4:
                        return
                else:
                    src_eng = "act"
                if src_eng == "act":
                    k.wait("dve", "act")
                k.wait("dve", "pe")
                def fq2(qkout=qkout):
                    return v.tensor_tensor(sb["at"][:, 0:512], qkout, qkout, OP.mult)
                k.dve(fq2)
                k.wait("pe", "dve")
                k.bw("b3")
                def fnm():
                    return t.matmul(ps["b3"][0:2, 0:512], sb["headmaskb"][:, :],
                                    sb["at"][:, 0:512], start=True, stop=True)
                k.pe(fnm)
                k.bwd("b3")
                k.br("dve", "b3")
                col = g * NW + w
                def fnr(col=col):
                    return v.tensor_reduce(sb["qw"][0:2, col:col + 1],
                                           ps["b3"][0:2, 0:512], AX.X, OP.max)
                k.dve(fnr)
                k.brd("dve", "b3")
                if sub <= 5:
                    return
                k.wait("sp", src_eng)
                dstd = dram["Qs"] if isq else dram["Ks"]
                row0 = (g if isq else g - 8) * 128
                k.store(dstd[row0:row0 + 128, w * 512:(w + 1) * 512], qkout)
                if l == 1:
                    k.do_pump(1)



def _ph_v(nc, k, dram, sb, ps, l):
    k.fence_sp()
    v, t, a = nc.vector, nc.tensor, nc.scalar
    xnT = sb["A2"]
    Vsb = sb["AV"]
    k.wait("sp", "pe")
    k.wait("sp", "dve")
    k.load(sb["brow"][:, :], dram["bv%d" % l][:, :], "brow")
    k.load(sb["wblk"][:, :].rearrange("p (c d) -> p c d", c=8),
           dram["wv%dT" % l].rearrange("(c p) d -> p c d", p=128), "wvf")
    k.wld("act", "wvf")
    k.wld("dve", "brow")
    k.wld("pe", "brow")
    k.wait("act", "pe")
    k.act(lambda: a.activation(sb["wblk2"][:, :].bitcast(BF16)[:, 0:8192],
                               sb["wblk"][:, :], AF.Identity))
    tcastv = k.cnt["act"]
    wvb = sb["wblk2"][:, :].bitcast(BF16)[:, 0:8192].rearrange("p (c d) -> p c d", c=8)
    k.dve(lambda: v.memset(
        Vsb[:, :].rearrange("p (tt h c) -> p tt h c", tt=NT, h=16)[:, :, :, 64], 1.0))
    k.wait("pe", "act", tcastv)
    k.wait("pe", "dve")
    for tt in range(NT):
        for dh in range(2):
            bk = "b%d" % dh
            k.bw(bk)
            for fc in range(NC):
                def fmm(fc=fc, tt=tt, dh=dh, bk=bk):
                    lhsT = xnT[:, :].rearrange("p (c s) -> p c s", s=S)[
                        :, fc, tt * 128:(tt + 1) * 128]
                    rhs = wvb[:, fc, dh * 512:(dh + 1) * 512]
                    return t.matmul(ps[bk][:, :], lhsT, rhs, start=(fc == 0), stop=False)
                k.peq(fmm)
            def fbias(dh=dh, bk=bk):
                return t.matmul(ps[bk][:, :], sb["ones1"][:, :],
                                sb["brow"][0:1, dh * 512:(dh + 1) * 512],
                                start=False, stop=True)
            k.pe(fbias)
            k.bwd(bk)
            k.br("dve", bk)
            def fev(tt=tt, dh=dh, bk=bk):
                dst = Vsb[:, :].rearrange("p (tt h c) -> p tt h c", tt=NT, h=16)[
                    :, tt, dh * 8:(dh + 1) * 8, 0:64]
                return v.tensor_copy(dst, ps[bk][:, :].rearrange("p (h d) -> p h d", d=64))
            k.dve(fev)
            k.brd("dve", bk)
        k.do_pump(1)


def _ph_mbias(nc, k, dram, sb, ps, l):
    v, t, a = nc.vector, nc.tensor, nc.scalar
    def fkm():
        return v.tensor_reduce(
            sb["cmaxK"][0:2, 0:8],
            sb["qw"][0:2, 32:64].rearrange("p (g w) -> p g w", w=NW), AX.X, OP.max)
    k.dve(fkm)
    def fpr():
        knB = sb["cmaxK"][0:2, 0:8].unsqueeze(2).broadcast_to((2, 8, NW))
        return v.tensor_tensor(
            sb["mrow"][0:2, 0:32].rearrange("p (g w) -> p g w", w=NW),
            sb["qw"][0:2, 0:32].rearrange("p (g w) -> p g w", w=NW), knB, OP.mult)
    k.dve(fpr)
    k.wait("act", "dve")
    k.act(lambda: a.activation(sb["mrow"][0:2, 0:32], sb["mrow"][0:2, 0:32], AF.Sqrt))
    k.wait("dve", "act")
    def fb():
        src = sb["mrow"][0:2, 0:32].rearrange("p (g w) -> p g w", w=4)
        src = src.unsqueeze(2).broadcast_to((2, 8, 2, 4))
        return v.tensor_tensor(
            sb["mrow"][0:2, 32:96].rearrange("p (g r w) -> p g r w", g=8, r=2), src,
            sb["parity"][:, :].rearrange("p (g r w) -> p g r w", g=8, r=2), OP.mult)
    k.dve(fb)
    k.wait("pe", "dve")
    k.bw("b3")
    k.pe(lambda: t.matmul(ps["b3"][:, 0:64], sb["ones2"][:, :], sb["mrow"][0:2, 32:96],
                          start=True, stop=True))
    k.bwd("b3")
    k.br("dve", "b3")
    k.dve(lambda: v.tensor_copy(sb["Bsb"][:, :], ps["b3"][:, 0:64]))
    k.brd("dve", "b3")



def _ph_att(nc, k, dram, sb, ps, l):
    k.fence_sp()
    v, t, a = nc.vector, nc.tensor, nc.scalar
    C_ = 11.0
    Vsb = sb["AV"]
    otm = sb["A2"]
    wb16 = sb["wblk"][:, :].bitcast(BF16)
    khv = wb16[:, 0:4160].rearrange("p (b s) -> p b s", b=2)
    qhv = wb16[:, 4160:5200].rearrange("p (b s) -> p b s", b=2)
    mrow_p = sb["wblk"][:, 2700:2704]
    k.dve(lambda: v.tensor_scalar(sb["Bsb"][:, :], sb["Bsb"][:, :], 1.0, None, OP.mult)) if False else None
    k.dve(lambda: v.tensor_scalar(sb["mrow"][0:2, 0:64] if False else sb["Bsb"][:, :],
                                  sb["Bsb"][:, :], 1.0, None, OP.bypass)) if False else None
    k.dve(lambda: v.tensor_scalar(sb["cmaxM"][:, 0:16], sb["cmaxM"][:, 0:16], 1.0,
                                  None, OP.mult)) if False else None
    Bc = sb["qw"][:, 0:64]
    k.dve(lambda: v.tensor_scalar(Bc, sb["Bsb"][:, :], 1.0 / C_, None, OP.mult))
    k.dve(lambda: v.memset(khv[64:65, 0:2, 0:2048].squeeze(0) if False else
                           sb["wblk"][:, :].bitcast(BF16)[64:65, 0:4160], 1.0))
    k.wait("act", "dve")
    k.wait("pe", "dve")
    kh_user = [0, 0]
    qh_user = [0, 0]
    at_user = [0, 0]
    for h in range(H):
        kb = h % 2
        row0 = (h // 2) * 128 + (h % 2) * 64
        k.wait("sp", "pe", kh_user[kb])
        k.wait("sp", "st")
        k.load(khv[0:64, kb, 0:2048], dram["Ks"][row0:row0 + 64, :], "kh%d" % kb)
        for w in range(NW):
            qb = w % 2
            kcmax = NT if l == 0 else 4 * (w + 1)
            k.wait("sp", "pe", qh_user[qb])
            k.load(qhv[0:64, qb, 0:512], dram["Qs"][row0:row0 + 64, w * 512:(w + 1) * 512],
                   "qh%d" % qb)
            k.bw("b3")
            def r1_s(kc):
                sbk = "b%d" % (kc % 2)
                k.wld("pe", "kh%d" % kb)
                k.wld("pe", "qh%d" % qb)
                k.bw(sbk)
                def fs(kc=kc, kb=kb, qb=qb, sbk=sbk):
                    return t.matmul(ps[sbk][:, :], khv[0:64, kb, kc * 128:(kc + 1) * 128],
                                    qhv[0:64, qb, 0:512], start=True, stop=True)
                k.pe(fs)
                k.bwd(sbk)
                kh_user[kb] = qh_user[qb] = k.cnt["pe"]
            def r1_t(kc):
                sbk = "b%d" % (kc % 2)
                atb = sb["at"][:, (kc % 2) * 512:(kc % 2 + 1) * 512]
                k.wait("act", "pe", at_user[kc % 2])
                k.br("act", sbk)
                def fe(h=h, w=w, sbk=sbk, atb=atb):
                    return a.activation(atb, ps[sbk][:, :], AF.Exp,
                                        bias=Bc[:, h * NW + w:h * NW + w + 1],
                                        scale=0.125 / C_)
                k.act(fe)
                k.brd("act", sbk)
                if l == 1 and kc >= 4 * w:
                    off = 384 + 512 * w - 128 * kc
                    k.wait("dve", "act")
                    def fm(atb=atb, off=off):
                        return v.tensor_tensor(atb, atb, sb["causal"][:, off:off + 512],
                                               OP.mult)
                    k.dve(fm)
                    k.wait("pe", "dve")
                else:
                    k.wait("pe", "act")
                def fsum(kc=kc, atb=atb, kcm=kcmax):
                    return t.matmul(ps["b3"][64:65, 0:512], sb["onescol"][:, :], atb,
                                    start=(kc == 0), stop=(kc == kcm - 1),
                                    tile_position=(0, 64))
                k.pe(fsum)
                at_user[kc % 2] = k.cnt["pe"]
            r1_s(0)
            for kc in range(1, kcmax):
                r1_s(kc)
                r1_t(kc - 1)
            r1_t(kcmax - 1)
            k.bwd("b3")
            k.br("dve", "b3")
            mq = sb["wblk2"][64:65, 0:512]
            def fev():
                return v.tensor_scalar(mq, ps["b3"][64:65, 0:512], 1e-30, None, OP.add)
            k.dve(fev)
            k.brd("dve", "b3")
            k.wait("act", "dve")
            k.act(lambda: a.activation(mq, mq, AF.Ln))
            k.wait("dve", "act")
            def fmq(h=h, w=w):
                return v.tensor_scalar(mq, mq, C_, sb["Bsb"][64:65, h * NW + w:h * NW + w + 1],
                                       OP.mult, OP.subtract)
            k.dve(fmq)
            k.wld("dve", "qh%d" % qb)
            def faug(qb=qb):
                return v.tensor_scalar(qhv[64:65, qb, 0:512], mq, -8.0, None, OP.mult)
            k.dve(faug)
            k.wait("pe", "dve")
            def r2_s(kc):
                sbk = "b%d" % (kc % 2)
                k.bw(sbk)
                def fs2(kc=kc, kb=kb, qb=qb, sbk=sbk):
                    return t.matmul(ps[sbk][:, :], khv[0:65, kb, kc * 128:(kc + 1) * 128],
                                    qhv[0:65, qb, 0:512], start=True, stop=True)
                k.pe(fs2)
                k.bwd(sbk)
                kh_user[kb] = qh_user[qb] = k.cnt["pe"]
            def r2_av(kc):
                sbk = "b%d" % (kc % 2)
                atb = sb["at"][:, (kc % 2) * 512:(kc % 2 + 1) * 512]
                k.wait("act", "pe", at_user[kc % 2])
                k.br("act", sbk)
                def fe2(sbk=sbk, atb=atb):
                    return a.activation(atb, ps[sbk][:, :], AF.Exp, scale=0.125)
                k.act(fe2)
                k.brd("act", sbk)
                if l == 1 and kc >= 4 * w:
                    off = 384 + 512 * w - 128 * kc
                    k.wait("dve", "act")
                    def fm2(atb=atb, off=off):
                        return v.tensor_tensor(atb, atb, sb["causal"][:, off:off + 512],
                                               OP.mult)
                    k.dve(fm2)
                    k.wait("pe", "dve")
                else:
                    k.wait("pe", "act")
                if kc == 0:
                    k.bw("b2")
                for sub in range(4):
                    def fav(h=h, kc=kc, sub=sub, atb=atb,
                            st=(kc == 0 and sub == 0),
                            sp_=(kc == kcmax - 1 and sub == 3)):
                        return t.matmul(
                            ps["b2"][:, sub * 65:sub * 65 + 65],
                            atb[:, sub * 128:(sub + 1) * 128],
                            Vsb[:, :].rearrange("p (tt c) -> p tt c", c=16 * 65)[
                                :, kc, h * 65:h * 65 + 65],
                            start=st, stop=sp_)
                    (k.peq if sub < 3 else k.pe)(fav)
                at_user[kc % 2] = k.cnt["pe"]
            r2_s(0)
            for kc in range(1, kcmax):
                r2_s(kc)
                r2_av(kc - 1)
            r2_av(kcmax - 1)
            k.bwd("b2")
            k.do_pump(4)
            k.br("dve", "b2")
            pv = ps["b2"][:, 0:260].rearrange("p (sub x) -> p sub x", x=65)
            k.dve(lambda: v.reciprocal(sb["rsum"][:, :], pv[:, 0:4, 64]))
            for sub in range(4):
                def fn(h=h, w=w, sub=sub):
                    tt = w * 4 + sub
                    return v.tensor_scalar(
                        otm[:, tt * E + h * 64: tt * E + (h + 1) * 64],
                        pv[:, sub, 0:64], sb["rsum"][:, sub:sub + 1], None, OP.mult)
                k.dve(fn)
            k.brd("dve", "b2")


def _ph_wo(nc, k, dram, sb, ps, l):
    k.fence_sp()
    v, t, a = nc.vector, nc.tensor, nc.scalar
    otm = sb["A2"]
    ofm = sb["wblk"][:, :].bitcast(BF16)
    stm = sb["AV"][:, 0:16384]
    wob = sb["wsh"]
    k.wait("sp", "pe")
    k.wait("sp", "dve")
    k.load(sb["brow"][:, :], dram["bo%d" % l][:, :], "brow")
    k.wld("dve", "brow")
    k.dve(lambda: v.tensor_copy(sb["browb"][:, :], sb["brow"][0:1, :]))
    for hf in range(2):
        k.wait("sp", "act")
        k.load(sb["wblk2"][:, 0:4096].rearrange("p (c d) -> p c d", c=8),
               dram["wo%dT" % l].rearrange("(c p) d -> p c d", p=128)[
                   :, :, hf * 512:(hf + 1) * 512], "wos")
        k.wld("act", "wos")
        def fcast(hf=hf):
            return a.activation(
                wob[:, :].rearrange("p (c d) -> p c d", c=8)[:, :, hf * 512:(hf + 1) * 512],
                sb["wblk2"][:, 0:4096].rearrange("p (c d) -> p c d", c=8), AF.Identity)
        k.act(fcast)
    k.wait("pe", "dve")
    for c in range(NC):
        for tg in range(2):
            bk = "t%d" % (tg % 2)
            k.bw(bk)
            for j in range(8):
                tt = tg * 8 + j
                def ftr(c=c, tt=tt, j=j, bk=bk):
                    return t.transpose(ps[bk][:, j * 128:(j + 1) * 128],
                                       otm[:, tt * E + c * 128:tt * E + (c + 1) * 128],
                                       sb["identb"][:, :])
                (k.peq if j < 7 else k.pe)(ftr)
            k.bwd(bk)
            k.br("dve", bk)
            def fev(c=c, tg=tg, bk=bk):
                return v.tensor_copy(
                    ofm[:, c * S + tg * 1024: c * S + (tg + 1) * 1024], ps[bk][:, :])
            k.dve(fev)
            k.brd("dve", bk)
    k.wait("pe", "act")
    for tt in range(NT):
        for dh in range(2):
            bk = "b%d" % (3 + dh)
            k.bw(bk)
            k.wait("pe", "dve")
            for c in range(NC):
                def fmm(tt=tt, dh=dh, c=c, bk=bk):
                    lhsT = ofm[:, c * S + tt * 128: c * S + (tt + 1) * 128]
                    rhs = wob[:, c * E + dh * 512: c * E + (dh + 1) * 512]
                    return t.matmul(ps[bk][:, :], lhsT, rhs, start=(c == 0), stop=False)
                k.peq(fmm)
            def fbias(dh=dh, bk=bk):
                return t.matmul(ps[bk][:, :], sb["onesb"][:, :],
                                sb["browb"][0:1, dh * 512:(dh + 1) * 512],
                                start=False, stop=True)
            k.pe(fbias)
            k.bwd(bk)
            dstv = stm.rearrange("p (tt e) -> p tt e", e=E)
            if l == 0:
                k.br("dve", bk)
                def fev2(tt=tt, dh=dh, bk=bk, dstv=dstv):
                    return v.tensor_copy(dstv[:, tt, dh * 512:(dh + 1) * 512], ps[bk][:, :])
                k.dve(fev2)
                k.brd("dve", bk)
                k.do_pump(2)
            else:
                k.br("dve", bk)
                def fmx(bk=bk):
                    return v.tensor_scalar(sb["qkrot"][:, 0:512], ps[bk][:, :],
                                           0.0, None, OP.max)
                k.dve(fmx)
                def fmn(bk=bk):
                    return v.tensor_scalar(sb["qkrot"][:, 512:1024], ps[bk][:, :],
                                           0.0, None, OP.min)
                k.dve(fmn)
                k.brd("dve", bk)
                def fadd2(tt=tt, dh=dh, dstv=dstv):
                    return v.tensor_tensor(dstv[:, tt, dh * 512:(dh + 1) * 512],
                                           sb["qkrot"][:, 0:512],
                                           sb["qkrot"][:, 512:1024], OP.add)
                k.dve(fadd2)
                k.do_pump(1)



def _ph_ffn0(nc, k, dram, sb, ps, xsrc, xdst):
    k.fence_sp()
    k.wld("dve", "wql")
    k.wait("dve", "act")
    v, t, a = nc.vector, nc.tensor, nc.scalar
    hwin = sb["A2"]
    wbf = sb["wblk"]
    wbb = sb["wblk"][:, :].bitcast(BF16)
    w1st = wbf[:, 0:2048].rearrange("p (b c s) -> p b c s", b=2, c=8)
    w1bf = wbb[:, 4096:6144].rearrange("p (b c s) -> p b c s", b=2, c=8)
    w2st = wbf[:, 3072:4096].rearrange("p (b s) -> p b s", b=2)
    w2bf = wbb[:, 8192:9216].rearrange("p (b s) -> p b s", b=2)
    xnw = sb["wblk2"][:, :].bitcast(BF16)[:, 0:8192].rearrange(
        "p (b c s) -> p b c s", b=2, c=8)
    k.wait("sp", "pe")
    k.wait("sp", "dve")
    k.load(sb["brow"][:, :], dram["b2"][:, :], "brow")
    k.wld("pe", "brow")
    k.wait("sp", "st")
    w1_user = [0, 0]
    w2_user = [0, 0]
    xn_user = [0, 0]
    for w in range(NW):
        xb = w % 2
        k.wait("sp", "pe", max(xn_user[xb], w2_user[0], w2_user[1]))
        k.load(xnw[:, xb], dram["xn_fm"].rearrange("(c p) s -> p c s", p=128)[
            :, :, w * 512:(w + 1) * 512], "xnw%d" % xb)
        for ic in range(NIC):
            wb = ic % 2
            if wb == 0:
                k.wait("sp", "pe", max(w1_user[0], w1_user[1]))
                k.wait("sp", "dve")
                k.load(sb["wblk"][:, 0:2048].rearrange("p (c bi) -> p c bi", bi=256),
                       dram["W1"].rearrange("(c p) i -> p c i", p=128)[
                           :, :, ic * 128:(ic + 2) * 128], "w1p")
            k.wld("act", "w1p")
            k.wait("act", "pe", w1_user[wb])
            def fc1(wb=wb):
                w1v = sb["wblk"][:, 0:2048].rearrange("p (c b s) -> p c b s", b=2, s=128)
                return a.activation(w1bf[:, wb], w1v[:, :, wb, :], AF.Identity)
            k.act(fc1)
            tcast = k.cnt["act"]
            bk = ("b0", "b1", "b4", "b5")[ic % 4]
            k.wld("pe", "xnw%d" % xb)
            k.wait("pe", "act", tcast)
            k.bw(bk)
            for fc in range(NC):
                def fmm(fc=fc, wb=wb, xb=xb, bk=bk):
                    return t.matmul(ps[bk][:, :], w1bf[:, wb, fc], xnw[:, xb, fc],
                                    start=(fc == 0), stop=(fc == NC - 1))
                (k.peq if fc < NC - 1 else k.pe)(fmm)
            k.bwd(bk)
            w1_user[wb] = xn_user[xb] = k.cnt["pe"]
            k.br("dve", bk)
            def fre(ic=ic, bk=bk):
                return v.tensor_scalar(hwin[:, ic * 512:(ic + 1) * 512], ps[bk][:, :],
                                       sb["b1"][:, ic:ic + 1], 0.0, OP.add, OP.max)
            k.dve(fre)
            k.brd("dve", bk)
        for dh in range(2):
            for ic in range(NIC):
                wb2 = ic % 2
                if wb2 == 0:
                    k.wait("sp", "pe", max(w2_user[0], w2_user[1]))
                    k.wait("sp", "dve")
                    k.load(w2st[:, 0:2],
                           dram["W2"][ic * 128:(ic + 2) * 128,
                                      dh * 512:(dh + 1) * 512].rearrange(
                               "(b p) s -> p b s", p=128), "w2p")
                k.wld("act", "w2p")
                k.wait("act", "pe", w2_user[wb2])
                def fc2(wb2=wb2):
                    return a.activation(w2bf[:, wb2], w2st[:, wb2], AF.Identity)
                k.act(fc2)
                tcast = k.cnt["act"]
                k.wait("pe", "act", tcast)
                k.wait("pe", "dve")
                for tj in range(4):
                    bk = "b%d" % (2 + tj)
                    if ic == 0:
                        k.bw(bk)
                    def fmm(ic=ic, tj=tj, wb2=wb2, bk=bk):
                        return t.matmul(
                            ps[bk][:, :],
                            hwin[:, ic * 512 + tj * 128: ic * 512 + (tj + 1) * 128],
                            w2bf[:, wb2], start=(ic == 0), stop=False)
                    (k.peq if tj < 3 else k.pe)(fmm)
                w2_user[wb2] = k.cnt["pe"]
            for tj in range(4):
                bk = "b%d" % (2 + tj)
                def fbias(bk=bk, dh=dh):
                    return t.matmul(ps[bk][:, :], sb["ones1"][:, :],
                                    sb["brow"][0:1, dh * 512:(dh + 1) * 512],
                                    start=False, stop=True)
                k.pe(fbias)
                k.bwd(bk)
                tt = w * 4 + tj
                k.br("act", bk)
                k.wait("act", "dve")
                def fsi(tj=tj, dh=dh, bk=bk):
                    gw = sb["wblk"][:, :].bitcast(BF16)[:, 9216:13312]
                    return a.activation(
                        gw[:, tj * 1024 + dh * 512: tj * 1024 + (dh + 1) * 512],
                        ps[bk][:, :], AF.Silu)
                k.act(fsi)
                k.brd("act", bk)
        _merge_window(nc, k, dram, sb, xsrc, xdst, w,
                      sb["wblk"][:, :].bitcast(BF16)[:, 9216:13312])


def _merge_window(nc, k, dram, sb, xsrc, xdst, w, gwin):
    v = nc.vector
    stm = sb["AV"][:, 0:16384]
    for tj in range(4):
        tt = w * 4 + tj
        k.wait("sp", "st")
        k.wait("sp", "dve")
        k.load(sb["xio"][:, :], xsrc[tt * 128:(tt + 1) * 128, :], "xio")
        k.wld("dve", "xio")
        k.wait("dve", "act")
        def fmul(tt=tt, tj=tj):
            return v.tensor_tensor(sb["wblk2"][:, 4096:5120],
                                   stm[:, tt * E:(tt + 1) * E],
                                   gwin[:, tj * 1024:(tj + 1) * 1024], OP.mult)
        k.dve(fmul)
        def fadd():
            return v.tensor_tensor(sb["xio"][:, :], sb["xio"][:, :],
                                   sb["wblk2"][:, 4096:5120], OP.add)
        k.dve(fadd)
        k.wait("sp", "dve")
        k.store(xdst[tt * 128:(tt + 1) * 128, :], sb["xio"][:, :])



def _wquant(nc, k, dram, sb, ps):
    k.fence_sp()
    """gate/up/down -> bf16 q*sc8, transposed, in HBM (wgq/wuq/wdq).
    scl: [9]=gs_w, [10]=gs_w/6, [11]=1/gs_w (per weight);
    [6]/[7]/[8] = 1/(gs_act*gs_w) evac scales."""
    v, t, a = nc.vector, nc.tensor, nc.scalar
    specs = [("gw", "wgq", I, E, 0, 6), ("uw", "wuq", I, E, 0, 7),
             ("dw", "wdq", E, I, 3, 8)]
    wload = sb["wblk"][:, :].rearrange("p (b s) -> p b s", b=2)
    stage = sb["A2"]
    for (nm, qn, NR, NCL, actbase, sci) in specs:
        nrch = NR // 128
        for rc in range(nrch):
            buf = rc % 2
            k.wait("sp", "dve")
            k.load(wload[:, buf, 0:NCL], dram[nm][rc * 128:(rc + 1) * 128, :],
                   "wl%d" % buf)
            k.wld("dve", "wl%d" % buf)
            def fam(rc=rc, buf=buf, NCL=NCL):
                return v.tensor_reduce(sb["qw"][:, rc:rc + 1], wload[:, buf, 0:NCL],
                                       AX.X, OP.max, apply_absolute_value=True)
            k.dve(fam)
        k.dve(lambda nrch=nrch: v.tensor_reduce(sb["scl"][:, 15:16],
                                                sb["qw"][:, 0:nrch], AX.X, OP.max))
        k.wait("pe", "dve")
        k.bw("b5")
        k.pe(lambda: t.matmul(ps["b5"][0:1, 0:128], sb["scl"][:, 15:16],
                              sb["identf"][:, :], start=True, stop=True,
                              is_transpose=True))
        k.bwd("b5")
        k.br("dve", "b5")
        k.dve(lambda: v.tensor_reduce(sb["mrow"][0:1, 0:1], ps["b5"][0:1, 0:128],
                                      AX.X, OP.max))
        k.brd("dve", "b5")
        k.wait("pe", "dve")
        k.bw("b5")
        k.pe(lambda: t.matmul(ps["b5"][:, 1:2], sb["ones1"][:, :],
                              sb["mrow"][0:1, 0:1], start=True, stop=True))
        k.bwd("b5")
        k.br("dve", "b5")
        k.dve(lambda: v.tensor_copy(sb["scl"][:, 14:15], ps["b5"][:, 1:2]))
        k.brd("dve", "b5")
        k.dve(lambda: v.reciprocal(sb["scl"][:, 9:10], sb["scl"][:, 14:15]))
        k.dve(lambda: v.tensor_scalar(sb["scl"][:, 9:10], sb["scl"][:, 9:10],
                                      448.0 * 6.0, None, OP.mult))
        k.dve(lambda: v.tensor_scalar(sb["scl"][:, 10:11], sb["scl"][:, 9:10],
                                      1.0 / 6.0, None, OP.mult))
        k.dve(lambda: v.reciprocal(sb["scl"][:, 11:12], sb["scl"][:, 9:10]))
        k.dve(lambda sci=sci, actbase=actbase: v.tensor_tensor(
            sb["scl"][:, sci:sci + 1], sb["scl"][:, 9:10],
            sb["scl"][:, actbase:actbase + 1], OP.mult))
        k.dve(lambda sci=sci: v.reciprocal(sb["scl"][:, sci:sci + 1],
                                           sb["scl"][:, sci:sci + 1]))
        ncols = NCL // 128
        for rg in range(nrch // 4):
            k.wait("dve", "st")
            for j in range(4):
                rc = rg * 4 + j
                buf = rc % 2
                k.wait("sp", "dve")
                k.wait("sp", "act")
                k.load(wload[:, buf, 0:NCL], dram[nm][rc * 128:(rc + 1) * 128, :],
                       "wl%d" % buf)
                k.wld("dve", "wl%d" % buf)
                k.wld("act", "wl%d" % buf)
                nhalf = 2048 if NCL > 2048 else NCL
                for hh in range(NCL // nhalf):
                    _quant_tile(nc, k, sb, ps,
                                src=wload[:, buf, hh * nhalf:(hh + 1) * nhalf],
                                n=nhalf, scol=9,
                                dst=stage, dst_stride=512,
                                dst_c0=hh * (nhalf // 128), dst_off=j * 128,
                                src_slot="wl%d" % buf)
            k.wait("sp", "dve")
            k.store(dram[qn].rearrange("(c p) i -> p c i", p=128)[
                :, :, rg * 512:(rg + 1) * 512],
                stage[:, 0:ncols * 512].rearrange("p (c s) -> p c s", s=512))



def _ph_ffn1(nc, k, dram, sb, ps, xsrc, xdst):
    k.cnt["pump_at_drain"] = k.cnt["pump"]
    while k.pump is not None:
        k.do_pump(8)
    k.fence_sp()
    v, t, a = nc.vector, nc.tensor, nc.scalar
    hid = sb["A2"]
    hqw = sb["wblk"][:, :].bitcast(BF16)
    wq2 = sb["wsh"][:, :].rearrange("p (b s) -> p b s", b=2)
    xqT = sb["xqT"]
    wq_user = [0]
    for w in range(NW):
        for ic in range(8):
            k.wait("sp", "pe", wq_user[0])
            k.wait("sp", "st")
            k.wait("sp", "gst", k.wq_done.get("wuq", 0))
            k.load(wq2[:, 0].rearrange("p (c s) -> p c s", c=8),
                   dram["wgq"].rearrange("(c p) i -> p c i", p=128)[
                       :, :, ic * 512:(ic + 1) * 512], "wqa")
            k.load(wq2[:, 1].rearrange("p (c s) -> p c s", c=8),
                   dram["wuq"].rearrange("(c p) i -> p c i", p=128)[
                       :, :, ic * 512:(ic + 1) * 512], "wqb")
            for tj in range(4):
                tt = w * 4 + tj
                bg = "b%d" % (2 * (tj % 2))
                bu = "b%d" % (2 * (tj % 2) + 1)
                sgb = sb["qkrot"][:, (tj % 2) * 512:(tj % 2 + 1) * 512]
                k.wld("pe", "wqa")
                k.wld("pe", "wqb")
                k.bw(bg)
                for hc in range(NC):
                    def fg(hc=hc, tt=tt, bg=bg):
                        return t.matmul(
                            ps[bg][:, :],
                            xqT[:, :].rearrange("p (c s) -> p c s", s=S)[
                                :, hc, tt * 128:(tt + 1) * 128],
                            wq2[:, 0].rearrange("p (c s) -> p c s", c=8)[:, hc],
                            start=(hc == 0), stop=(hc == NC - 1))
                    (k.peq if hc < NC - 1 else k.pe)(fg)
                k.bwd(bg)
                k.bw(bu)
                for hc in range(NC):
                    def fu(hc=hc, tt=tt, bu=bu):
                        return t.matmul(
                            ps[bu][:, :],
                            xqT[:, :].rearrange("p (c s) -> p c s", s=S)[
                                :, hc, tt * 128:(tt + 1) * 128],
                            wq2[:, 1].rearrange("p (c s) -> p c s", c=8)[:, hc],
                            start=(hc == 0), stop=(hc == NC - 1))
                    (k.peq if hc < NC - 1 else k.pe)(fu)
                k.bwd(bu)
                wq_user[0] = k.cnt["pe"]
                k.br("act", bg)
                k.wait("act", "dve", k.cnt["dve"] - 1)
                def fsg(bg=bg, sgb=sgb):
                    return a.activation(sgb, ps[bg][:, :], AF.Silu,
                                        scale=sb["scl"][:, 6:7])
                k.act(fsg)
                k.brd("act", bg)
                k.br("dve", bu)
                k.wait("dve", "act")
                def fh(tj=tj, ic=ic, bu=bu, sgb=sgb):
                    return v.scalar_tensor_tensor(
                        hid[:, :].rearrange("p (tj i) -> p tj i", i=I)[
                            :, tj, ic * 512:(ic + 1) * 512],
                        ps[bu][:, :], sb["scl"][:, 7:8], sgb,
                        OP.mult, OP.mult)
                k.dve(fh)
                k.brd("dve", bu)
        k.wait("dve", "pe")
        for tj in range(4):
            for hh in range(2):
                _quant_tile(nc, k, sb, ps,
                            src=hid[:, :].rearrange("p (tj i) -> p tj i", i=I)[
                                :, tj, hh * 2048:(hh + 1) * 2048],
                            n=2048, scol=3,
                            dst=hqw, dst_stride=512, dst_c0=hh * 16, dst_off=tj * 128)
        dn_user = [0, 0]
        k.wait("sp", "pe")
        for dh in range(2):
            for ic in range(NIC):
                wb = ic % 2
                if wb == 0:
                    k.wait("sp", "pe", max(dn_user[0], dn_user[1]))
                    k.wait("sp", "gst", k.wq_done.get("wdq", 0))
                    k.load(wq2[:, 0:2, 0:512],
                           dram["wdq"][ic * 128:(ic + 2) * 128,
                                       dh * 512:(dh + 1) * 512].rearrange(
                               "(b p) s -> p b s", p=128), "wqa")
                k.wld("pe", "wqa")
                if ic == 0:
                    k.wait("pe", "dve")
                for tj in range(4):
                    bk = "b%d" % (2 + tj)
                    if ic == 0:
                        k.bw(bk)
                    def fd(ic=ic, tj=tj, wb=wb, bk=bk):
                        return t.matmul(
                            ps[bk][:, :],
                            hqw[:, :].rearrange("p (c s) -> p c s", s=512)[
                                :, ic, tj * 128:(tj + 1) * 128],
                            wq2[:, wb, 0:512],
                            start=(ic == 0), stop=(ic == NIC - 1))
                    (k.peq if tj < 3 else k.pe)(fd)
                dn_user[wb] = k.cnt["pe"]
            for tj in range(4):
                bk = "b%d" % (2 + tj)
                k.bwd(bk)
                tt = w * 4 + tj
                k.br("act", bk)
                k.wait("act", "dve")
                def fsi(tj=tj, dh=dh, bk=bk):
                    gw = sb["A2"][:, 0:4096]
                    return a.activation(
                        gw[:, tj * 1024 + dh * 512: tj * 1024 + (dh + 1) * 512],
                        ps[bk][:, :], AF.Silu, scale=sb["scl"][:, 8:9])
                k.act(fsi)
                k.brd("act", bk)
        wq_user[0] = k.cnt["pe"]
        _merge_window(nc, k, dram, sb, xsrc, xdst, w, sb["A2"][:, 0:4096])



def kernel(**kw):
    maps = _prep_inputs(kw)
    nc = build_program()
    res = run_bass_kernel_spmd(nc, maps, core_ids=list(range(8)))
    return np.stack([r["y"] for r in res.results], axis=0)
